# version 17
# baseline (speedup 1.0000x reference)
import numpy as np
import ml_dtypes
import sys as _sys
from contextlib import ExitStack

def _ln_():
    f = _sys._getframe(2)
    return f.f_lineno

import concourse.bass as bass
import concourse.mybir as mybir
from concourse.bass_utils import run_bass_kernel_spmd

F32 = mybir.dt.float32
BF16 = mybir.dt.bfloat16
U32 = mybir.dt.uint32
AF = mybir.ActivationFunctionType
ALU = mybir.AluOpType

D_MODEL = 1024
SEQ = 2048
BATCH = 8
KT = D_MODEL // 128
ENGS = ("sync", "gpsimd", "tensor", "scalar", "vector")


class Prog:

    def __init__(self, ctx, nc):
        self.nc = nc
        self.ctx = ctx
        self.ops = {e: [] for e in ENGS}
        self.log = {}
        self.sems = {}

    def sem(self, name, producer):
        h = self.ctx.enter_context(self.nc.semaphore(name))
        self.sems[name] = [h, 0, producer]
        return name

    def n(self, sname):
        return self.sems[sname][1]

    def emit(self, eng, fn, inc=None, dma=False):
        if inc is not None:
            rec = self.sems[inc]
            assert rec[2] == eng, (inc, rec[2], eng)
            amt = 16 if dma else 1
            rec[1] += amt

            def wrapped(e, fn=fn, h=rec[0], amt=amt):
                fn(e).then_inc(h, amt)
            self.ops[eng].append(wrapped)
            self.log.setdefault(eng, []).append(("inc", inc, amt, _ln_()))
        else:
            self.ops[eng].append(fn)
            self.log.setdefault(eng, []).append(("op", None, None, _ln_()))

    def wait(self, eng, sname, val=None):
        rec = self.sems[sname]
        v = rec[1] if val is None else val
        if v <= 0:
            return

        def w(e, h=rec[0], v=v):
            e.wait_ge(h, v)
        self.ops[eng].append(w)
        self.log.setdefault(eng, []).append(("wait", sname, v, _ln_()))

    def simulate(self):
        pos = {e: 0 for e in ENGS}
        val = {name: 0 for name in self.sems}
        progress = True
        while progress:
            progress = False
            for e in ENGS:
                q = self.log.get(e, [])
                while pos[e] < len(q):
                    ev = q[pos[e]]
                    if ev[0] == "wait" and val[ev[1]] < ev[2]:
                        break
                    if ev[0] == "inc":
                        val[ev[1]] += ev[2]
                    pos[e] += 1
                    progress = True
        stuck = {e: pos[e] for e in ENGS if pos[e] < len(self.log.get(e, []))}
        if stuck:
            msg = []
            for e, p in stuck.items():
                ev = self.log[e][p]
                msg.append(f"{e}@{p}/{len(self.log[e])} line{ev[3]} waiting {ev[1]}>={ev[2]} (now {val[ev[1]]})")
            raise RuntimeError("DEADLOCK: " + "; ".join(msg))

    def build_block(self, block):
        self.simulate()
        for eng in ENGS:
            if not self.ops[eng]:
                continue
            body = self.ops[eng]

            def run(e, body=body):
                for fn in body:
                    fn(e)
            getattr(block, eng)(run)


class PsumPool:

    def __init__(self, P):
        self.P = P
        self.state = {}

    def use(self, bank):
        st = self.state.get(id(bank))
        if st is not None:
            self.P.wait("tensor", st[0], st[1])

    def free(self, bank, sname):
        self.state[id(bank)] = (sname, self.P.n(sname))


def _sb(ctx, nc, name, shape, dtype):
    return ctx.enter_context(nc.sbuf_tensor(name, shape, dtype))


def _ps(ctx, nc, name, shape, dtype=F32):
    return ctx.enter_context(nc.psum_tensor(name, shape, dtype))


def emit_ffn(nc, P, pool, d, pfx="e"):
    x_d = d["x"]
    w1t_d = d["w1t"]
    w2t_d = d["w2t"]
    h1t_d = d["h1t"]
    h2t_d = d["h2t"]
    w3t_d = d["w3t"]
    s1_d = d["s1"]
    s2_d = d["s2"]
    sh1_d = d["sh1"]
    sh2_d = d["sh2"]
    s3_d = d["s3"]
    ones_d = d["ones"]
    ones1_d = d["ones1"]
    out_d = d["out"]

    C = min(1024, SEQ)
    NCH = SEQ // C
    NW = min(512, C)
    NN = C // NW

    w1t_r = w1t_d
    w2t_r = w2t_d
    h1t_r = h1t_d
    h2t_r = h2t_d
    w3t_r = w3t_d
    x_r = x_d.rearrange("(k p) t -> p k t", p=128)
    out_r = out_d.rearrange("(k p) t -> p k t", p=128)


    with ExitStack() as ctx:
        x_bf = _sb(ctx, nc, pfx + "x_bf", [128, KT, C], BF16)
        x_f = _sb(ctx, nc, pfx + "x_f", [128, KT, C], F32)
        wa = [_sb(ctx, nc, pfx + f"wa{i}", [128, KT, 128], BF16) for i in range(2)]
        wb = [_sb(ctx, nc, pfx + f"wb{i}", [128, KT, 128], BF16) for i in range(2)]
        hw = [_sb(ctx, nc, pfx + f"hw{i}", [128, 32, 128], BF16) for i in range(2)]
        hbuf = _sb(ctx, nc, pfx + "hbuf", [128, 32, C], BF16)
        gbuf = _sb(ctx, nc, pfx + "gbuf", [128, 32, C], BF16)
        _t1 = _sb(ctx, nc, pfx + "t1b", [128, NW], F32)
        t1 = [_t1, _t1]
        s1c = _sb(ctx, nc, pfx + "s1c", [128, 32], F32)
        s2c = _sb(ctx, nc, pfx + "s2c", [128, 32], F32)
        sh1c = _sb(ctx, nc, pfx + "sh1c", [128, 32], F32)
        sh2c = _sb(ctx, nc, pfx + "sh2c", [128, 32], F32)
        s3c = _sb(ctx, nc, pfx + "s3c", [128, 8], F32)
        onesc = _sb(ctx, nc, pfx + "onesc", [128, 1], F32)
        ones1c = _sb(ctx, nc, pfx + "ones1c", [1, 128], F32)
        xsq = t1
        srow = _sb(ctx, nc, pfx + "srow", [1, 2, NW], F32)

        pa = [_ps(ctx, nc, pfx + f"pa{i}", [128, NW]) for i in range(2)]
        pb = [_ps(ctx, nc, pfx + f"pb{i}", [128, NW]) for i in range(2)]
        pst = _ps(ctx, nc, pfx + "pst", [1, NW])
        pst2 = _ps(ctx, nc, pfx + "pst2", [1, NW])
        pbc = _ps(ctx, nc, pfx + "pbc", [128, NW])
        pbc2 = _ps(ctx, nc, pfx + "pbc2", [128, NW])
        big = [pa[0], pb[0], pa[1], pb[1]]

        s_lc = P.sem(pfx + "s_lc", "sync")
        s_lx = P.sem(pfx + "s_lx", "sync")
        s_wx = P.sem(pfx + "s_wx", "gpsimd")
        s_wab = [P.sem(pfx + "s_wab0", "gpsimd"), P.sem(pfx + "s_wab1", "gpsimd")]
        s_whw = [P.sem(pfx + "s_whw0", "gpsimd"), P.sem(pfx + "s_whw1", "gpsimd")]
        s_pe = P.sem(pfx + "s_pe", "tensor")
        s_act = P.sem(pfx + "s_act", "scalar")
        s_ve = P.sem(pfx + "s_ve", "vector")
        s_st = P.sem(pfx + "s_st", "sync")

        for nm, dst, src in [("s1", s1c, s1_d), ("s2", s2c, s2_d), ("sh1", sh1c, sh1_d),
                             ("sh2", sh2c, sh2_d), ("s3", s3c, s3_d), ("ones", onesc, ones_d),
                             ("ones1", ones1c, ones1_d)]:
            P.emit("sync", lambda e, dst=dst, src=src: e.dma_start(out=dst[:], in_=src),
                   inc=s_lc, dma=True)
        ld_prolog = P.n(s_lc)
        P.wait("tensor", s_lc, ld_prolog)
        P.wait("scalar", s_lc, ld_prolog)
        P.wait("vector", s_lc, ld_prolog)

        fam_free = {"ab": [], "hw": []}

        fam_sems = {"ab": s_wab, "hw": s_whw}

        def stream_block(family, dmas):
            fam = fam_free[family]
            idx = len(fam)
            if idx >= 2:
                P.wait("gpsimd", *fam[idx - 2])
            sname = fam_sems[family][idx % 2]
            for fn in dmas:
                P.emit("gpsimd", fn, inc=sname, dma=True)
            fam.append(None)
            return idx, (sname, P.n(sname))

        def free_block(family, idx):
            fam_free[family][idx] = (s_pe, P.n(s_pe))

        t1_guard = [0, 0]
        xsq_guard = [0, 0]
        xf_guard = 0

        for c in range(NCH):
            tc = c * C
            if xf_guard:
                P.wait("sync", s_st, xf_guard)
            P.emit("sync", lambda e, tc=tc: e.dma_start(out=x_f[:], in_=x_r[:, :, tc:tc + C]),
                   inc=s_lx, dma=True)
            x_f_ready = P.n(s_lx)
            if c > 0:
                P.wait("gpsimd", *fam_free["ab"][-1])
            P.emit("gpsimd", lambda e, tc=tc: e.dma_start(out=x_bf[:], in_=x_r[:, :, tc:tc + C]),
                   inc=s_wx, dma=True)
            P.wait("tensor", s_wx, P.n(s_wx))

            for j in range(32):
                par = len(fam_free["ab"]) % 2
                idx, (wsem, wv) = stream_block("ab", [
                    lambda e, j=j, par=par: e.dma_start(out=wa[par][:], in_=w1t_r[:, j]),
                    lambda e, j=j, par=par: e.dma_start(out=wb[par][:], in_=w2t_r[:, j]),
                ])
                P.wait("tensor", wsem, wv)
                for n in range(NN):
                    ts = n * NW
                    A, B = pa[par], pb[par]
                    pool.use(A)
                    for k in range(KT):
                        P.emit("tensor", lambda e, k=k, par=par, ts=ts, A=A: e.matmul(
                            A[:], wa[par][:, k, :], x_bf[:, k, ts:ts + NW],
                            start=(k == 0), stop=(k == KT - 1)),
                            inc=(s_pe if k == KT - 1 else None))
                    pe_a = P.n(s_pe)
                    pool.use(B)
                    for k in range(KT):
                        P.emit("tensor", lambda e, k=k, par=par, ts=ts, B=B: e.matmul(
                            B[:], wb[par][:, k, :], x_bf[:, k, ts:ts + NW],
                            start=(k == 0), stop=(k == KT - 1)),
                            inc=(s_pe if k == KT - 1 else None))
                    pe_b = P.n(s_pe)
                    P.wait("scalar", s_pe, pe_a)
                    if t1_guard[0]:
                        P.wait("scalar", s_ve, t1_guard[0])
                    P.emit("scalar", lambda e, j=j, n=n, A=A: e.activation(
                        t1[n][:], A[:], AF.Silu, scale=s1c[:, j:j + 1]), inc=s_act)
                    pool.free(A, s_act)
                    act_t1 = P.n(s_act)
                    P.wait("vector", s_pe, pe_b)
                    P.wait("vector", s_act, act_t1)
                    P.emit("vector", lambda e, j=j, n=n, B=B: e.scalar_tensor_tensor(
                        out=hbuf[:, j, n * NW:(n + 1) * NW], in0=B[:], scalar=s2c[:, j:j + 1],
                        in1=t1[n][:], op0=ALU.mult, op1=ALU.mult), inc=s_ve)
                    t1_guard[0] = P.n(s_ve)
                    pool.free(B, s_ve)
                free_block("ab", idx)

            for phase, (wsrc, sin, sout, scol) in enumerate([
                    (h1t_r, hbuf, gbuf, sh1c), (h2t_r, gbuf, hbuf, sh2c)]):
                if phase == 0:
                    P.wait("tensor", s_ve, P.n(s_ve))
                else:
                    P.wait("tensor", s_act, P.n(s_act))
                for j in range(32):
                    par = len(fam_free["hw"]) % 2
                    idx, (wsem, wv) = stream_block("hw", [
                        lambda e, j=j, par=par, wsrc=wsrc: e.dma_start(
                            out=hw[par][:], in_=wsrc[:, j])])
                    P.wait("tensor", wsem, wv)
                    for n in range(NN):
                        G = big[(j * NN + n) % 4]
                        pool.use(G)
                        for k in range(32):
                            P.emit("tensor", lambda e, k=k, par=par, n=n, G=G, sin=sin: e.matmul(
                                G[:], hw[par][:, k, :], sin[:, k, n * NW:(n + 1) * NW],
                                start=(k == 0), stop=(k == 31)),
                                inc=(s_pe if k == 31 else None))
                        pe_g = P.n(s_pe)
                        P.wait("scalar", s_pe, pe_g)
                        P.emit("scalar", lambda e, j=j, n=n, G=G, sout=sout, scol=scol: e.activation(
                            sout[:, j, n * NW:(n + 1) * NW], G[:], AF.Gelu, scale=scol[:, j:j + 1]),
                            inc=s_act)
                        pool.free(G, s_act)
                    free_block("hw", idx)

            P.wait("tensor", s_act, P.n(s_act))
            P.wait("vector", s_lx, x_f_ready)
            for j3 in range(KT):
                par = len(fam_free["hw"]) % 2
                idx, (wsem, wv) = stream_block("hw", [
                    lambda e, j3=j3, par=par: e.dma_start(
                        out=hw[par][:], in_=w3t_r[:, j3])])
                P.wait("tensor", wsem, wv)
                for n in range(NN):
                    G = big[(j3 * NN + n) % 4]
                    pool.use(G)
                    for k in range(32):
                        P.emit("tensor", lambda e, k=k, par=par, n=n, G=G: e.matmul(
                            G[:], hw[par][:, k, :], hbuf[:, k, n * NW:(n + 1) * NW],
                            start=(k == 0), stop=(k == 31)),
                            inc=(s_pe if k == 31 else None))
                    pe_g = P.n(s_pe)
                    P.wait("vector", s_pe, pe_g)
                    P.emit("vector", lambda e, j3=j3, n=n, G=G: e.scalar_tensor_tensor(
                        out=x_f[:, j3, n * NW:(n + 1) * NW], in0=G[:], scalar=s3c[:, j3:j3 + 1],
                        in1=x_f[:, j3, n * NW:(n + 1) * NW], op0=ALU.mult, op1=ALU.add), inc=s_ve)
                    pool.free(G, s_ve)
                free_block("hw", idx)
            x4_done = P.n(s_ve)

            for n in range(NN):
                ns = n * NW
                P.wait("scalar", s_ve, x4_done)
                pool.use(pst)
                pool.use(pst2)
                P.wait("tensor", s_ve, x4_done)
                for j in range(KT):
                    if xsq_guard[0]:
                        P.wait("scalar", s_pe, xsq_guard[0])
                    P.emit("scalar", lambda e, j=j, n=n, ns=ns: e.activation(
                        xsq[0][:], x_f[:, j, ns:ns + NW], AF.Square), inc=s_act)
                    sqv = P.n(s_act)
                    P.wait("tensor", s_act, sqv)
                    P.emit("tensor", lambda e, j=j, ns=ns: e.matmul(
                        pst[:], onesc[:], x_f[:, j, ns:ns + NW],
                        start=(j == 0), stop=(j == KT - 1), skip_group_check=True),
                        inc=(s_pe if j == KT - 1 else None))
                    P.emit("tensor", lambda e, j=j: e.matmul(
                        pst2[:], onesc[:], xsq[0][:],
                        start=(j == 0), stop=(j == KT - 1), skip_group_check=True), inc=s_pe)
                    xsq_guard[0] = P.n(s_pe)
                pe_stats = P.n(s_pe)
                P.wait("vector", s_pe, pe_stats)
                P.emit("vector", lambda e: e.tensor_scalar(
                    out=srow[:, 0, :], in0=pst[:], scalar1=1.0 / D_MODEL, scalar2=None,
                    op0=ALU.mult), inc=s_ve)
                v_mean = P.n(s_ve)
                P.wait("vector", s_ve, v_mean)
                P.emit("vector", lambda e: e.tensor_tensor(
                    out=srow[:, 1, :], in0=srow[:, 0, :], in1=srow[:, 0, :], op=ALU.mult), inc=s_ve)
                P.wait("vector", s_ve, P.n(s_ve))
                P.emit("vector", lambda e: e.scalar_tensor_tensor(
                    out=srow[:, 1, :], in0=pst2[:], scalar=1.0 / D_MODEL, in1=srow[:, 1, :],
                    op0=ALU.mult, op1=ALU.subtract), inc=s_ve)
                pool.free(pst, s_ve)
                pool.free(pst2, s_ve)
                P.wait("vector", s_ve, P.n(s_ve))
                P.emit("vector", lambda e: e.tensor_scalar(
                    out=srow[:, 1, :], in0=srow[:, 1, :], scalar1=1e-5, scalar2=None,
                    op0=ALU.add), inc=s_ve)
                v_var = P.n(s_ve)
                P.wait("scalar", s_ve, v_var)
                P.emit("scalar", lambda e: e.activation(srow[:, 1, :], srow[:, 1, :],
                                                        AF.Ln), inc=s_act)
                P.wait("scalar", s_act, P.n(s_act))
                P.emit("scalar", lambda e: e.activation(srow[:, 1, :], srow[:, 1, :], AF.Exp,
                                                        scale=-0.5), inc=s_act)
                act_r = P.n(s_act)
                pool.use(pbc)
                P.wait("tensor", s_ve, v_mean)
                P.emit("tensor", lambda e: e.matmul(pbc[:], ones1c[:], srow[:, 0, :],
                                                    start=True, stop=True), inc=s_pe)
                pool.use(pbc2)
                P.wait("tensor", s_act, act_r)
                P.emit("tensor", lambda e: e.matmul(pbc2[:], ones1c[:], srow[:, 1, :],
                                                    start=True, stop=True), inc=s_pe)
                pe_bc = P.n(s_pe)
                P.wait("vector", s_pe, pe_bc)
                P.wait("vector", s_ve, x4_done)
                for j in range(KT):
                    P.emit("vector", lambda e, j=j, ns=ns: e.tensor_tensor(
                        out=x_f[:, j, ns:ns + NW], in0=x_f[:, j, ns:ns + NW], in1=pbc[:],
                        op=ALU.subtract), inc=s_ve)
                    P.wait("vector", s_ve, P.n(s_ve))
                    P.emit("vector", lambda e, j=j, ns=ns: e.tensor_tensor(
                        out=x_f[:, j, ns:ns + NW], in0=x_f[:, j, ns:ns + NW], in1=pbc2[:],
                        op=ALU.mult), inc=s_ve)
                pool.free(pbc, s_ve)
                pool.free(pbc2, s_ve)
                v_out = P.n(s_ve)
                P.wait("sync", s_ve, v_out)
                P.emit("sync", lambda e, tc=tc, ns=ns: e.dma_start(
                    out=out_r[:, :, tc + ns:tc + ns + NW], in_=x_f[:, :, ns:ns + NW]),
                    inc=s_st, dma=True)
            xf_guard = P.n(s_st)

        P.wait("sync", s_st, P.n(s_st))



def build_ffn_nc():
    nc = bass.Bass("TRN2", target_bir_lowering=False)
    d = {}
    d["x"] = nc.dram_tensor("x", [D_MODEL, SEQ], F32, kind="ExternalInput").ap()
    d["w1t"] = nc.dram_tensor("w1t", [128, 32, 8, 128], F32, kind="ExternalInput").ap()
    d["w2t"] = nc.dram_tensor("w2t", [128, 32, 8, 128], F32, kind="ExternalInput").ap()
    d["h1t"] = nc.dram_tensor("h1t", [128, 32, 32, 128], F32, kind="ExternalInput").ap()
    d["h2t"] = nc.dram_tensor("h2t", [128, 32, 32, 128], F32, kind="ExternalInput").ap()
    d["w3t"] = nc.dram_tensor("w3t", [128, 8, 32, 128], F32, kind="ExternalInput").ap()
    d["s1"] = nc.dram_tensor("s1", [128, 32], F32, kind="ExternalInput").ap()
    d["s2"] = nc.dram_tensor("s2", [128, 32], F32, kind="ExternalInput").ap()
    d["sh1"] = nc.dram_tensor("sh1", [128, 32], F32, kind="ExternalInput").ap()
    d["sh2"] = nc.dram_tensor("sh2", [128, 32], F32, kind="ExternalInput").ap()
    d["s3"] = nc.dram_tensor("s3", [128, 8], F32, kind="ExternalInput").ap()
    d["ones"] = nc.dram_tensor("ones", [128, 1], F32, kind="ExternalInput").ap()
    d["ones1"] = nc.dram_tensor("ones1", [1, 128], F32, kind="ExternalInput").ap()
    d["out"] = nc.dram_tensor("out", [D_MODEL, SEQ], F32, kind="ExternalOutput").ap()
    with ExitStack() as ctx:
        block = ctx.enter_context(nc.Block())
        P = Prog(ctx, nc)
        pool = PsumPool(P)
        emit_ffn(nc, P, pool, d)
        P.build_block(block)
    return nc

def _t(a):
    return np.ascontiguousarray(np.asarray(a, np.float32).T)


def _wrl(a):
    K, F = a.shape
    return np.ascontiguousarray(a.reshape(K // 128, 128, F // 512, 512).transpose(1, 2, 0, 3))


def _wrl128(a):
    K, F = a.shape
    return np.ascontiguousarray(a.reshape(K // 128, 128, F // 128, 128).transpose(1, 2, 0, 3))


def _col(s, nt):
    return np.ascontiguousarray(np.asarray(s, np.float32).reshape(nt, 128).T)


def host_ffn_inputs(x_fm_core, w):
    return dict(
        x=np.ascontiguousarray(x_fm_core, dtype=np.float32),
        w1t=_wrl128(_t(w["w1_w"])), w2t=_wrl128(_t(w["w2_w"])), h1t=_wrl128(_t(w["h1_w"])),
        h2t=_wrl128(_t(w["h2_w"])), w3t=_wrl128(_t(w["w3_w"])),
        s1=_col(w["w1_s"], 32), s2=_col(w["w2_s"], 32),
        sh1=_col(w["h1_s"], 32), sh2=_col(w["h2_s"], 32), s3=_col(w["w3_s"], 8),
        ones=np.ones((128, 1), np.float32), ones1=np.ones((1, 128), np.float32),
    )


def emit_attn(nc, P, pool, d, pfx="d"):
    NQT = SEQ // 128
    NQB = SEQ // 512
    C = min(512, SEQ)
    NCH = SEQ // C
    NW = min(512, C)
    NN = C // NW
    x_d = d["x"]
    wqt_d = d["wqt"]
    wkt_d = d["wkt"]
    wvt_d = d["wvt"]
    wot_d = d["wot"]
    maskS_d = d["maskS"]
    maskT_d = d["maskT"]
    ones_d = d["ones"]
    ones1_d = d["ones1"]
    identbf_d = d["identbf"]
    identf_d = d["identf"]
    out_d = d["out"]

    wr = {w.name: w for w in (wqt_d, wkt_d, wvt_d, wot_d)}
    x_r = x_d.rearrange("(k p) t -> p k t", p=128)
    out_r = out_d.rearrange("(k p) t -> p k t", p=128)


    with ExitStack() as ctx:
        x_bf = _sb(ctx, nc, pfx + "x_bf", [128, KT, SEQ], BF16)
        attn_bf = x_bf
        q_bf = _sb(ctx, nc, pfx + "q_bf", [128, KT, SEQ], BF16)
        k_bf = _sb(ctx, nc, pfx + "k_bf", [128, KT, SEQ], BF16)
        v_tm = _sb(ctx, nc, pfx + "v_tm", [128, NQT, 1024], BF16)
        x_f = _sb(ctx, nc, pfx + "x_f", [128, KT, C], F32)
        wbuf = [_sb(ctx, nc, pfx + f"wbuf{i}", [128, KT, 512], BF16) for i in range(2)]
        maskS = _sb(ctx, nc, pfx + "maskS_s", [128, 4, 512], BF16)
        maskT = _sb(ctx, nc, pfx + "maskT_s", [128, 4, 512], BF16)
        identbf = _sb(ctx, nc, pfx + "identbf_s", [128, 128], BF16)
        identf = _sb(ctx, nc, pfx + "identf_s", [128, 128], F32)
        onesc = _sb(ctx, nc, pfx + "onesc", [128, 1], F32)
        ones1c = _sb(ctx, nc, pfx + "ones1c", [1, 128], F32)
        pT = [_sb(ctx, nc, pfx + f"pT{i}", [128, 512], BF16) for i in range(4)]
        dummy = _sb(ctx, nc, pfx + "escratch", [128, 512], BF16)
        mstat = _sb(ctx, nc, pfx + "mstat", [128, 32], F32)
        sstat = _sb(ctx, nc, pfx + "sstat", [128, 32], F32)
        mqv = _sb(ctx, nc, pfx + "mqv", [128, 8], F32)
        fstat = _sb(ctx, nc, pfx + "fstat", [128, 32], F32)
        scol4 = _sb(ctx, nc, pfx + "scol4", [128, 8], F32)
        lns4 = _sb(ctx, nc, pfx + "lns4", [128, 8], F32)
        biasa = _sb(ctx, nc, pfx + "biasa", [128, 8], F32)
        btrow = [_sb(ctx, nc, pfx + f"btrow{i}", [1, 512], F32) for i in range(2)]
        mbx = [_sb(ctx, nc, pfx + f"mbx{i}", [128, 512], F32) for i in range(2)]
        tmpx = [_sb(ctx, nc, pfx + f"tmpx{i}", [128, 512], F32) for i in range(4)]
        xsq = _sb(ctx, nc, pfx + "xsq", [128, 512], F32)
        srow = _sb(ctx, nc, pfx + "srow", [1, 2, 512], F32)

        ps_s = [_ps(ctx, nc, pfx + f"ps_s{i}", [128, 512]) for i in range(4)]
        ps_t = [_ps(ctx, nc, pfx + f"ps_t{i}", [128, 512]) for i in range(2)]
        ps_pv = [_ps(ctx, nc, pfx + f"ps_pv{i}", [128, 512]) for i in range(2)]
        pst, pst2, pbc, pbc2 = ps_s[0], ps_s[1], ps_s[2], ps_s[3]
        big = ps_t + ps_pv

        s_lc = P.sem(pfx + "s_lc", "sync")
        s_lx = P.sem(pfx + "s_lx", "sync")
        s_wx = P.sem(pfx + "s_wx", "gpsimd")
        s_wb = [P.sem(pfx + "s_wb0", "gpsimd"), P.sem(pfx + "s_wb1", "gpsimd")]
        s_pe = P.sem(pfx + "s_pe", "tensor")
        s_act = P.sem(pfx + "s_act", "scalar")
        s_ve = P.sem(pfx + "s_ve", "vector")
        s_st = P.sem(pfx + "s_st", "sync")

        for dst, src in [(identf, identf_d), (onesc, ones_d), (ones1c, ones1_d)]:
            P.emit("sync", lambda e, dst=dst, src=src: e.dma_start(out=dst[:], in_=src),
                   inc=s_lc, dma=True)
        ld_prolog = P.n(s_lc)
        for eng in ("tensor", "scalar", "vector"):
            P.wait(eng, s_lc, ld_prolog)
        for dst, src in [(maskS, maskS_d), (maskT, maskT_d), (identbf, identbf_d)]:
            P.emit("gpsimd", lambda e, dst=dst, src=src: e.dma_start(out=dst[:], in_=src),
                   inc=s_wx, dma=True)
        P.emit("gpsimd", lambda e: e.dma_start(out=x_bf[:], in_=x_r), inc=s_wx, dma=True)
        P.wait("tensor", s_wx, P.n(s_wx))

        wfree = []

        def stream_w(src_r, fh):
            idx = len(wfree)
            par = idx % 2
            if idx >= 2:
                P.wait("gpsimd", *wfree[idx - 2])
            P.emit("gpsimd", lambda e, src_r=src_r, fh=fh, par=par: e.dma_start(
                out=wbuf[par][:], in_=src_r[:, fh]),
                inc=s_wb[par], dma=True)
            wfree.append(None)
            return idx, par, (s_wb[par], P.n(s_wb[par]))

        for wi, wname in enumerate(["wqt", "wkt", "wvt"]):
            for fh in range(2):
                idx, par, (wsem, wv) = stream_w(wr[wname], fh)
                P.wait("tensor", wsem, wv)
                if wi < 2:
                    dst = q_bf if wi == 0 else k_bf
                    for ft in range(4):
                        fg = fh * 4 + ft
                        for n in range(SEQ // 512):
                            G = big[(ft * (SEQ // 512) + n) % 4]
                            pool.use(G)
                            for k in range(KT):
                                P.emit("tensor", lambda e, k=k, par=par, ft=ft, n=n, G=G: e.matmul(
                                    G[:], wbuf[par][:, k, ft * 128:(ft + 1) * 128],
                                    x_bf[:, k, n * 512:(n + 1) * 512],
                                    start=(k == 0), stop=(k == KT - 1)),
                                    inc=(s_pe if k == KT - 1 else None))
                            pe_g = P.n(s_pe)
                            P.wait("scalar", s_pe, pe_g)
                            P.emit("scalar", lambda e, dst=dst, fg=fg, n=n, G=G, sc=(0.125 if wi == 0 else 1.0): e.activation(
                                dst[:, fg, n * 512:(n + 1) * 512], G[:], AF.Copy, scale=sc), inc=s_act)
                            pool.free(G, s_act)
                else:
                    for tt in range(NQT):
                        G = big[tt % 4]
                        pool.use(G)
                        for k in range(KT):
                            P.emit("tensor", lambda e, k=k, par=par, tt=tt, G=G: e.matmul(
                                G[:], x_bf[:, k, tt * 128:(tt + 1) * 128],
                                wbuf[par][:, k, 0:512],
                                start=(k == 0), stop=(k == KT - 1)),
                                inc=(s_pe if k == KT - 1 else None))
                        pe_g = P.n(s_pe)
                        P.wait("scalar", s_pe, pe_g)
                        P.emit("scalar", lambda e, tt=tt, fh=fh, G=G: e.activation(
                            v_tm[:, tt, fh * 512:(fh + 1) * 512], G[:], AF.Copy),
                            inc=s_act)
                        pool.free(G, s_act)
                wfree[idx] = (s_pe, P.n(s_pe))
        qkv_done_act = P.n(s_act)
        P.wait("tensor", s_act, qkv_done_act)

        pt_guard = [0, 0, 0, 0]
        tmp_guard = [0, 0, 0, 0]
        stats_exp_done = [0, 0]
        ln_done = [0, 0]
        biasa_stt = [0, 0]
        bias_tr = [0, 0]
        biasT2_g = [0, 0]
        fexp_done = [0, 0]
        for hb in range(KT):
            for qb in range(NQB):
                PV = ps_pv[qb % 2]
                pool.use(PV)
                nch = qb + 1
                if stats_exp_done[0]:
                    P.wait("vector", s_act, max(stats_exp_done))
                P.emit("vector", lambda e: e.memset(mstat[:], 1e30), inc=s_ve)
                P.emit("vector", lambda e: e.memset(sstat[:], 0.0), inc=s_ve)
                for qi in range(4):
                    qt = qb * 4 + qi
                    for ho2 in range(2):
                        ho = ho2 * 64
                        tp = (ho, 0)
                        sl = (2 * qi + ho2) * nch
                        for kc in range(nch):
                            S = ps_s[(sl + kc) % 4]
                            pool.use(S)
                            last = kc == nch - 1
                            P.emit("tensor", lambda e, S=S, ho=ho, hb=hb, qt=qt, kc=kc, tp=tp, last=last: e.matmul(
                                S[:], q_bf[ho:ho + 64, hb, qt * 128:(qt + 1) * 128].squeeze(),
                                k_bf[ho:ho + 64, hb, kc * 512:(kc + 1) * 512].squeeze(),
                                start=True, stop=not last, tile_position=tp,
                                skip_group_check=True),
                                inc=(None if last else s_pe))
                            if last:
                                P.emit("tensor", lambda e, S=S, qi2=qt % 4: e.matmul(
                                    S[:], identbf[:], maskS[:, qi2, :].squeeze(),
                                    start=False, stop=True, skip_group_check=True), inc=s_pe)
                        pe_s = P.n(s_pe)
                        P.wait("vector", s_pe, pe_s)
                        for kc in range(nch):
                            P.emit("vector", lambda e, kc=kc, sl=sl, c0=ho2 * 16 + qi * 4, nch=nch: e.tensor_reduce(
                                out=mstat[:, c0 + kc:c0 + kc + 1], in_=ps_s[(sl + kc) % 4][:],
                                axis=mybir.AxisListType.X, op=ALU.max, negate=True), inc=s_ve)
                        v_red = P.n(s_ve)
                        P.wait("scalar", s_ve, v_red)
                        for kc in range(nch):
                            P.emit("scalar", lambda e, kc=kc, sl=sl, c0=ho2 * 16 + qi * 4, nch=nch: e.activation(
                                dummy[:], ps_s[(sl + kc) % 4][:], AF.Exp, scale=1.0,
                                bias=mstat[:, c0 + kc:c0 + kc + 1],
                                accum_out=sstat[:, c0 + kc:c0 + kc + 1]), inc=s_act)
                            pool.free(ps_s[(sl + kc) % 4], s_act)
                stats_exp_done[0] = stats_exp_done[1] = P.n(s_act)
                P.wait("vector", s_act, stats_exp_done[0])
                for m0 in range(8):
                    P.emit("vector", lambda e, m0=m0: e.tensor_reduce(
                        out=mqv[:, m0:m0 + 1], in_=mstat[:, m0 * 4:m0 * 4 + 4],
                        axis=mybir.AxisListType.X, op=ALU.min), inc=s_ve)
                P.wait("vector", s_ve, P.n(s_ve))
                P.emit("vector", lambda e: e.tensor_tensor(
                    out=fstat[:].rearrange("p (m k) -> p m k", k=4),
                    in0=mqv[:].unsqueeze(2).broadcast_to([128, 8, 4]),
                    in1=mstat[:].rearrange("p (m k) -> p m k", k=4),
                    op=ALU.subtract), inc=s_ve)
                v_fs = P.n(s_ve)
                P.wait("scalar", s_ve, v_fs)
                P.emit("scalar", lambda e: e.activation(
                    fstat[:], fstat[:], AF.Exp), inc=s_act)
                fexp_done[0] = fexp_done[1] = P.n(s_act)
                P.wait("vector", s_act, fexp_done[0])
                P.emit("vector", lambda e: e.tensor_tensor(
                    out=fstat[:], in0=fstat[:], in1=sstat[:], op=ALU.mult), inc=s_ve)
                P.wait("vector", s_ve, P.n(s_ve))
                for m0 in range(8):
                    P.emit("vector", lambda e, m0=m0: e.tensor_reduce(
                        out=scol4[:, m0:m0 + 1], in_=fstat[:, m0 * 4:m0 * 4 + 4],
                        axis=mybir.AxisListType.X, op=ALU.add), inc=s_ve)
                P.wait("vector", s_ve, P.n(s_ve))
                P.emit("vector", lambda e: e.tensor_scalar(
                    out=scol4[:], in0=scol4[:], scalar1=1e-8, scalar2=None,
                    op0=ALU.add), inc=s_ve)
                v_sum = P.n(s_ve)
                P.wait("scalar", s_ve, v_sum)
                if biasa_stt[0]:
                    P.wait("scalar", s_ve, max(biasa_stt))
                P.emit("scalar", lambda e: e.activation(
                    lns4[:], scol4[:], AF.Ln), inc=s_act)
                ln_done[0] = ln_done[1] = P.n(s_act)
                P.wait("vector", s_act, ln_done[0])
                if bias_tr[0]:
                    P.wait("vector", s_pe, max(bias_tr))
                P.emit("vector", lambda e: e.scalar_tensor_tensor(
                    out=biasa[:], in0=lns4[:], scalar=-1.0,
                    in1=mqv[:], op0=ALU.mult, op1=ALU.add), inc=s_ve)
                biasa_stt[0] = biasa_stt[1] = P.n(s_ve)
                P.wait("tensor", s_ve, biasa_stt[0])
                for ho2 in range(2):
                    T0 = ps_t[ho2]
                    pool.use(T0)
                    if biasT2_g[ho2]:
                        P.wait("vector", s_pe, biasT2_g[ho2])
                    for qi in range(4):
                        P.emit("tensor", lambda e, m0=ho2 * 4 + qi, qi=qi, T0=T0: e.transpose(
                            T0[0:1, qi * 128:(qi + 1) * 128], biasa[:, m0:m0 + 1], identf[:]),
                            inc=(s_pe if qi == 3 else None))
                    bias_tr[ho2] = P.n(s_pe)
                    P.wait("vector", s_pe, bias_tr[ho2])
                    P.emit("vector", lambda e, ho2=ho2, T0=T0: e.tensor_copy(
                        btrow[ho2][:], T0[0:1, 0:512]), inc=s_ve)
                    pool.free(T0, s_ve)
                for ho2 in range(2):
                    MB = ps_t[ho2]
                    pool.use(MB)
                    P.wait("tensor", s_ve, P.n(s_ve))
                    P.emit("tensor", lambda e, ho2=ho2, MB=MB: e.matmul(
                        MB[:], ones1c[:], btrow[ho2][:],
                        start=True, stop=True, skip_group_check=True), inc=s_pe)
                    biasT2_g[ho2] = P.n(s_pe)
                    P.wait("vector", s_pe, biasT2_g[ho2])
                    P.emit("vector", lambda e, ho2=ho2, MB=MB: e.tensor_copy(
                        mbx[ho2][:], MB[:]), inc=s_ve)
                    pool.free(MB, s_ve)
                P.wait("vector", s_ve, P.n(s_ve))
                Tbanks = [ps_t[0], ps_t[1], ps_s[0], ps_s[1]]
                pend = []

                def emit_pv(ktt, ho2, par, act_pt, PV=PV, hb=hb, qb=qb):
                    ho = ho2 * 64
                    h = 2 * hb + ho2
                    P.wait("tensor", s_act, act_pt)
                    P.emit("tensor", lambda e, PV=PV, ho=ho, h=h, ktt=ktt, par=par, qb=qb: e.matmul(
                        PV[ho:ho + 64, :], v_tm[:, ktt, h * 64:h * 64 + 64],
                        pT[par][:], start=(ktt == 0), stop=(ktt == 4 * qb + 3),
                        tile_position=(0, ho), skip_group_check=True), inc=s_pe)
                    pt_guard[par] = P.n(s_pe)

                for ktt in range(4 * qb + 4):
                    for ho2 in range(2):
                        ho = ho2 * 64
                        tp = (ho, 0)
                        par = (2 * ktt + ho2) % 4
                        T = Tbanks[par]
                        pool.use(T)
                        diag = ktt >= 4 * qb
                        P.emit("tensor", lambda e, T=T, ho=ho, hb=hb, ktt=ktt, qb=qb, tp=tp, diag=diag: e.matmul(
                            T[:], k_bf[ho:ho + 64, hb, ktt * 128:(ktt + 1) * 128].squeeze(),
                            q_bf[ho:ho + 64, hb, qb * 512:(qb + 1) * 512].squeeze(),
                            start=True, stop=not diag, tile_position=tp,
                            skip_group_check=True), inc=(None if diag else s_pe))
                        if diag:
                            P.emit("tensor", lambda e, T=T, r=ktt - 4 * qb: e.matmul(
                                T[:], identbf[:], maskT[:, r, :].squeeze(),
                                start=False, stop=True, skip_group_check=True), inc=s_pe)
                        pe_t = P.n(s_pe)
                        P.wait("vector", s_pe, pe_t)
                        if tmp_guard[par]:
                            P.wait("vector", s_act, tmp_guard[par])
                        P.emit("vector", lambda e, T=T, par=par, ho2=ho2: e.tensor_tensor(
                            out=tmpx[par][:], in0=T[:], in1=mbx[ho2][:], op=ALU.add), inc=s_ve)
                        pool.free(T, s_ve)
                        v_tmp = P.n(s_ve)
                        P.wait("scalar", s_ve, v_tmp)
                        if pt_guard[par]:
                            P.wait("scalar", s_pe, pt_guard[par])
                        P.emit("scalar", lambda e, par=par: e.activation(
                            pT[par][:], tmpx[par][:], AF.Exp, scale=1.0), inc=s_act)
                        tmp_guard[par] = P.n(s_act)
                        pend.append((ktt, ho2, par, P.n(s_act)))
                        if len(pend) > 2:
                            emit_pv(*pend.pop(0))
                while pend:
                    emit_pv(*pend.pop(0))
                pe_pv = P.n(s_pe)
                P.wait("scalar", s_pe, pe_pv)
                P.emit("scalar", lambda e, PV=PV, hb=hb, qb=qb: e.activation(
                    attn_bf[:, hb, qb * 512:(qb + 1) * 512], PV[:], AF.Copy),
                    inc=s_act)
                pool.free(PV, s_act)
        attn_done_act = P.n(s_act)
        P.wait("tensor", s_act, attn_done_act)

        xf_guard = 0
        xsq_guard = [0]
        for c in range(NCH):
            tc = c * C
            if xf_guard:
                P.wait("sync", s_st, xf_guard)
            P.emit("sync", lambda e, tc=tc: e.dma_start(out=x_f[:], in_=x_r[:, :, tc:tc + C]),
                   inc=s_lx, dma=True)
            x_f_ready = P.n(s_lx)
            P.wait("vector", s_lx, x_f_ready)
            for fh in range(2):
                idx, par, (wsem, wv) = stream_w(wr["wot"], fh)
                P.wait("tensor", wsem, wv)
                for ft in range(4):
                    j3 = fh * 4 + ft
                    for n in range(NN):
                        G = big[(ft * NN + n) % 4]
                        pool.use(G)
                        for k in range(KT):
                            P.emit("tensor", lambda e, k=k, par=par, ft=ft, n=n, G=G, tc=tc: e.matmul(
                                G[:], wbuf[par][:, k, ft * 128:(ft + 1) * 128],
                                attn_bf[:, k, tc + n * NW:tc + (n + 1) * NW],
                                start=(k == 0), stop=(k == KT - 1)),
                                inc=(s_pe if k == KT - 1 else None))
                        pe_g = P.n(s_pe)
                        P.wait("vector", s_pe, pe_g)
                        P.emit("vector", lambda e, j3=j3, n=n, G=G: e.tensor_tensor(
                            out=x_f[:, j3, n * NW:(n + 1) * NW], in0=G[:],
                            in1=x_f[:, j3, n * NW:(n + 1) * NW], op=ALU.add), inc=s_ve)
                        pool.free(G, s_ve)
                wfree[idx] = (s_pe, P.n(s_pe))
            x4_done = P.n(s_ve)

            for n in range(NN):
                ns = n * NW
                P.wait("scalar", s_ve, x4_done)
                pool.use(pst)
                pool.use(pst2)
                P.wait("tensor", s_ve, x4_done)
                for j in range(KT):
                    if xsq_guard[0]:
                        P.wait("scalar", s_pe, xsq_guard[0])
                    P.emit("scalar", lambda e, j=j, ns=ns: e.activation(
                        xsq[:], x_f[:, j, ns:ns + NW], AF.Square), inc=s_act)
                    sqv = P.n(s_act)
                    P.wait("tensor", s_act, sqv)
                    P.emit("tensor", lambda e, j=j, ns=ns: e.matmul(
                        pst[:1, :NW], onesc[:], x_f[:, j, ns:ns + NW],
                        start=(j == 0), stop=(j == KT - 1), skip_group_check=True),
                        inc=(s_pe if j == KT - 1 else None))
                    P.emit("tensor", lambda e, j=j: e.matmul(
                        pst2[:1, :NW], onesc[:], xsq[:, :NW],
                        start=(j == 0), stop=(j == KT - 1), skip_group_check=True), inc=s_pe)
                    xsq_guard[0] = P.n(s_pe)
                pe_stats = P.n(s_pe)
                P.wait("vector", s_pe, pe_stats)
                P.emit("vector", lambda e: e.tensor_scalar(
                    out=srow[:, 0, :], in0=pst[:1, :NW], scalar1=1.0 / D_MODEL, scalar2=None,
                    op0=ALU.mult), inc=s_ve)
                v_mean = P.n(s_ve)
                P.wait("vector", s_ve, v_mean)
                P.emit("vector", lambda e: e.tensor_tensor(
                    out=srow[:, 1, :], in0=srow[:, 0, :], in1=srow[:, 0, :], op=ALU.mult), inc=s_ve)
                P.wait("vector", s_ve, P.n(s_ve))
                P.emit("vector", lambda e: e.scalar_tensor_tensor(
                    out=srow[:, 1, :], in0=pst2[:1, :NW], scalar=1.0 / D_MODEL, in1=srow[:, 1, :],
                    op0=ALU.mult, op1=ALU.subtract), inc=s_ve)
                pool.free(pst, s_ve)
                pool.free(pst2, s_ve)
                P.wait("vector", s_ve, P.n(s_ve))
                P.emit("vector", lambda e: e.tensor_scalar(
                    out=srow[:, 1, :], in0=srow[:, 1, :], scalar1=1e-5, scalar2=None,
                    op0=ALU.add), inc=s_ve)
                v_var = P.n(s_ve)
                P.wait("scalar", s_ve, v_var)
                P.emit("scalar", lambda e: e.activation(srow[:, 1, :], srow[:, 1, :],
                                                        AF.Ln), inc=s_act)
                P.wait("scalar", s_act, P.n(s_act))
                P.emit("scalar", lambda e: e.activation(srow[:, 1, :], srow[:, 1, :],
                                                        AF.Exp, scale=-0.5), inc=s_act)
                act_r = P.n(s_act)
                pool.use(pbc)
                P.wait("tensor", s_ve, v_mean)
                P.emit("tensor", lambda e: e.matmul(pbc[:, :NW], ones1c[:], srow[:, 0, :],
                                                    start=True, stop=True,
                                                    skip_group_check=True))
                pool.use(pbc2)
                P.wait("tensor", s_act, act_r)
                P.emit("tensor", lambda e: e.matmul(pbc2[:, :NW], ones1c[:], srow[:, 1, :],
                                                    start=True, stop=True,
                                                    skip_group_check=True), inc=s_pe)
                pe_bc = P.n(s_pe)
                P.wait("vector", s_pe, pe_bc)
                P.wait("vector", s_ve, x4_done)
                for j in range(KT):
                    P.emit("vector", lambda e, j=j, ns=ns: e.tensor_tensor(
                        out=x_f[:, j, ns:ns + NW], in0=x_f[:, j, ns:ns + NW], in1=pbc[:, :NW],
                        op=ALU.subtract), inc=s_ve)
                    P.wait("vector", s_ve, P.n(s_ve))
                    P.emit("vector", lambda e, j=j, ns=ns: e.tensor_tensor(
                        out=x_f[:, j, ns:ns + NW], in0=x_f[:, j, ns:ns + NW], in1=pbc2[:, :NW],
                        op=ALU.mult), inc=s_ve)
                pool.free(pbc, s_ve)
                pool.free(pbc2, s_ve)
                v_out = P.n(s_ve)
                P.wait("sync", s_ve, v_out)
                P.emit("sync", lambda e, tc=tc, ns=ns: e.dma_start(
                    out=out_r[:, :, tc + ns:tc + ns + NW], in_=x_f[:, :, ns:ns + NW]),
                    inc=s_st, dma=True)
            xf_guard = P.n(s_st)

        P.wait("sync", s_st, P.n(s_st))



def build_attn_nc(dbg=False):
    nc = bass.Bass("TRN2", target_bir_lowering=False)
    d = {}
    if dbg:
        d["dbgb"] = nc.dram_tensor("dbgb", [128, 8, 4, 2, 4], F32, kind="ExternalOutput").ap()
        d["dbgc"] = nc.dram_tensor("dbgc", [128, 8, 4, 2, 4], F32, kind="ExternalOutput").ap()
        d["dbgd"] = nc.dram_tensor("dbgd", [128, 8, 4, 2, 4], F32, kind="ExternalOutput").ap()
        d["dbge"] = nc.dram_tensor("dbge", [128, 8, 4, 2, 16], F32, kind="ExternalOutput").ap()
    d["x"] = nc.dram_tensor("x", [D_MODEL, SEQ], F32, kind="ExternalInput").ap()
    d["wqt"] = nc.dram_tensor("wqt", [128, 2, 8, 512], F32, kind="ExternalInput").ap()
    d["wkt"] = nc.dram_tensor("wkt", [128, 2, 8, 512], F32, kind="ExternalInput").ap()
    d["wvt"] = nc.dram_tensor("wvt", [128, 2, 8, 512], F32, kind="ExternalInput").ap()
    d["wot"] = nc.dram_tensor("wot", [128, 2, 8, 512], F32, kind="ExternalInput").ap()
    d["maskS"] = nc.dram_tensor("maskS", [128, 4, 512], F32, kind="ExternalInput").ap()
    d["maskT"] = nc.dram_tensor("maskT", [128, 4, 512], F32, kind="ExternalInput").ap()
    d["ones"] = nc.dram_tensor("ones", [128, 1], F32, kind="ExternalInput").ap()
    d["ones1"] = nc.dram_tensor("ones1", [1, 128], F32, kind="ExternalInput").ap()
    d["identbf"] = nc.dram_tensor("identbf", [128, 128], F32, kind="ExternalInput").ap()
    d["identf"] = nc.dram_tensor("identf", [128, 128], F32, kind="ExternalInput").ap()
    d["out"] = nc.dram_tensor("out", [D_MODEL, SEQ], F32, kind="ExternalOutput").ap()
    with ExitStack() as ctx:
        block = ctx.enter_context(nc.Block())
        P = Prog(ctx, nc)
        pool = PsumPool(P)
        emit_attn(nc, P, pool, d)
        P.build_block(block)
    return nc

def host_attn_masks(seq):
    i = np.arange(128)
    tri_up = (i[None, :] > i[:, None]).astype(np.float32) * -1e30
    tri_lo = (i[:, None] > i[None, :]).astype(np.float32) * -1e30
    full = np.full((128, 128), -1e30, np.float32)
    zero = np.zeros((128, 128), np.float32)
    maskS = np.zeros((128, 4, 512), np.float32)
    maskT = np.zeros((128, 4, 512), np.float32)
    for s in range(4):
        for r in range(4):
            maskS[:, s, r * 128:(r + 1) * 128] = zero if r < s else (tri_up if r == s else full)
            maskT[:, r, s * 128:(s + 1) * 128] = full if s < r else (tri_lo if s == r else zero)
    return maskS, maskT


def host_attn_inputs(x_fm_core, w):
    maskS, maskT = host_attn_masks(SEQ)
    return dict(
        x=np.ascontiguousarray(x_fm_core, dtype=np.float32),
        wqt=_wrl(_t(w["wq"])), wkt=_wrl(_t(w["wk"])), wvt=_wrl(_t(w["wv"])), wot=_wrl(_t(w["wo"])),
        maskS=maskS, maskT=maskT,
        ones=np.ones((128, 1), np.float32), ones1=np.ones((1, 128), np.float32),
        identbf=np.eye(128, dtype=np.float32), identf=np.eye(128, dtype=np.float32),
    )
def emit_mamba(nc, P, pool, d, pfx="a"):
    DI = 2048
    NH = 32
    NP = 16
    CD = 2176
    C = 512
    NCH = SEQ // C
    NQ = C // 128
    RMS_EPS = 1.1920929e-07

    x_d = d["x"]
    wzt_d = d["wzt"]
    wxt_d = d["wxt"]
    wdt_d = d["wdt"]
    wot_d = d["wot_m"]
    normw_d = d["normw"]
    gnw_d = d["gnw"]
    convw_d = d["convw"]
    convb_d = d["convb"]
    drep_d = d["drep"]
    dtb_d = d["dtb"]
    alog_d = d["alog"]
    rw1t_d = d["rw1t"]
    rb1_d = d["rb1"]
    rw2t_d = d["rw2t"]
    rb2_d = d["rb2"]
    rwt_d = d["rwt"]
    rbr_d = d["rbr"]
    ones_d = d["ones"]
    ones1_d = d["ones1"]
    identf_d = d["identf"]
    identb_d = d["identb"]
    triu_d = d["triu"]
    negblk_d = d["negblk"]
    maskd_d = d["maskd"]
    out_d = d["out"]
    xtm_d = d["xtm"]
    lg_d = d["logits"]
    cbounce = d["cbounce"]

    x_r = x_d.rearrange("(k p) t -> p k t", p=128)
    out_r = out_d.rearrange("(k p) t -> p k t", p=128)
    wzt_r = wzt_d
    wxt_r = wxt_d
    wdt_r = wdt_d.rearrange("(k p) f -> p k f", p=128)
    wot_r = wot_d


    with ExitStack() as ctx:
        xnb = _sb(ctx, nc, pfx + "xnb", [128, KT, SEQ], BF16)
        rstd_b = _sb(ctx, nc, pfx + "rstd_b", [128, SEQ], BF16)
        x_c = _sb(ctx, nc, pfx + "x_c", [128, KT, C], F32)
        xBCc = _sb(ctx, nc, pfx + "xBCc", [128, 17, C + 3], BF16)
        xbc = _sb(ctx, nc, pfx + "xbc", [128, 17, C], BF16)
        xh_tm = _sb(ctx, nc, pfx + "xh_tm", [128, NQ, DI], BF16)
        ycb = _sb(ctx, nc, pfx + "ycb", [128, NP, C], BF16)
        x1c = _sb(ctx, nc, pfx + "x1c", [128, KT, C], F32)
        wsb = [_sb(ctx, nc, pfx + f"wsb{i}", [128, 16, 128], BF16) for i in range(2)]
        wdtb = _sb(ctx, nc, pfx + "wdtb", [128, KT, 32], BF16)
        dt_tm = _sb(ctx, nc, pfx + "dt_tm", [128, NQ, 32], F32)
        W_all = _sb(ctx, nc, pfx + "W_all", [128, NH, 128], BF16)
        EbAll = _sb(ctx, nc, pfx + "EbAll", [128, NP, 128], BF16)
        CbAll = _sb(ctx, nc, pfx + "CbAll", [128, NP, 128], F32)
        H_all = _sb(ctx, nc, pfx + "H_all", [128, NP, 64], F32)
        E_q = _sb(ctx, nc, pfx + "E_q", [128, 1024], BF16)
        G_sb = _sb(ctx, nc, pfx + "G_sb", [128, 128], F32)
        cum_tm = _sb(ctx, nc, pfx + "cum_tm", [128, 32], F32)
        cumfmq = _sb(ctx, nc, pfx + "cumfmq", [8, 4, 128], F32)
        cumflat = _sb(ctx, nc, pfx + "cumflat", [1, 2048], F32)
        dA = _sb(ctx, nc, pfx + "dA", [128, 32], F32)
        totr = _sb(ctx, nc, pfx + "totr", [1, 32], F32)
        tot_b = _sb(ctx, nc, pfx + "tot_b", [128, 32], F32)
        etot_b = _sb(ctx, nc, pfx + "etot_b", [128, 32], F32)
        etot_p = _sb(ctx, nc, pfx + "etot_p", [128, NP], F32)
        w2a = _sb(ctx, nc, pfx + "w2a", [128, 32], F32)
        dtw2 = _sb(ctx, nc, pfx + "dtw2", [128, 32], F32)
        XdtA = _sb(ctx, nc, pfx + "XdtA", [128, 1024], BF16)
        XwA = _sb(ctx, nc, pfx + "XwA", [128, 1024], BF16)
        BCtm = _sb(ctx, nc, pfx + "BCtm", [128, 4, 128], BF16)
        Clo4 = _sb(ctx, nc, pfx + "Clo4", [64, 4, 128], BF16)
        sp1 = _sb(ctx, nc, pfx + "sp1", [128, 32], F32)
        sp2 = _sb(ctx, nc, pfx + "sp2", [128, 32], F32)
        zsil = _sb(ctx, nc, pfx + "zsil", [128, C], F32)
        xsqa = _sb(ctx, nc, pfx + "xsqa", [128, 512], F32)
        xnt = _sb(ctx, nc, pfx + "xnt", [128, 512], F32)
        meanacc = _sb(ctx, nc, pfx + "meanacc", [128, KT], F32)
        hrt = _sb(ctx, nc, pfx + "hrt", [32, 1], F32)
        prb = _sb(ctx, nc, pfx + "prb", [1, 8], F32)
        pfc = _sb(ctx, nc, pfx + "pfc", [128, 1], F32)
        pmc = _sb(ctx, nc, pfx + "pmc", [128, 1], F32)
        srow = _sb(ctx, nc, pfx + "srow_m", [1, 2, 512], F32)
        lgb = _sb(ctx, nc, pfx + "lgb", [128, NQ, 8], F32)
        normw = _sb(ctx, nc, pfx + "normw_s", [128, KT], F32)
        gnw = _sb(ctx, nc, pfx + "gnw_s", [128, NP], F32)
        convw = _sb(ctx, nc, pfx + "convw_s", [128, 17, 4], F32)
        convb = _sb(ctx, nc, pfx + "convb_s", [128, 17], F32)
        drep = _sb(ctx, nc, pfx + "drep_s", [128, NP], F32)
        dtbr = _sb(ctx, nc, pfx + "dtbr_s", [1, 32], F32)
        alogb = _sb(ctx, nc, pfx + "alog_s", [128, 32], F32)
        aneg = _sb(ctx, nc, pfx + "aneg_s", [128, 32], F32)
        rw1t = _sb(ctx, nc, pfx + "rw1t_s", [128, KT, 32], F32)
        rb1c = _sb(ctx, nc, pfx + "rb1_s", [32, 1], F32)
        rw2t = _sb(ctx, nc, pfx + "rw2t_s", [32, 3], F32)
        rb2r = _sb(ctx, nc, pfx + "rb2_s", [1, 3], F32)
        rwt = _sb(ctx, nc, pfx + "rwt_s", [128, KT, 8], F32)
        rbr = _sb(ctx, nc, pfx + "rbr_s", [1, 8], F32)
        onesc = _sb(ctx, nc, pfx + "onesc_m", [128, 1], F32)
        ones1c = _sb(ctx, nc, pfx + "ones1c_m", [1, 128], F32)
        identf = _sb(ctx, nc, pfx + "identf_m", [128, 128], F32)
        triu = _sb(ctx, nc, pfx + "triu_m", [128, 128], F32)
        negblk = _sb(ctx, nc, pfx + "negblk_m", [8, 1024], F32)
        maskd = _sb(ctx, nc, pfx + "maskd_m", [128, 512], F32)

        T1b = _ps(ctx, nc, pfx + "T1b", [128, 1024])
        T1m = _ps(ctx, nc, pfx + "T1m", [128, 1024])
        gm = _ps(ctx, nc, pfx + "gm", [128, 512])
        io = _ps(ctx, nc, pfx + "io", [128, 512])
        py = _ps(ctx, nc, pfx + "py", [128, 128])
        pu = _ps(ctx, nc, pfx + "pu", [128, 64])
        T1b_h = [T1b[:, 0:512], T1b[:, 512:1024]]
        T1m_h = [T1m[:, 0:512], T1m[:, 512:1024]]

        s_lc = P.sem(pfx + "s_lc", "sync")
        s_lx = P.sem(pfx + "s_lx", "sync")
        s_wx = P.sem(pfx + "s_wx", "gpsimd")
        s_ws = [P.sem(pfx + "s_ws0", "gpsimd"), P.sem(pfx + "s_ws1", "gpsimd")]
        s_cb = [P.sem(pfx + "s_cb0", "sync"), P.sem(pfx + "s_cb1", "sync")]
        s_tp = P.sem(pfx + "s_tp", "scalar")
        s_pe = P.sem(pfx + "s_pe", "tensor")
        s_act = P.sem(pfx + "s_act", "scalar")
        s_ve = P.sem(pfx + "s_ve", "vector")
        s_st = P.sem(pfx + "s_st", "sync")

        def chain(eng):
            P.wait(eng, {"tensor": s_pe, "scalar": s_act, "vector": s_ve}[eng])

        for dst, src in [(normw, normw_d), (gnw, gnw_d), (convw, convw_d), (convb, convb_d),
                         (drep, drep_d), (dtbr, dtb_d), (alogb, alog_d), (rw1t, rw1t_d),
                         (rb1c, rb1_d), (rw2t, rw2t_d), (rb2r, rb2_d), (rwt, rwt_d),
                         (rbr, rbr_d), (onesc, ones_d), (ones1c, ones1_d), (identf, identf_d),
                         (triu, triu_d), (negblk, negblk_d), (maskd, maskd_d)]:
            P.emit("sync", lambda e, dst=dst, src=src: e.dma_start(out=dst[:], in_=src),
                   inc=s_lc, dma=True)
        ld_prolog = P.n(s_lc)
        for eng in ("tensor", "scalar", "vector"):
            P.wait(eng, s_lc, ld_prolog)
        P.emit("gpsimd", lambda e: e.dma_start(out=wdtb[:], in_=wdt_r), inc=s_wx, dma=True)
        P.wait("tensor", s_wx, P.n(s_wx))
        P.emit("scalar", lambda e: e.activation(aneg[:], alogb[:], AF.Exp, scale=1.0), inc=s_act)
        P.wait("vector", s_act, P.n(s_act))
        P.emit("vector", lambda e: e.tensor_scalar(out=aneg[:], in0=aneg[:], scalar1=-1.0,
                                                   scalar2=None, op0=ALU.mult), inc=s_ve)
        P.emit("vector", lambda e: e.memset(H_all[:], 0.0), inc=s_ve)
        P.emit("vector", lambda e: e.memset(xBCc[:, :, 0:3], 0.0), inc=s_ve)
        P.emit("vector", lambda e: e.memset(meanacc[:], 0.0), inc=s_ve)

        xsq_guard = [0]
        p1_guard = [0, 0]
        for cp in range(NCH):
            tc = cp * C
            if p1_guard[0]:
                P.wait("sync", s_act, p1_guard[0])
                P.wait("sync", s_ve, p1_guard[1])
            P.emit("sync", lambda e, tc=tc: e.dma_start(out=x_c[:], in_=x_r[:, :, tc:tc + C]),
                   inc=s_lx, dma=True)
            xrdy = P.n(s_lx)
            P.wait("scalar", s_lx, xrdy)
            P.wait("tensor", s_lx, xrdy)
            P.wait("vector", s_lx, xrdy)
            pool.use(gm)
            p1bufs = [xsqa, zsil]
            p1pend = []

            def emit_p1(k, a_sq):
                P.wait("tensor", s_act, a_sq)
                P.emit("tensor", lambda e, k=k: e.matmul(
                    gm[0:1, :], onesc[:], p1bufs[k % 2][:], start=(k == 0),
                    stop=(k == KT - 1), skip_group_check=True), inc=s_pe)
                p1cnt[k] = P.n(s_pe)
                xsq_guard[0] = P.n(s_pe)

            p1cnt = {}
            for k in range(KT):
                if k >= 2:
                    P.wait("scalar", s_pe, p1cnt[k - 2])
                elif xsq_guard[0]:
                    P.wait("scalar", s_pe, xsq_guard[0])
                P.emit("scalar", lambda e, k=k: e.activation(
                    p1bufs[k % 2][:], x_c[:, k, :], AF.Square), inc=s_act)
                p1pend.append((k, P.n(s_act)))
                if len(p1pend) > 1:
                    emit_p1(*p1pend.pop(0))
            while p1pend:
                emit_p1(*p1pend.pop(0))
            pe_ss = P.n(s_pe)
            P.wait("vector", s_pe, pe_ss)
            P.emit("vector", lambda e: e.tensor_scalar(
                out=srow[:, 0, :], in0=gm[0:1, :], scalar1=1.0 / D_MODEL, scalar2=RMS_EPS,
                op0=ALU.mult, op1=ALU.add), inc=s_ve)
            pool.free(gm, s_ve)
            v_ms = P.n(s_ve)
            P.wait("scalar", s_ve, v_ms)
            P.emit("scalar", lambda e: e.activation(srow[:, 1, :], srow[:, 0, :], AF.Ln), inc=s_act)
            chain("scalar")
            P.emit("scalar", lambda e: e.activation(srow[:, 1, :], srow[:, 1, :], AF.Exp,
                                                    scale=-0.5), inc=s_act)
            a_rs = P.n(s_act)
            pool.use(io)
            P.wait("tensor", s_act, a_rs)
            P.emit("tensor", lambda e: e.matmul(io[:], ones1c[:], srow[:, 1, :],
                                                start=True, stop=True), inc=s_pe)
            pe_bc = P.n(s_pe)
            P.wait("vector", s_pe, pe_bc)
            P.emit("vector", lambda e, tc=tc: e.tensor_copy(rstd_b[:, tc:tc + C], io[:]), inc=s_ve)
            pool.free(io, s_ve)
            chain("vector")
            for k in range(KT):
                P.emit("vector", lambda e, k=k, tc=tc: e.scalar_tensor_tensor(
                    out=xnt[:], in0=x_c[:, k, :], scalar=normw[:, k:k + 1],
                    in1=rstd_b[:, tc:tc + C], op0=ALU.mult, op1=ALU.mult), inc=s_ve)
                chain("vector")
                P.emit("vector", lambda e, k=k: e.tensor_reduce(
                    out=sp1[:, 0:1], in_=xnt[:], axis=mybir.AxisListType.X, op=ALU.add),
                    inc=s_ve)
                chain("vector")
                P.emit("vector", lambda e, k=k: e.tensor_tensor(
                    out=meanacc[:, k:k + 1], in0=meanacc[:, k:k + 1], in1=sp1[:, 0:1],
                    op=ALU.add), inc=s_ve)
                P.emit("vector", lambda e, k=k, tc=tc: e.tensor_copy(
                    xnb[:, k, tc:tc + C], xnt[:]), inc=s_ve)
                chain("vector")
            p1_guard = [P.n(s_act), P.n(s_ve)]
        P.emit("vector", lambda e: e.tensor_scalar(
            out=meanacc[:], in0=meanacc[:], scalar1=1.0 / SEQ, scalar2=None, op0=ALU.mult),
            inc=s_ve)
        v_mc = P.n(s_ve)
        pool.use(gm)
        P.wait("tensor", s_ve, v_mc)
        for k in range(KT):
            P.emit("tensor", lambda e, k=k: e.matmul(
                gm[0:32, 0:1], rw1t[:, k, :], meanacc[:, k:k + 1],
                start=(k == 0), stop=(k == KT - 1), skip_group_check=True),
                inc=(s_pe if k == KT - 1 else None))
        pe_h = P.n(s_pe)
        P.wait("vector", s_pe, pe_h)
        P.emit("vector", lambda e: e.tensor_tensor(out=hrt[:], in0=gm[0:32, 0:1], in1=rb1c[:],
                                                   op=ALU.add), inc=s_ve)
        pool.free(gm, s_ve)
        v_h = P.n(s_ve)
        P.wait("scalar", s_ve, v_h)
        P.emit("scalar", lambda e: e.activation(hrt[:], hrt[:], AF.Silu), inc=s_act)
        a_h = P.n(s_act)
        pool.use(gm)
        P.wait("tensor", s_act, a_h)
        P.emit("tensor", lambda e: e.matmul(gm[0:1, 0:3], hrt[:], rw2t[:],
                                            start=True, stop=True), inc=s_pe)
        pe_l3 = P.n(s_pe)
        P.wait("vector", s_pe, pe_l3)
        P.emit("vector", lambda e: e.tensor_tensor(out=prb[:, 0:3], in0=gm[0:1, 0:3],
                                                   in1=rb2r[:], op=ALU.add), inc=s_ve)
        pool.free(gm, s_ve)
        chain("vector")
        P.emit("vector", lambda e: e.tensor_reduce(out=prb[:, 3:4], in_=prb[:, 0:3],
                                                   axis=mybir.AxisListType.X, op=ALU.max), inc=s_ve)
        chain("vector")
        P.emit("vector", lambda e: e.tensor_scalar(out=prb[:, 3:4], in0=prb[:, 3:4],
                                                   scalar1=-1.0, scalar2=None, op0=ALU.mult),
               inc=s_ve)
        v_nm3 = P.n(s_ve)
        P.wait("scalar", s_ve, v_nm3)
        P.emit("scalar", lambda e: e.activation(prb[:, 0:3], prb[:, 0:3], AF.Exp,
                                                bias=prb[:, 3:4]), inc=s_act)
        a_e3 = P.n(s_act)
        P.wait("vector", s_act, a_e3)
        P.emit("vector", lambda e: e.tensor_reduce(out=prb[:, 4:5], in_=prb[:, 0:3],
                                                   axis=mybir.AxisListType.X, op=ALU.add), inc=s_ve)
        chain("vector")
        P.emit("vector", lambda e: e.reciprocal(prb[:, 4:5], prb[:, 4:5]), inc=s_ve)
        chain("vector")
        P.emit("vector", lambda e: e.tensor_scalar(
            out=prb[:, 0:3], in0=prb[:, 0:3], scalar1=prb[:, 4:5], scalar2=None,
            op0=ALU.mult), inc=s_ve)
        chain("vector")
        P.emit("vector", lambda e: e.tensor_tensor(out=prb[:, 5:6], in0=prb[:, 1:2],
                                                   in1=prb[:, 2:3], op=ALU.add), inc=s_ve)
        v_pm = P.n(s_ve)
        pool.use(gm)
        P.wait("tensor", s_ve, v_pm)
        P.emit("tensor", lambda e: e.matmul(gm[:, 0:1], ones1c[:], prb[:, 0:1],
                                            start=True, stop=True, skip_group_check=True))
        P.emit("tensor", lambda e: e.matmul(gm[:, 1:2], ones1c[:], prb[:, 5:6],
                                            start=True, stop=True, skip_group_check=True),
               inc=s_pe)
        pe_pp = P.n(s_pe)
        P.wait("vector", s_pe, pe_pp)
        P.emit("vector", lambda e: e.tensor_copy(pfc[:], gm[:, 0:1]), inc=s_ve)
        P.emit("vector", lambda e: e.tensor_copy(pmc[:], gm[:, 1:2]), inc=s_ve)
        pool.free(gm, s_ve)
        xnb_done = P.n(s_ve)
        P.wait("tensor", s_ve, xnb_done)

        wfree = []

        def stream_ws(src_r, fi, nk):
            idx = len(wfree)
            par = idx % 2
            if idx >= 2:
                P.wait("gpsimd", *wfree[idx - 2])
            P.emit("gpsimd", lambda e, src_r=src_r, fi=fi, par=par, nk=nk: e.dma_start(
                out=wsb[par][:, 0:nk, :], in_=src_r[:, fi]),
                inc=s_ws[par], dma=True)
            wfree.append(None)
            return idx, par, (s_ws[par], P.n(s_ws[par]))

        xc_guard = p1_guard[1]
        xc_guard_a = p1_guard[0]
        hg = [0]
        xh_guard = [0]
        xp_guard = [0]
        eq_guard = [0]
        tmb_guard = [0]
        sp_guard = [0]
        for c in range(NCH):
            tc = c * C
            if xc_guard:
                P.wait("sync", s_ve, xc_guard)
            if xc_guard_a:
                P.wait("sync", s_act, xc_guard_a)
                xc_guard_a = 0
            if c > 0:
                P.wait("sync", s_st, P.n(s_st))
            P.emit("sync", lambda e, tc=tc: e.dma_start(out=x_c[:], in_=x_r[:, :, tc:tc + C]),
                   inc=s_lx, dma=True)
            xrdy = P.n(s_lx)

            iop = [io, gm]
            a_convs = []
            zbufs = [zsil, xnt]
            for f in range(17):
                idx, par, (wsem, wv) = stream_ws(wxt_r, f, 8)
                P.wait("tensor", wsem, wv)
                PS = iop[f % 2]
                pool.use(PS)
                for k in range(KT):
                    P.emit("tensor", lambda e, k=k, par=par, tc=tc, PS=PS: e.matmul(
                        PS[:], wsb[par][:, k, :], xnb[:, k, tc:tc + C],
                        start=(k == 0), stop=(k == KT - 1), skip_group_check=True),
                        inc=(s_pe if k == KT - 1 else None))
                pe_f = P.n(s_pe)
                P.wait("vector", s_pe, pe_f)
                P.emit("vector", lambda e, f=f, PS=PS: e.tensor_copy(
                    xBCc[:, f, 3:3 + C], PS[:]), inc=s_ve)
                pool.free(PS, s_ve)
                wfree[idx] = (s_pe, pe_f)
                zb = zbufs[f % 2]
                if len(a_convs) >= 2:
                    P.wait("vector", s_act, a_convs[-2])
                chain("vector")
                P.emit("vector", lambda e, f=f, zb=zb: e.tensor_scalar(
                    out=zb[:], in0=xBCc[:, f, 0:C], scalar1=convw[:, f, 0:1], scalar2=None,
                    op0=ALU.mult), inc=s_ve)
                for kk in range(1, 4):
                    chain("vector")
                    P.emit("vector", lambda e, f=f, kk=kk, zb=zb: e.scalar_tensor_tensor(
                        out=zb[:], in0=xBCc[:, f, kk:kk + C], scalar=convw[:, f, kk:kk + 1],
                        in1=zb[:], op0=ALU.mult, op1=ALU.add), inc=s_ve)
                v_cv = P.n(s_ve)
                P.wait("scalar", s_ve, v_cv)
                P.emit("scalar", lambda e, f=f, zb=zb: e.activation(
                    xbc[:, f, :], zb[:], AF.Silu, bias=convb[:, f:f + 1]), inc=s_act)
                a_convs.append(P.n(s_act))
                if f == 0 and xh_guard[0]:
                    P.wait("scalar", s_pe, xh_guard[0])
                P.wait("scalar", s_act, P.n(s_act))
                for q in range(NQ):
                    if f < NP:
                        P.emit("scalar", lambda e, q=q, f=f: e.dma_start(
                            out=xh_tm[:, q, f * 128:(f + 1) * 128],
                            in_=xbc[:, f, q * 128:(q + 1) * 128], transpose=True),
                            inc=s_tp, dma=True)
                    else:
                        P.emit("scalar", lambda e, q=q: e.dma_start(
                            out=BCtm[:, q, :], in_=xbc[:, 16, q * 128:(q + 1) * 128],
                            transpose=True), inc=s_tp, dma=True)
            for q in range(NQ):
                PS = iop[q % 2]
                pool.use(PS)
                for k in range(KT):
                    P.emit("tensor", lambda e, k=k, q=q, tc=tc, PS=PS: e.matmul(
                        PS[0:128, 0:32], xnb[:, k, tc + q * 128:tc + (q + 1) * 128], wdtb[:, k, :],
                        start=(k == 0), stop=False, skip_group_check=True))
                P.emit("tensor", lambda e, PS=PS: e.matmul(
                    PS[0:128, 0:32], ones1c[:], dtbr[:], start=False, stop=True,
                    skip_group_check=True), inc=s_pe)
                pe_dt = P.n(s_pe)
                P.wait("scalar", s_pe, pe_dt)
                if sp_guard[0]:
                    P.wait("scalar", s_ve, sp_guard[0])
                P.emit("scalar", lambda e, PS=PS: e.activation(sp1[:], PS[0:128, 0:32], AF.Abs),
                       inc=s_act)
                P.emit("scalar", lambda e, PS=PS: e.activation(sp2[:], PS[0:128, 0:32], AF.Relu),
                       inc=s_act)
                pool.free(PS, s_act)
                chain("scalar")
                P.emit("scalar", lambda e: e.activation(sp1[:], sp1[:], AF.Exp, scale=-1.0),
                       inc=s_act)
                chain("scalar")
                P.emit("scalar", lambda e: e.activation(sp1[:], sp1[:], AF.Ln,
                                                        bias=onesc[:]), inc=s_act)
                a_sp = P.n(s_act)
                P.wait("vector", s_act, a_sp)
                P.emit("vector", lambda e, q=q: e.tensor_tensor(
                    out=dt_tm[:, q, :], in0=sp2[:], in1=sp1[:], op=ALU.add), inc=s_ve)
                sp_guard[0] = P.n(s_ve)
                chain("vector")

            a_conv = P.n(s_act)
            P.emit("vector", lambda e: e.tensor_copy(xBCc[:, :, 0:3], xBCc[:, :, C:C + 3]),
                   inc=s_ve)
            tail_done = P.n(s_ve)

            P.wait("scalar", s_act, a_conv)
            tp1 = P.n(s_tp)
            P.wait("scalar", s_tp, tp1)
            for q in range(NQ):
                P.emit("scalar", lambda e, q=q: e.dma_start(
                    out=Clo4[:, q, :], in_=xbc[64:128, 16, q * 128:(q + 1) * 128]),
                    inc=s_tp, dma=True)
            tp_all = P.n(s_tp)
            P.wait("scalar", s_tp, tp_all)

            for q in range(NQ):
                P.emit("vector", lambda e, q=q: e.tensor_tensor(
                    out=dA[:], in0=dt_tm[:, q, :], in1=aneg[:], op=ALU.mult), inc=s_ve)
                v_dA = P.n(s_ve)
                pool.use(gm)
                P.wait("tensor", s_ve, v_dA)
                P.emit("tensor", lambda e: e.matmul(gm[:, 0:32], triu[:], dA[:],
                                                    start=True, stop=True,
                                                    skip_group_check=True))
                P.emit("tensor", lambda e: e.matmul(gm[0:1, 32:64], onesc[:], dA[:],
                                                    start=True, stop=True,
                                                    skip_group_check=True), inc=s_pe)
                pe_cum = P.n(s_pe)
                P.wait("vector", s_pe, pe_cum)
                P.emit("vector", lambda e: e.tensor_copy(cum_tm[:], gm[:, 0:32]), inc=s_ve)
                P.emit("vector", lambda e: e.tensor_copy(totr[:], gm[0:1, 32:64]), inc=s_ve)
                pool.free(gm, s_ve)
                v_cum = P.n(s_ve)
                pool.use(gm)
                P.wait("tensor", s_ve, v_cum)
                P.emit("tensor", lambda e: e.matmul(gm[:, 0:32], ones1c[:], totr[:],
                                                    start=True, stop=True,
                                                    skip_group_check=True), inc=s_pe)
                pe_tb = P.n(s_pe)
                P.wait("vector", s_pe, pe_tb)
                P.emit("vector", lambda e: e.tensor_copy(tot_b[:], gm[:, 0:32]), inc=s_ve)
                pool.free(gm, s_ve)
                v_tb = P.n(s_ve)
                P.wait("scalar", s_ve, v_tb)
                P.emit("scalar", lambda e: e.activation(etot_b[:], tot_b[:], AF.Exp), inc=s_act)
                a_et = P.n(s_act)
                P.wait("vector", s_act, a_et)
                P.emit("vector", lambda e: e.tensor_copy(etot_p[0:64, :], etot_b[0:64, 0:32:2]),
                       inc=s_ve)
                P.emit("vector", lambda e: e.tensor_copy(etot_p[64:128, :], etot_b[64:128, 1:32:2]),
                       inc=s_ve)
                P.emit("vector", lambda e: e.tensor_tensor(out=w2a[:], in0=tot_b[:],
                                                           in1=cum_tm[:], op=ALU.subtract),
                       inc=s_ve)
                v_w2 = P.n(s_ve)
                P.wait("scalar", s_ve, v_w2)
                P.emit("scalar", lambda e: e.activation(w2a[:], w2a[:], AF.Exp), inc=s_act)
                a_w2 = P.n(s_act)
                P.wait("vector", s_act, a_w2)
                P.emit("vector", lambda e, q=q: e.tensor_tensor(
                    out=dtw2[:], in0=dt_tm[:, q, :], in1=w2a[:], op=ALU.mult), inc=s_ve)
                v_dtw2 = P.n(s_ve)

                pool.use(gm)
                P.wait("tensor", s_ve, v_cum)
                for Q in range(4):
                    P.emit("tensor", lambda e, Q=Q: e.transpose(
                        gm[0:8, Q * 128:(Q + 1) * 128], cum_tm[:, Q * 8:(Q + 1) * 8], identf[:]),
                        inc=s_pe)
                pe_tr = P.n(s_pe)
                P.wait("vector", s_pe, pe_tr)
                for Q in range(4):
                    P.emit("vector", lambda e, Q=Q: e.tensor_copy(
                        cumfmq[:, Q, :], gm[0:8, Q * 128:(Q + 1) * 128]), inc=s_ve)
                pool.free(gm, s_ve)
                v_cfm = P.n(s_ve)
                cpar = (c * NQ + q) % 2
                P.wait("sync", s_ve, v_cfm)
                if c * NQ + q >= 2:
                    P.wait("sync", s_cb[cpar], P.n(s_cb[cpar]))
                P.emit("sync", lambda e, cpar=cpar: e.dma_start(
                    out=cbounce[cpar].rearrange("(a b) t -> b a t", b=8), in_=cumfmq[:]),
                    inc=s_cb[cpar], dma=True)
                P.wait("sync", s_cb[cpar], P.n(s_cb[cpar]))
                P.emit("sync", lambda e, cpar=cpar: e.dma_start(
                    out=cumflat[:], in_=cbounce[cpar][0:16, :].flatten().unsqueeze(0)),
                    inc=s_cb[cpar], dma=True)
                cflat_rdy = (s_cb[cpar], P.n(s_cb[cpar]))

                pool.use(py)
                P.wait("tensor", s_tp, tp_all)
                P.emit("tensor", lambda e, q=q: e.matmul(
                    py[:], xbc[0:64, 16, q * 128:(q + 1) * 128], Clo4[:, q, :],
                    start=True, stop=True, skip_group_check=True), inc=s_pe)
                pe_g = P.n(s_pe)
                P.wait("vector", s_pe, pe_g)
                P.emit("vector", lambda e: e.tensor_copy(G_sb[:], py[:]), inc=s_ve)
                pool.free(py, s_ve)
                v_g = P.n(s_ve)

                pool.use(T1b)
                pool.use(T1m)
                P.wait("tensor", *cflat_rdy)
                a_eq = 0
                for Q in range(4):
                    if Q == 2:
                        pe_now = P.n(s_pe)
                        P.wait("sync", s_pe, pe_now)
                        P.emit("sync", lambda e, cpar=cpar: e.dma_start(
                            out=cumflat[:], in_=cbounce[cpar][16:32, :].flatten().unsqueeze(0)),
                            inc=s_cb[cpar], dma=True)
                        P.wait("tensor", s_cb[cpar], P.n(s_cb[cpar]))
                    Qo = Q % 2
                    if a_eq:
                        P.wait("tensor", s_act, a_eq)
                    for hhalf in range(2):
                        P.emit("tensor", lambda e, Qo=Qo, hh=hhalf: e.matmul(
                            T1b_h[hh][:], ones1c[:],
                            cumflat[:, Qo * 1024 + hh * 512: Qo * 1024 + (hh + 1) * 512],
                            start=True, stop=True, skip_group_check=True))
                        P.emit("tensor", lambda e, Qo=Qo, hh=hhalf: e.matmul(
                            T1m_h[hh][:], ones1c[:],
                            cumflat[:, Qo * 1024 + hh * 512: Qo * 1024 + (hh + 1) * 512],
                            start=True, stop=False, skip_group_check=True))
                        P.emit("tensor", lambda e, Q=Q, hh=hhalf: e.matmul(
                            T1m_h[hh][:], cumfmq[:, Q, :],
                            negblk[:, hh * 512:(hh + 1) * 512],
                            start=False, stop=False, skip_group_check=True))
                        P.emit("tensor", lambda e, hh=hhalf: e.matmul(
                            T1m_h[hh][:], identf[:], maskd[:],
                            start=False, stop=True, skip_group_check=True), inc=s_pe)
                    pe_t1 = P.n(s_pe)
                    P.wait("scalar", s_pe, pe_t1)
                    for pq in range(4):
                        pj = Q * 4 + pq
                        P.emit("scalar", lambda e, pq=pq, pj=pj: e.activation(
                            EbAll[0:64, pj, :], T1b[0:64, pq * 256:pq * 256 + 128], AF.Exp),
                            inc=s_act)
                        P.emit("scalar", lambda e, pq=pq, pj=pj: e.activation(
                            EbAll[64:128, pj, :], T1b[64:128, pq * 256 + 128:pq * 256 + 256],
                            AF.Exp), inc=s_act)
                    if eq_guard[0]:
                        P.wait("scalar", s_ve, eq_guard[0])
                    if tmb_guard[0]:
                        P.wait("scalar", s_st, tmb_guard[0])
                    P.emit("scalar", lambda e, Q=Q: e.activation(E_q[:], T1m[:], AF.Exp),
                           inc=s_act)
                    a_eq = P.n(s_act)
                    P.wait("vector", s_act, a_eq)
                    P.wait("vector", s_ve, v_g)
                    P.emit("vector", lambda e, Q=Q: e.tensor_tensor(
                        out=W_all[:, Q * 8:(Q + 1) * 8, :],
                        in0=E_q[:].rearrange("p (h t) -> p h t", t=128),
                        in1=G_sb[:].unsqueeze(1).broadcast_to([128, 8, 128]), op=ALU.mult),
                        inc=s_ve)
                    eq_guard[0] = P.n(s_ve)
                pool.free(T1b, s_act)
                pool.free(T1m, s_act)
                v_wall = P.n(s_ve)
                P.emit("vector", lambda e, q=q: e.tensor_tensor(
                    out=CbAll[0:64, :, :], in0=EbAll[0:64, :, :],
                    in1=Clo4[:, q, :].unsqueeze(1).broadcast_to([64, NP, 128]), op=ALU.mult),
                    inc=s_ve)
                P.emit("vector", lambda e, q=q: e.tensor_tensor(
                    out=CbAll[64:128, :, :], in0=EbAll[64:128, :, :],
                    in1=xbc[64:128, 16, q * 128:(q + 1) * 128].unsqueeze(1).broadcast_to(
                        [64, NP, 128]), op=ALU.mult), inc=s_ve)
                v_cb = P.n(s_ve)

                P.wait("vector", s_tp, tp_all)
                P.wait("vector", s_ve, max(v_wall, v_cb, v_dtw2))
                for r in range(2):
                    if xp_guard[0]:
                        P.wait("vector", s_pe, xp_guard[0])
                    P.wait("vector", s_ve, P.n(s_ve))
                    dtr = dt_tm[:, q, r * 16:(r + 1) * 16].unsqueeze(2).broadcast_to(
                        [128, 16, 64])
                    dwr = dtw2[:, r * 16:(r + 1) * 16].unsqueeze(2).broadcast_to(
                        [128, 16, 64])
                    xhv = xh_tm[:, q, r * 1024:(r + 1) * 1024].rearrange(
                        "p (h v) -> p h v", v=64)
                    P.emit("vector", lambda e, dtr=dtr, xhv=xhv: e.tensor_tensor(
                        out=XdtA[:].rearrange("p (h v) -> p h v", v=64), in0=xhv, in1=dtr,
                        op=ALU.mult), inc=s_ve)
                    P.emit("vector", lambda e, dwr=dwr, xhv=xhv: e.tensor_tensor(
                        out=XwA[:].rearrange("p (h v) -> p h v", v=64), in0=xhv, in1=dwr,
                        op=ALU.mult), inc=s_ve)
                    v_xp = P.n(s_ve)
                    pool.use(T1b)
                    P.wait("tensor", s_ve, v_xp)
                    if hg[0]:
                        P.wait("tensor", s_ve, hg[0])
                    for pj8 in range(8):
                        pj = r * 8 + pj8
                        he, hod = 2 * pj, 2 * pj + 1
                        yc = T1b[:, pj8 * 128:(pj8 + 1) * 128]
                        P.emit("tensor", lambda e, pj8=pj8, yc=yc, he=he: e.matmul(
                            yc[0:64, :], XdtA[:, pj8 * 128:pj8 * 128 + 64],
                            W_all[:, he, :],
                            start=True, stop=False, skip_group_check=True))
                        P.emit("tensor", lambda e, pj=pj, yc=yc: e.matmul(
                            yc[0:64, :], H_all[0:64, pj, :], CbAll[0:64, pj, :],
                            start=False, stop=True, skip_group_check=True))
                        P.emit("tensor", lambda e, pj8=pj8, yc=yc, hod=hod: e.matmul(
                            yc[64:128, :], XdtA[:, pj8 * 128 + 64:pj8 * 128 + 128],
                            W_all[:, hod, :],
                            start=True, stop=False, tile_position=(0, 64),
                            skip_group_check=True))
                        P.emit("tensor", lambda e, pj=pj, yc=yc: e.matmul(
                            yc[64:128, :], H_all[64:128, pj, :], CbAll[64:128, pj, :],
                            start=False, stop=True, tile_position=(64, 64),
                            skip_group_check=True))
                        if r == 0 and pj8 == 0:
                            pool.use(T1m)
                        P.emit("tensor", lambda e, pj=pj, q=q, pj8=pj8: e.matmul(
                            T1m[0:64, pj * 64:(pj + 1) * 64], BCtm[:, q, 0:64],
                            XwA[:, pj8 * 128:pj8 * 128 + 64],
                            start=True, stop=True, skip_group_check=True))
                        P.emit("tensor", lambda e, pj=pj, q=q, pj8=pj8: e.matmul(
                            T1m[64:128, pj * 64:(pj + 1) * 64], BCtm[:, q, 0:64],
                            XwA[:, pj8 * 128 + 64:pj8 * 128 + 128],
                            start=True, stop=True, tile_position=(0, 64),
                            skip_group_check=True), inc=(s_pe if pj8 == 7 else None))
                    pe_yu = P.n(s_pe)
                    xp_guard[0] = pe_yu
                    P.wait("vector", s_pe, pe_yu)
                    drr = drep[:, r * 8:(r + 1) * 8].unsqueeze(2).broadcast_to(
                        [128, 8, 128])
                    P.emit("vector", lambda e, r=r, q=q, drr=drr: e.tensor_tensor(
                        out=XdtA[:].rearrange("p (h v) -> p h v", v=128),
                        in0=xbc[:, r * 8:(r + 1) * 8, q * 128:(q + 1) * 128], in1=drr,
                        op=ALU.mult), inc=s_ve)
                    P.wait("vector", s_ve, P.n(s_ve))
                    P.emit("vector", lambda e, r=r, q=q: e.tensor_tensor(
                        out=ycb[:, r * 8:(r + 1) * 8, q * 128:(q + 1) * 128],
                        in0=XdtA[:].rearrange("p (h v) -> p h v", v=128), in1=T1b[:, :].rearrange(
                            "p (h v) -> p h v", v=128), op=ALU.add), inc=s_ve)
                    pool.free(T1b, s_ve)
                P.wait("vector", s_pe, P.n(s_pe))
                etr = etot_p[:].unsqueeze(2).broadcast_to([128, 16, 64])
                P.wait("vector", s_ve, P.n(s_ve))
                P.emit("vector", lambda e, etr=etr: e.tensor_tensor(
                    out=H_all[:], in0=H_all[:], in1=etr, op=ALU.mult), inc=s_ve)
                P.wait("vector", s_ve, P.n(s_ve))
                P.emit("vector", lambda e: e.tensor_tensor(
                    out=H_all[:], in0=H_all[:], in1=T1m[:, :].rearrange(
                        "p (h v) -> p h v", v=64), op=ALU.add), inc=s_ve)
                pool.free(T1m, s_ve)
                hg[0] = P.n(s_ve)
            xh_guard[0] = P.n(s_pe)

            pool.use(T1b)
            gsum_pend = []
            xsqa2 = [xsqa, xnt]
            gcnt = {}

            def emit_gsum(j, a_sq):
                P.wait("tensor", s_act, a_sq)
                P.emit("tensor", lambda e, j=j: e.matmul(
                    T1b[0:1, 0:512], onesc[:], xsqa2[j % 2][:], start=(j == 0),
                    stop=(j == NP - 1), skip_group_check=True), inc=s_pe)
                gcnt[j] = P.n(s_pe)

            for j in range(NP):
                idx, par, (wsem, wv) = stream_ws(wzt_r, j, 8)
                P.wait("tensor", wsem, wv)
                PS = iop[j % 2]
                pool.use(PS)
                for k in range(KT):
                    P.emit("tensor", lambda e, k=k, par=par, tc=tc, PS=PS: e.matmul(
                        PS[:], wsb[par][:, k, :], xnb[:, k, tc:tc + C],
                        start=(k == 0), stop=(k == KT - 1), skip_group_check=True),
                        inc=(s_pe if k == KT - 1 else None))
                pe_z = P.n(s_pe)
                if gsum_pend:
                    emit_gsum(*gsum_pend.pop(0))
                P.wait("scalar", s_pe, pe_z)
                P.emit("scalar", lambda e, PS=PS: e.activation(zsil[:], PS[:], AF.Silu),
                       inc=s_act)
                pool.free(PS, s_act)
                a_z = P.n(s_act)
                wfree[idx] = (s_pe, pe_z)
                P.wait("vector", s_act, a_z)
                P.emit("vector", lambda e, j=j: e.tensor_tensor(
                    out=ycb[:, j, :], in0=ycb[:, j, :], in1=zsil[:], op=ALU.mult), inc=s_ve)
                v_yg = P.n(s_ve)
                P.wait("scalar", s_ve, v_yg)
                if j >= 2:
                    P.wait("scalar", s_pe, gcnt[j - 2])
                P.emit("scalar", lambda e, j=j: e.activation(
                    xsqa2[j % 2][:], ycb[:, j, :], AF.Square), inc=s_act)
                gsum_pend.append((j, P.n(s_act)))
            while gsum_pend:
                emit_gsum(*gsum_pend.pop(0))
            pe_gs = P.n(s_pe)
            P.wait("vector", s_pe, pe_gs)
            P.emit("vector", lambda e: e.tensor_scalar(
                out=srow[:, 0, :], in0=T1b[0:1, 0:512], scalar1=1.0 / DI, scalar2=1e-5,
                op0=ALU.mult, op1=ALU.add), inc=s_ve)
            pool.free(T1b, s_ve)
            v_gms = P.n(s_ve)
            P.wait("scalar", s_ve, v_gms)
            P.emit("scalar", lambda e: e.activation(srow[:, 1, :], srow[:, 0, :], AF.Ln),
                   inc=s_act)
            chain("scalar")
            P.emit("scalar", lambda e: e.activation(srow[:, 1, :], srow[:, 1, :], AF.Exp,
                                                    scale=-0.5), inc=s_act)
            a_gr = P.n(s_act)
            pool.use(io)
            P.wait("tensor", s_act, a_gr)
            P.emit("tensor", lambda e: e.matmul(io[:], ones1c[:], srow[:, 1, :],
                                                start=True, stop=True, skip_group_check=True),
                   inc=s_pe)
            pe_gb = P.n(s_pe)
            P.wait("vector", s_pe, pe_gb)
            P.emit("vector", lambda e: e.tensor_copy(zsil[:], io[:]), inc=s_ve)
            pool.free(io, s_ve)
            chain("vector")
            for j in range(NP):
                P.emit("vector", lambda e, j=j: e.scalar_tensor_tensor(
                    out=ycb[:, j, :], in0=ycb[:, j, :], scalar=gnw[:, j:j + 1], in1=zsil[:],
                    op0=ALU.mult, op1=ALU.mult), inc=s_ve)
            ygn_done = P.n(s_ve)
            P.wait("tensor", s_ve, ygn_done)

            P.wait("vector", s_lx, xrdy)
            for j3 in range(KT):
                idx, par, (wsem, wv) = stream_ws(wot_r, j3, 16)
                P.wait("tensor", wsem, wv)
                PS = iop[j3 % 2]
                pool.use(PS)
                for k in range(16):
                    P.emit("tensor", lambda e, k=k, par=par, PS=PS: e.matmul(
                        PS[:], wsb[par][:, k, :], ycb[:, k, :],
                        start=(k == 0), stop=(k == 15), skip_group_check=True),
                        inc=(s_pe if k == 15 else None))
                pe_o = P.n(s_pe)
                wfree[idx] = (s_pe, pe_o)
                P.wait("vector", s_pe, pe_o)
                chain("vector")
                P.emit("vector", lambda e, j3=j3, tc=tc: e.scalar_tensor_tensor(
                    out=xnt[:], in0=x_c[:, j3, :], scalar=normw[:, j3:j3 + 1],
                    in1=rstd_b[:, tc:tc + C], op0=ALU.mult, op1=ALU.mult), inc=s_ve)
                chain("vector")
                P.emit("vector", lambda e, j3=j3: e.scalar_tensor_tensor(
                    out=xnt[:], in0=xnt[:], scalar=pfc[:], in1=x_c[:, j3, :],
                    op0=ALU.mult, op1=ALU.add), inc=s_ve)
                chain("vector")
                P.emit("vector", lambda e, j3=j3: e.tensor_tensor(
                    out=xnt[:], in0=xnt[:], in1=x_c[:, j3, :], op=ALU.add), inc=s_ve)
                chain("vector")
                P.emit("vector", lambda e, j3=j3, PS=PS: e.scalar_tensor_tensor(
                    out=x1c[:, j3, :], in0=PS[:], scalar=pmc[:], in1=xnt[:],
                    op0=ALU.mult, op1=ALU.add), inc=s_ve)
                pool.free(PS, s_ve)
            x1_done = P.n(s_ve)

            P.wait("scalar", s_pe, pe_gs)
            P.wait("scalar", s_ve, x1_done)
            pool.use(T1b)
            pool.use(T1m)
            P.wait("tensor", s_ve, x1_done)
            plbufs = [xsqa, xnt]
            plpend = []
            plcnt = {}

            def emit_plm(j, a_sq):
                P.wait("tensor", s_act, a_sq)
                P.emit("tensor", lambda e, j=j: e.matmul(
                    T1m[0:1, 0:512], onesc[:], plbufs[j % 2][:], start=(j == 0),
                    stop=(j == KT - 1), skip_group_check=True), inc=s_pe)
                plcnt[j] = P.n(s_pe)
                xsq_guard[0] = P.n(s_pe)

            for j in range(KT):
                P.emit("tensor", lambda e, j=j: e.matmul(
                    T1b[0:1, 0:512], onesc[:], x1c[:, j, :], start=(j == 0), stop=(j == KT - 1),
                    skip_group_check=True), inc=(s_pe if j == KT - 1 else None))
                if j >= 2:
                    P.wait("scalar", s_pe, plcnt[j - 2])
                elif xsq_guard[0]:
                    P.wait("scalar", s_pe, xsq_guard[0])
                P.emit("scalar", lambda e, j=j: e.activation(
                    plbufs[j % 2][:], x1c[:, j, :], AF.Square), inc=s_act)
                plpend.append((j, P.n(s_act)))
                if len(plpend) > 1:
                    emit_plm(*plpend.pop(0))
            while plpend:
                emit_plm(*plpend.pop(0))
            pe_st2 = P.n(s_pe)
            P.wait("vector", s_pe, pe_st2)
            P.emit("vector", lambda e: e.tensor_scalar(
                out=srow[:, 0, :], in0=T1b[0:1, 0:512], scalar1=1.0 / D_MODEL, scalar2=None,
                op0=ALU.mult), inc=s_ve)
            v_mean = P.n(s_ve)
            chain("vector")
            P.emit("vector", lambda e: e.tensor_tensor(
                out=srow[:, 1, :], in0=srow[:, 0, :], in1=srow[:, 0, :], op=ALU.mult), inc=s_ve)
            chain("vector")
            P.emit("vector", lambda e: e.scalar_tensor_tensor(
                out=srow[:, 1, :], in0=T1m[0:1, 0:512], scalar=1.0 / D_MODEL, in1=srow[:, 1, :],
                op0=ALU.mult, op1=ALU.subtract), inc=s_ve)
            pool.free(T1b, s_ve)
            pool.free(T1m, s_ve)
            chain("vector")
            P.emit("vector", lambda e: e.tensor_scalar(
                out=srow[:, 1, :], in0=srow[:, 1, :], scalar1=1e-5, scalar2=None, op0=ALU.add),
                inc=s_ve)
            v_var = P.n(s_ve)
            P.wait("scalar", s_ve, v_var)
            P.emit("scalar", lambda e: e.activation(srow[:, 1, :], srow[:, 1, :], AF.Ln),
                   inc=s_act)
            chain("scalar")
            P.emit("scalar", lambda e: e.activation(srow[:, 1, :], srow[:, 1, :], AF.Exp,
                                                    scale=-0.5), inc=s_act)
            act_r = P.n(s_act)
            pool.use(io)
            P.wait("tensor", s_ve, v_mean)
            P.emit("tensor", lambda e: e.matmul(io[:, 0:512], ones1c[:], srow[:, 0, :],
                                                start=True, stop=True, skip_group_check=True))
            pool.use(gm)
            P.wait("tensor", s_act, act_r)
            P.emit("tensor", lambda e: e.matmul(gm[:, 0:512], ones1c[:], srow[:, 1, :],
                                                start=True, stop=True, skip_group_check=True),
                   inc=s_pe)
            pe_bc2 = P.n(s_pe)
            P.wait("vector", s_pe, pe_bc2)
            for j in range(KT):
                P.emit("vector", lambda e, j=j: e.tensor_tensor(
                    out=x1c[:, j, :], in0=x1c[:, j, :], in1=io[:, 0:512], op=ALU.subtract),
                    inc=s_ve)
                chain("vector")
                P.emit("vector", lambda e, j=j: e.tensor_tensor(
                    out=x1c[:, j, :], in0=x1c[:, j, :], in1=gm[:, 0:512], op=ALU.mult), inc=s_ve)
            pool.free(io, s_ve)
            pool.free(gm, s_ve)
            v_ln = P.n(s_ve)
            P.wait("sync", s_ve, v_ln)
            P.emit("sync", lambda e, tc=tc: e.dma_start(
                out=out_r[:, :, tc:tc + C], in_=x1c[:]), inc=s_st, dma=True)

            P.wait("tensor", s_ve, v_ln)
            for q in range(NQ):
                pool.use(gm)
                for k in range(KT):
                    P.emit("tensor", lambda e, k=k, q=q: e.matmul(
                        gm[:, 0:8], x1c[:, k, q * 128:(q + 1) * 128], rwt[:, k, :],
                        start=(k == 0), stop=False, skip_group_check=True))
                P.emit("tensor", lambda e: e.matmul(
                    gm[:, 0:8], ones1c[:], rbr[:], start=False, stop=True,
                    skip_group_check=True), inc=s_pe)
                pe_lg = P.n(s_pe)
                P.wait("vector", s_pe, pe_lg)
                P.emit("vector", lambda e, q=q: e.tensor_copy(lgb[:, q, :], gm[:, 0:8]),
                       inc=s_ve)
                pool.free(gm, s_ve)
                if tmb_guard[0]:
                    P.wait("vector", s_st, tmb_guard[0])
                for j in range(KT):
                    pool.use(io)
                    P.emit("tensor", lambda e, j=j, q=q: e.transpose(
                        io[:, 0:128], x1c[:, j, q * 128:(q + 1) * 128], identf[:]), inc=s_pe)
                    pe_tt = P.n(s_pe)
                    P.wait("vector", s_pe, pe_tt)
                    P.emit("vector", lambda e, j=j: e.tensor_copy(
                        E_q[:].rearrange("p (k f) -> p k f", f=128)[:, j, :], io[:, 0:128]),
                        inc=s_ve)
                    pool.free(io, s_ve)
                v_tm_q = P.n(s_ve)
                P.wait("sync", s_ve, v_tm_q)
                P.emit("sync", lambda e, tc=tc, q=q: e.dma_start(
                    out=xtm_d[tc + q * 128: tc + (q + 1) * 128, :].rearrange(
                        "p (k f) -> p k f", f=128),
                    in_=E_q[:].rearrange("p (k f) -> p k f", f=128)), inc=s_st, dma=True)
                tmb_guard[0] = P.n(s_st)
            v_lg_all = P.n(s_ve)
            P.wait("sync", s_ve, v_lg_all)
            P.emit("sync", lambda e, tc=tc: e.dma_start(
                out=lg_d[tc:tc + C, :].rearrange("(q p) f -> p q f", p=128), in_=lgb[:]),
                inc=s_st, dma=True)
            xc_guard = P.n(s_ve)

        P.wait("sync", s_st, P.n(s_st))



def build_mamba_nc():
    nc = bass.Bass("TRN2", target_bir_lowering=False)
    DI, CD, NP = 2048, 2176, 16
    d = {}
    d["x"] = nc.dram_tensor("x", [D_MODEL, SEQ], F32, kind="ExternalInput").ap()
    d["wzt"] = nc.dram_tensor("wzt", [128, 16, 8, 128], F32, kind="ExternalInput").ap()
    d["wxt"] = nc.dram_tensor("wxt", [128, 17, 8, 128], F32, kind="ExternalInput").ap()
    d["wdt"] = nc.dram_tensor("wdt", [1024, 32], F32, kind="ExternalInput").ap()
    d["wot_m"] = nc.dram_tensor("wot_m", [128, 8, 16, 128], F32, kind="ExternalInput").ap()
    d["normw"] = nc.dram_tensor("normw", [128, KT], F32, kind="ExternalInput").ap()
    d["gnw"] = nc.dram_tensor("gnw", [128, NP], F32, kind="ExternalInput").ap()
    d["convw"] = nc.dram_tensor("convw", [128, 17, 4], F32, kind="ExternalInput").ap()
    d["convb"] = nc.dram_tensor("convb", [128, 17], F32, kind="ExternalInput").ap()
    d["drep"] = nc.dram_tensor("drep", [128, NP], F32, kind="ExternalInput").ap()
    d["dtb"] = nc.dram_tensor("dtb", [1, 32], F32, kind="ExternalInput").ap()
    d["alog"] = nc.dram_tensor("alog", [128, 32], F32, kind="ExternalInput").ap()
    d["rw1t"] = nc.dram_tensor("rw1t", [128, 8, 32], F32, kind="ExternalInput").ap()
    d["rb1"] = nc.dram_tensor("rb1", [32, 1], F32, kind="ExternalInput").ap()
    d["rw2t"] = nc.dram_tensor("rw2t", [32, 3], F32, kind="ExternalInput").ap()
    d["rb2"] = nc.dram_tensor("rb2", [1, 3], F32, kind="ExternalInput").ap()
    d["rwt"] = nc.dram_tensor("rwt", [128, 8, 8], F32, kind="ExternalInput").ap()
    d["rbr"] = nc.dram_tensor("rbr", [1, 8], F32, kind="ExternalInput").ap()
    d["ones"] = nc.dram_tensor("ones", [128, 1], F32, kind="ExternalInput").ap()
    d["ones1"] = nc.dram_tensor("ones1", [1, 128], F32, kind="ExternalInput").ap()
    d["identf"] = nc.dram_tensor("identf", [128, 128], F32, kind="ExternalInput").ap()
    d["identb"] = nc.dram_tensor("identb", [128, 128], F32, kind="ExternalInput").ap()
    d["triu"] = nc.dram_tensor("triu", [128, 128], F32, kind="ExternalInput").ap()
    d["negblk"] = nc.dram_tensor("negblk", [8, 1024], F32, kind="ExternalInput").ap()
    d["maskd"] = nc.dram_tensor("maskd", [128, 512], F32, kind="ExternalInput").ap()
    d["out"] = nc.dram_tensor("out", [D_MODEL, SEQ], F32, kind="ExternalOutput").ap()
    d["xtm"] = nc.dram_tensor("xtm", [SEQ, 1024], BF16, kind="ExternalOutput").ap()
    d["logits"] = nc.dram_tensor("logits", [SEQ, 8], F32, kind="ExternalOutput").ap()
    d["cbounce"] = [nc.dram_tensor(f"cbounce{i}", [32, 128], F32).ap() for i in range(2)]
    with ExitStack() as ctx:
        block = ctx.enter_context(nc.Block())
        P = Prog(ctx, nc)
        pool = PsumPool(P)
        emit_mamba(nc, P, pool, d)
        P.build_block(block)
    return nc

def host_mamba_inputs(x_fm_core, w, router_w, router_b):
    i = np.arange(128)
    triu = (i[:, None] <= i[None, :]).astype(np.float32)
    maskd = np.tile((i[:, None] > i[None, :]).astype(np.float32) * -1e30, (1, 4))
    negblk = np.zeros((8, 1024), np.float32)
    for k in range(8):
        negblk[k, k * 128:(k + 1) * 128] = -1.0
    ip = np.asarray(w["in_proj_w"], np.float32)
    f32 = np.float32
    return dict(
        x=np.ascontiguousarray(x_fm_core, dtype=f32),
        wzt=np.ascontiguousarray(_t(ip[0:2048]).reshape(8, 128, 16, 128).transpose(1, 2, 0, 3)),
        wxt=np.ascontiguousarray(_t(ip[2048:4224]).reshape(8, 128, 17, 128).transpose(1, 2, 0, 3)),
        wdt=_t(ip[4224:4256]),
        wot_m=np.ascontiguousarray(_t(w["out_proj_w"]).reshape(16, 128, 8, 128).transpose(1, 2, 0, 3)),
        normw=np.ascontiguousarray(np.asarray(w["norm_w"], f32).reshape(8, 128).T),
        gnw=np.ascontiguousarray(np.asarray(w["gnorm_w"], f32).reshape(16, 128).T),
        convw=np.ascontiguousarray(np.asarray(w["conv_w"], f32).reshape(17, 128, 4).transpose(1, 0, 2)),
        convb=np.ascontiguousarray(np.asarray(w["conv_b"], f32).reshape(17, 128).T),
        drep=np.ascontiguousarray(np.repeat(np.asarray(w["D_param"], f32), 64).reshape(16, 128).T),
        dtb=np.asarray(w["dt_bias"], f32).reshape(1, 32).copy(),
        alog=np.ascontiguousarray(np.broadcast_to(np.asarray(w["A_log"], f32), (128, 32))),
        rw1t=np.ascontiguousarray(_t(w["r_w1"]).reshape(8, 128, 32).transpose(1, 0, 2)),
        rb1=np.asarray(w["r_b1"], f32).reshape(32, 1).copy(),
        rw2t=_t(w["r_w2"]), rb2=np.asarray(w["r_b2"], f32).reshape(1, 3).copy(),
        rwt=np.ascontiguousarray(_t(router_w).reshape(8, 128, 8).transpose(1, 0, 2)),
        rbr=np.asarray(router_b, f32).reshape(1, 8).copy(),
        ones=np.ones((128, 1), f32), ones1=np.ones((1, 128), f32),
        identf=np.eye(128, dtype=f32), identb=np.eye(128, dtype=f32),
        triu=np.ascontiguousarray(triu), negblk=negblk,
        maskd=np.ascontiguousarray(maskd),
    )
def emit_routing(P, pool, nc, names, NTILES, CAP=2048.0):
    (lgt, m1, m2c, eqa, eqb, mka, gka, p1c, p2c, offs, totev, prt, prt2,
     triu_s, ones1c, onesc, s_pe, s_act, s_ve, out_cb) = names
    ALUx = ALU
    for i in range(NTILES):
        P.wait("vector", s_ve, P.n(s_ve))
        P.emit("vector", lambda e, i=i: e.tensor_reduce(
            out=m1[:], in_=lgt[:, i, :], axis=mybir.AxisListType.X, op=ALUx.max), inc=s_ve)
        P.wait("vector", s_ve, P.n(s_ve))
        P.emit("vector", lambda e, i=i: e.tensor_tensor(
            out=eqa[:], in0=lgt[:, i, :], in1=m1[:].broadcast_to([128, 8]),
            op=ALUx.is_ge), inc=s_ve)
        P.wait("vector", s_ve, P.n(s_ve))
        P.emit("vector", lambda e, i=i: e.scalar_tensor_tensor(
            out=mka[:], in0=eqa[:], scalar=-1e30, in1=lgt[:, i, :],
            op0=ALUx.mult, op1=ALUx.add), inc=s_ve)
        P.wait("vector", s_ve, P.n(s_ve))
        P.emit("vector", lambda e: e.tensor_reduce(
            out=m2c[:], in_=mka[:], axis=mybir.AxisListType.X, op=ALUx.max), inc=s_ve)
        P.wait("vector", s_ve, P.n(s_ve))
        P.emit("vector", lambda e: e.tensor_tensor(
            out=eqb[:], in0=mka[:], in1=m2c[:].broadcast_to([128, 8]), op=ALUx.is_ge),
            inc=s_ve)
        P.emit("vector", lambda e: e.tensor_tensor(
            out=p1c[:], in0=m1[:], in1=m2c[:], op=ALUx.subtract), inc=s_ve)
        v_p = P.n(s_ve)
        P.wait("scalar", s_ve, v_p)
        P.emit("scalar", lambda e: e.activation(p1c[:], p1c[:], AF.Sigmoid), inc=s_act)
        a_p = P.n(s_act)
        P.wait("vector", s_act, a_p)
        P.emit("vector", lambda e: e.tensor_scalar(
            out=p2c[:], in0=p1c[:], scalar1=-1.0, scalar2=1.0, op0=ALUx.mult, op1=ALUx.add),
            inc=s_ve)
        P.wait("vector", s_ve, P.n(s_ve))
        P.emit("vector", lambda e: e.tensor_scalar(
            out=gka[:], in0=eqa[:], scalar1=p1c[:], scalar2=None, op0=ALUx.mult), inc=s_ve)
        P.wait("vector", s_ve, P.n(s_ve))
        P.emit("vector", lambda e: e.scalar_tensor_tensor(
            out=gka[:], in0=eqb[:], scalar=p2c[:], in1=gka[:], op0=ALUx.mult, op1=ALUx.add),
            inc=s_ve)
        P.emit("vector", lambda e: e.tensor_tensor(
            out=mka[:], in0=eqa[:], in1=eqb[:], op=ALUx.add), inc=s_ve)
        v_m = P.n(s_ve)
        pool.use(prt)
        P.wait("tensor", s_ve, v_m)
        P.emit("tensor", lambda e: e.matmul(prt[:, 0:8], triu_s[:], mka[:],
                                            start=True, stop=False, skip_group_check=True))
        P.emit("tensor", lambda e: e.matmul(prt[:, 0:8], ones1c[:], offs[:],
                                            start=False, stop=True, skip_group_check=True))
        pool.use(prt2)
        P.emit("tensor", lambda e: e.matmul(prt2[0:1, 0:8], onesc[:], mka[:],
                                            start=True, stop=True, skip_group_check=True),
               inc=s_pe)
        pe_c = P.n(s_pe)
        P.wait("vector", s_pe, pe_c)
        P.emit("vector", lambda e: e.tensor_tensor(
            out=offs[:], in0=offs[:], in1=prt2[0:1, 0:8], op=ALUx.add), inc=s_ve)
        pool.free(prt2, s_ve)
        P.wait("vector", s_ve, P.n(s_ve))
        P.emit("vector", lambda e, CAP=CAP: e.tensor_scalar(
            out=eqb[:], in0=prt[:, 0:8], scalar1=CAP, scalar2=None, op0=ALUx.is_lt), inc=s_ve)
        P.wait("vector", s_ve, P.n(s_ve))
        P.emit("vector", lambda e: e.tensor_tensor(
            out=mka[:], in0=mka[:], in1=eqb[:], op=ALUx.mult), inc=s_ve)
        P.wait("vector", s_ve, P.n(s_ve))
        P.emit("vector", lambda e: e.tensor_tensor(
            out=gka[:], in0=gka[:], in1=mka[:], op=ALUx.mult), inc=s_ve)
        out_cb(i)


def emit_routing16(P, pool, nc, n, NTILES):
    (lgt, m1, eqa, eqb, mka, gka, p1c, pfxr, offs, prt, prt2,
     triu_s, ones1c, onesc, s_pe, s_act, s_ve, out_cb) = n
    B3 = [128, 16, 8]
    for g in range(NTILES // 16):
        L = lgt[:, g * 16:(g + 1) * 16, :]
        P.wait("vector", s_ve, P.n(s_ve))
        P.emit("vector", lambda e, L=L: e.tensor_reduce(
            out=m1[:, :, 0], in_=L, axis=mybir.AxisListType.X, op=ALU.max), inc=s_ve)
        P.wait("vector", s_ve, P.n(s_ve))
        P.emit("vector", lambda e, L=L: e.tensor_tensor(
            out=eqa[:], in0=L, in1=m1[:, :, 0:1].broadcast_to(B3), op=ALU.is_ge), inc=s_ve)
        P.wait("vector", s_ve, P.n(s_ve))
        P.emit("vector", lambda e, L=L: e.scalar_tensor_tensor(
            out=mka[:], in0=eqa[:], scalar=-1e30, in1=L, op0=ALU.mult, op1=ALU.add),
            inc=s_ve)
        P.wait("vector", s_ve, P.n(s_ve))
        P.emit("vector", lambda e: e.tensor_reduce(
            out=m1[:, :, 1], in_=mka[:], axis=mybir.AxisListType.X, op=ALU.max), inc=s_ve)
        P.wait("vector", s_ve, P.n(s_ve))
        P.emit("vector", lambda e: e.tensor_tensor(
            out=eqb[:], in0=mka[:], in1=m1[:, :, 1:2].broadcast_to(B3), op=ALU.is_ge),
            inc=s_ve)
        P.emit("vector", lambda e: e.tensor_tensor(
            out=p1c[:], in0=m1[:, :, 0], in1=m1[:, :, 1], op=ALU.subtract), inc=s_ve)
        v_p = P.n(s_ve)
        P.wait("scalar", s_ve, v_p)
        P.emit("scalar", lambda e: e.activation(p1c[:], p1c[:], AF.Sigmoid), inc=s_act)
        a_p = P.n(s_act)
        P.wait("vector", s_act, a_p)
        P.emit("vector", lambda e: e.tensor_tensor(
            out=mka[:], in0=eqa[:], in1=eqb[:], op=ALU.add), inc=s_ve)
        P.emit("vector", lambda e: e.tensor_tensor(
            out=gka[:], in0=eqa[:], in1=p1c[:].unsqueeze(2).broadcast_to(B3),
            op=ALU.mult), inc=s_ve)
        P.wait("vector", s_ve, P.n(s_ve))
        P.emit("vector", lambda e: e.tensor_scalar(
            out=p1c[:], in0=p1c[:], scalar1=-1.0, scalar2=1.0, op0=ALU.mult, op1=ALU.add),
            inc=s_ve)
        P.wait("vector", s_ve, P.n(s_ve))
        P.emit("vector", lambda e: e.tensor_tensor(
            out=eqb[:], in0=eqb[:], in1=p1c[:].unsqueeze(2).broadcast_to(B3),
            op=ALU.mult), inc=s_ve)
        P.wait("vector", s_ve, P.n(s_ve))
        P.emit("vector", lambda e: e.tensor_tensor(
            out=gka[:], in0=gka[:], in1=eqb[:], op=ALU.add), inc=s_ve)
        v_m = P.n(s_ve)
        pool.use(prt)
        pool.use(prt2)
        P.wait("tensor", s_ve, v_m)
        P.emit("tensor", lambda e: e.matmul(
            prt2[0:1, 0:128], onesc[:], mka[:].rearrange("p a b -> p (a b)"),
            start=True, stop=True, skip_group_check=True), inc=s_pe)
        pe_tot = P.n(s_pe)
        P.wait("vector", s_pe, pe_tot)
        P.emit("vector", lambda e: e.tensor_copy(pfxr[:, 0, :], offs[:]), inc=s_ve)
        for t in range(1, 16):
            P.wait("vector", s_ve, P.n(s_ve))
            P.emit("vector", lambda e, t=t: e.scalar_tensor_tensor(
                out=pfxr[:, t, :], in0=prt2[0:1, (t - 1) * 8:t * 8], scalar=1.0,
                in1=pfxr[:, t - 1, :], op0=ALU.mult, op1=ALU.add), inc=s_ve)
        P.wait("vector", s_ve, P.n(s_ve))
        P.emit("vector", lambda e: e.scalar_tensor_tensor(
            out=offs[:], in0=prt2[0:1, 15 * 8:16 * 8], scalar=1.0, in1=pfxr[:, 15, :],
            op0=ALU.mult, op1=ALU.add), inc=s_ve)
        v_pfx = P.n(s_ve)
        pool.free(prt2, s_ve)
        P.wait("tensor", s_ve, v_pfx)
        for t in range(16):
            P.emit("tensor", lambda e, t=t: e.matmul(
                prt[:, t * 8:(t + 1) * 8], triu_s[:], mka[:, t, :],
                start=True, stop=False, skip_group_check=True))
            P.emit("tensor", lambda e, t=t: e.matmul(
                prt[:, t * 8:(t + 1) * 8], ones1c[:], pfxr[:, t, :],
                start=False, stop=True, skip_group_check=True),
                inc=(s_pe if t == 15 else None))
        pe_c = P.n(s_pe)
        P.wait("vector", s_pe, pe_c)
        P.emit("vector", lambda e: e.tensor_scalar(
            out=eqb[:], in0=prt[:, 0:128].rearrange("p (a b) -> p a b", b=8),
            scalar1=2048.0, scalar2=None, op0=ALU.is_lt), inc=s_ve)
        P.wait("vector", s_ve, P.n(s_ve))
        P.emit("vector", lambda e: e.tensor_tensor(
            out=mka[:], in0=mka[:], in1=eqb[:], op=ALU.mult), inc=s_ve)
        P.wait("vector", s_ve, P.n(s_ve))
        P.emit("vector", lambda e: e.tensor_tensor(
            out=gka[:], in0=gka[:], in1=mka[:], op=ALU.mult), inc=s_ve)
        out_cb(g)



def emit_moe(nc, P, pool, d, pfx="b"):
    NTOK = BATCH * SEQ
    NTILES = NTOK // 128
    lg_d = d["lg"]
    xf_d = d["xf"]
    we_d = d["we"]
    ve_d = d["vex"]
    wo_d = d["wo"]
    ehot_d = d["ehot"]
    slotinit_d = d["slotinit"]
    triu_d = d["trius"]
    ones_d = d["ones"]
    ones1_d = d["ones1"]
    identf_d = d["identf"]
    ye_d = d["ye"]
    slotmap = d["slotmap"]

    we_r = we_d
    ve_r = ve_d
    wo_r = wo_d.rearrange("(k p) f -> p k f", p=128)


    with ExitStack() as ctx:
        lgt = _sb(ctx, nc, pfx + "lgt", [128, NTILES, 8], F32)
        m1 = _sb(ctx, nc, pfx + "m1", [128, 16, 2], F32)
        eqa = _sb(ctx, nc, pfx + "eqa", [128, 16, 8], F32)
        eqb = _sb(ctx, nc, pfx + "eqb", [128, 16, 8], F32)
        mka = _sb(ctx, nc, pfx + "mka", [128, 16, 8], F32)
        gka = _sb(ctx, nc, pfx + "gka", [128, 16, 8], F32)
        p1c = _sb(ctx, nc, pfx + "p1c", [128, 16], F32)
        pfxr = _sb(ctx, nc, pfx + "pfxr", [1, 16, 8], F32)
        offs = _sb(ctx, nc, pfx + "offs", [1, 8], F32)
        tokc = _sb(ctx, nc, pfx + "tokc", [128, 1], U32)
        tokf = _sb(ctx, nc, pfx + "tokf", [128, 1], F32)
        t2048 = _sb(ctx, nc, pfx + "t2048", [128, 1], F32)
        toff0 = _sb(ctx, nc, pfx + "toff0", [128, 16], F32)
        toff2k = _sb(ctx, nc, pfx + "toff2k", [128, 16], F32)
        scr16 = _sb(ctx, nc, pfx + "scr16", [128, 6, 16], F32)
        ridx16 = _sb(ctx, nc, pfx + "ridx16", [128, 16, 8], F32)
        pay = [_sb(ctx, nc, pfx + f"pay{i}", [128, 16, 2], F32) for i in range(2)]
        slotu = [_sb(ctx, nc, pfx + f"slotu{i}", [128, 16], U32) for i in range(2)]
        smap = _sb(ctx, nc, pfx + "smap", [128, 16, 2], F32)
        gidx = _sb(ctx, nc, pfx + "gidx", [128, 1], U32)
        gatec = _sb(ctx, nc, pfx + "gatec", [128, 16], F32)
        xg = [_sb(ctx, nc, pfx + f"xg{i}", [128, 1024], F32) for i in range(2)]
        xe_bf = _sb(ctx, nc, pfx + "xe_bf", [128, KT, 2048], BF16)
        h_bf = _sb(ctx, nc, pfx + "h_bf", [128, 16, 2048], BF16)
        wob = _sb(ctx, nc, pfx + "wob", [128, 16, 1024], BF16)
        vsb = [_sb(ctx, nc, pfx + f"vsb{i}", [128, KT, 128], BF16) for i in range(2)]
        t1 = _sb(ctx, nc, pfx + "t1", [128, 512], F32)
        ytm = _sb(ctx, nc, pfx + "ytm", [128, 2, 512], BF16)
        ehot = _sb(ctx, nc, pfx + "ehot_s", [128, 8], F32)
        ebase = _sb(ctx, nc, pfx + "ebase_s", [128, 8], F32)
        triu_s = _sb(ctx, nc, pfx + "triu_s", [128, 128], F32)
        onesc = _sb(ctx, nc, pfx + "onesc_b", [128, 1], F32)
        ones1c = _sb(ctx, nc, pfx + "ones1c_b", [1, 128], F32)
        identf = _sb(ctx, nc, pfx + "identf_b", [128, 128], F32)
        identbf_s = _sb(ctx, nc, pfx + "identbf_b", [128, 128], BF16)

        prt = _ps(ctx, nc, pfx + "prt", [128, 512])
        prt2 = _ps(ctx, nc, pfx + "prt2", [128, 512])
        pa = [_ps(ctx, nc, pfx + f"pa{i}", [128, 512]) for i in range(2)]
        pb = [_ps(ctx, nc, pfx + f"pb{i}", [128, 512]) for i in range(2)]
        pyt = [_ps(ctx, nc, pfx + f"pyt{i}", [128, 512]) for i in range(2)]

        s_lc = P.sem(pfx + "s_lc", "sync")
        s_ibf = P.sem(pfx + "s_ibf", "gpsimd")
        s_w = [P.sem(pfx + "s_w0", "gpsimd"), P.sem(pfx + "s_w1", "gpsimd")]
        s_gi = P.sem(pfx + "s_gi", "gpsimd")
        s_gio = P.sem(pfx + "s_gio", "gpsimd")
        s_wo = P.sem(pfx + "s_wo", "gpsimd")
        s_gg = [P.sem(pfx + "s_gg0", "gpsimd"), P.sem(pfx + "s_gg1", "gpsimd")]
        s_pe = P.sem(pfx + "s_pe", "tensor")
        s_act = P.sem(pfx + "s_act", "scalar")
        s_ve = P.sem(pfx + "s_ve", "vector")
        s_st = P.sem(pfx + "s_st", "sync")

        for dst, src in [(ehot, ehot_d), (ebase, d["ebase"]), (triu_s, triu_d),
                         (onesc, ones_d), (ones1c, ones1_d), (identf, identf_d)]:
            P.emit("sync", lambda e, dst=dst, src=src: e.dma_start(out=dst[:], in_=src),
                   inc=s_lc, dma=True)
        P.emit("sync", lambda e: e.dma_start(
            out=lgt[:], in_=lg_d.rearrange("(i p) f -> p i f", p=128)), inc=s_lc, dma=True)
        P.emit("sync", lambda e: e.dma_start(out=slotmap, in_=slotinit_d), inc=s_lc, dma=True)
        ld0 = P.n(s_lc)
        for eng in ("tensor", "scalar", "vector", "gpsimd"):
            P.wait(eng, s_lc, ld0)
        P.emit("vector", lambda e: e.memset(offs[:], 0.0), inc=s_ve)
        P.emit("gpsimd", lambda e: e.iota(tokc[:], pattern=[[0, 1]], base=0,
                                          channel_multiplier=1), inc=s_gio)
        P.wait("vector", s_gio, P.n(s_gio))
        P.emit("vector", lambda e: e.tensor_copy(tokf[:], tokc[:]), inc=s_ve)
        P.wait("vector", s_ve, P.n(s_ve))
        P.emit("vector", lambda e: e.tensor_scalar(
            out=t2048[:], in0=tokf[:], scalar1=2048.0, scalar2=None, op0=ALU.add), inc=s_ve)
        for t in range(16):
            P.emit("vector", lambda e, t=t: e.tensor_scalar(
                out=toff0[:, t:t + 1], in0=tokf[:], scalar1=float(t * 128), scalar2=None,
                op0=ALU.add), inc=s_ve)
        P.wait("vector", s_ve, P.n(s_ve))
        P.emit("vector", lambda e: e.tensor_scalar(
            out=toff2k[:], in0=toff0[:], scalar1=2048.0, scalar2=None, op0=ALU.add), inc=s_ve)
        P.wait("vector", s_ve, P.n(s_ve))

        want_ridx = "ridx" in d
        scat_n = [0, 0]

        def group_cb(g):
            pr = g % 2
            B3 = [128, 16, 8]
            cntv = prt[:, 0:128].rearrange("p (a b) -> p a b", b=8)
            eb3 = ehot[:].unsqueeze(1).broadcast_to(B3)
            P.wait("vector", s_ve, P.n(s_ve))
            P.emit("vector", lambda e: e.tensor_tensor(out=eqa[:], in0=gka[:], in1=eb3,
                                                       op=ALU.mult), inc=s_ve)
            P.wait("vector", s_ve, P.n(s_ve))
            P.emit("vector", lambda e: e.tensor_reduce(
                out=scr16[:, 0, :], in_=eqa[:], axis=mybir.AxisListType.X, op=ALU.add),
                inc=s_ve)
            P.wait("vector", s_ve, P.n(s_ve))
            P.emit("vector", lambda e, cntv=cntv: e.tensor_tensor(
                out=eqa[:], in0=cntv, in1=eb3, op=ALU.mult), inc=s_ve)
            P.wait("vector", s_ve, P.n(s_ve))
            P.emit("vector", lambda e: e.tensor_reduce(
                out=scr16[:, 1, :], in_=eqa[:], axis=mybir.AxisListType.X, op=ALU.add),
                inc=s_ve)
            P.wait("vector", s_ve, P.n(s_ve))
            P.emit("vector", lambda e: e.tensor_tensor(out=eqa[:], in0=mka[:], in1=eb3,
                                                       op=ALU.mult), inc=s_ve)
            P.wait("vector", s_ve, P.n(s_ve))
            P.emit("vector", lambda e: e.tensor_reduce(
                out=scr16[:, 2, :], in_=eqa[:], axis=mybir.AxisListType.X, op=ALU.add),
                inc=s_ve)
            if want_ridx:
                ebb = ebase[:].unsqueeze(1).broadcast_to(B3)
                P.wait("vector", s_ve, P.n(s_ve))
                P.emit("vector", lambda e, cntv=cntv, ebb=ebb: e.tensor_tensor(
                    out=ridx16[:], in0=cntv, in1=ebb, op=ALU.add), inc=s_ve)
                P.wait("vector", s_ve, P.n(s_ve))
                P.emit("vector", lambda e: e.tensor_tensor(
                    out=ridx16[:], in0=ridx16[:], in1=mka[:], op=ALU.mult), inc=s_ve)
                P.wait("vector", s_ve, P.n(s_ve))
                P.emit("vector", lambda e: e.tensor_scalar(
                    out=eqa[:], in0=mka[:], scalar1=-16384.0, scalar2=16384.0,
                    op0=ALU.mult, op1=ALU.add), inc=s_ve)
                P.wait("vector", s_ve, P.n(s_ve))
                P.emit("vector", lambda e, g=g: e.tensor_tensor(
                    out=rix_sb[:, g * 16:(g + 1) * 16, :], in0=ridx16[:], in1=eqa[:],
                    op=ALU.add), inc=s_ve)
            pool.free(prt, s_ve)
            P.wait("vector", s_ve, P.n(s_ve))
            P.emit("vector", lambda e: e.tensor_scalar(
                out=scr16[:, 3, :], in0=scr16[:, 2, :], scalar1=-1.0, scalar2=1.0,
                op0=ALU.mult, op1=ALU.add), inc=s_ve)
            P.wait("vector", s_ve, P.n(s_ve))
            P.emit("vector", lambda e: e.tensor_tensor(
                out=scr16[:, 3, :], in0=scr16[:, 3, :], in1=toff2k[:], op=ALU.mult), inc=s_ve)
            P.wait("vector", s_ve, P.n(s_ve))
            P.emit("vector", lambda e: e.tensor_tensor(
                out=scr16[:, 4, :], in0=scr16[:, 1, :], in1=scr16[:, 2, :], op=ALU.mult),
                inc=s_ve)
            P.wait("vector", s_ve, P.n(s_ve))
            if scat_n[pr]:
                P.wait("vector", s_gi, scat_n[pr])
            P.emit("vector", lambda e, pr=pr: e.scalar_tensor_tensor(
                out=slotf(pr), in0=scr16[:, 4, :], scalar=1.0, in1=scr16[:, 3, :],
                op0=ALU.mult, op1=ALU.add), inc=s_ve) if False else None
            P.emit("vector", lambda e: e.tensor_tensor(
                out=scr16[:, 4, :], in0=scr16[:, 4, :], in1=scr16[:, 3, :], op=ALU.add),
                inc=s_ve)
            P.wait("vector", s_ve, P.n(s_ve))
            P.emit("vector", lambda e, pr=pr: e.tensor_copy(slotu[pr][:], scr16[:, 4, :]),
                   inc=s_ve)
            P.emit("vector", lambda e, g=g, pr=pr: e.tensor_scalar(
                out=pay[pr][:, :, 0], in0=toff0[:], scalar1=float(g * 2048), scalar2=None,
                op0=ALU.add), inc=s_ve)
            P.emit("vector", lambda e, pr=pr: e.tensor_copy(pay[pr][:, :, 1], scr16[:, 0, :]),
                   inc=s_ve)
            v_pay = P.n(s_ve)
            P.wait("gpsimd", s_ve, v_pay)
            if scat_n[1 - pr]:
                P.wait("gpsimd", s_gi, scat_n[1 - pr])
            for t in range(16):
                P.emit("gpsimd", lambda e, pr=pr, t=t: e.indirect_dma_start(
                    out=slotmap, out_offset=bass.IndirectOffsetOnAxis(
                        ap=slotu[pr][:, t:t + 1], axis=0),
                    in_=pay[pr][:, t, :], in_offset=None), inc=s_gi, dma=True)
            scat_n[pr] = P.n(s_gi)

        rix_sb = None
        if want_ridx:
            rix_sb = _sb(ctx, nc, pfx + "rix_sb", [128, NTILES, 8], F32)
        emit_routing16(P, pool, nc,
                       (lgt, m1, eqa, eqb, mka, gka, p1c, pfxr, offs, prt, prt2,
                        triu_s, ones1c, onesc, s_pe, s_act, s_ve, group_cb), NTILES)
        if want_ridx:
            r_done = P.n(s_ve)
            P.wait("sync", s_ve, r_done)
            P.emit("sync", lambda e: e.dma_start(
                out=d["ridx"].rearrange("(i p) f -> p i f", p=128), in_=rix_sb[:]),
                inc=s_lc, dma=True)

        P.wait("sync", s_gi, P.n(s_gi))
        P.emit("sync", lambda e: e.dma_start(
            out=smap[:], in_=slotmap[0:2048, :].rearrange("(s p) c -> p s c", p=128)),
            inc=s_lc, dma=True)
        sm_rdy = P.n(s_lc)
        P.wait("vector", s_lc, sm_rdy)
        P.emit("vector", lambda e: e.tensor_copy(gatec[:], smap[:, :, 1]), inc=s_ve)
        v_gate = P.n(s_ve)
        xg_guard = [0, 0]
        P.emit("gpsimd", lambda e: e.dma_start(out=wob[:], in_=wo_r), inc=s_wo, dma=True)
        wo_rdy = P.n(s_wo)
        for st in range(16):
            par = st % 2
            P.wait("vector", s_lc, sm_rdy)
            P.wait("vector", s_ve, P.n(s_ve))
            P.emit("vector", lambda e, st=st: e.tensor_copy(gidx[:], smap[:, st, 0:1]),
                   inc=s_ve)
            v_gi = P.n(s_ve)
            P.wait("gpsimd", s_ve, v_gi)
            if st >= 2:
                P.wait("gpsimd", s_pe, xg_guard[par])
            P.emit("gpsimd", lambda e, par=par: e.indirect_dma_start(
                out=xg[par][:], out_offset=None, in_=xf_d,
                in_offset=bass.IndirectOffsetOnAxis(ap=gidx[:], axis=0)), inc=s_gg[par],
                dma=True)
            g_rdy = P.n(s_gg[par])
            P.wait("tensor", s_gg[par], g_rdy)
            for j in range(KT):
                pool.use(prt)
                P.emit("tensor", lambda e, j=j, par=par: e.transpose(
                    prt[:, 0:128], xg[par][:, j * 128:(j + 1) * 128], identf[:]), inc=s_pe)
                pe_t = P.n(s_pe)
                P.wait("vector", s_pe, pe_t)
                P.emit("vector", lambda e, j=j, st=st: e.tensor_copy(
                    xe_bf[:, j, st * 128:(st + 1) * 128], prt[:, 0:128]), inc=s_ve)
                pool.free(prt, s_ve)
            xg_guard[par] = P.n(s_pe)
        xe_done = P.n(s_ve)
        P.wait("tensor", s_ve, xe_done)

        wfree = []

        def stream_b(src_r, fi, nk, buf):
            idx = len(wfree)
            par = idx % 2
            if idx >= 2:
                P.wait("gpsimd", *wfree[idx - 2])
            P.emit("gpsimd", lambda e, src_r=src_r, fi=fi, par=par, nk=nk, buf=buf: e.dma_start(
                out=buf[par][:, 0:nk, :], in_=src_r[:, fi]),
                inc=s_w[par], dma=True)
            wfree.append(None)
            return idx, par, (s_w[par], P.n(s_w[par]))

        t1_guard = [0]
        for j in range(16):
            idxw, parw, (ws, wv) = stream_b(we_r, j, 8, vsb)
            P.wait("tensor", ws, wv)
            idxv, parv, (ws2, wv2) = stream_b(ve_r, j, 8, vsb)
            P.wait("tensor", ws2, wv2)
            for n in range(4):
                A, B = pa[n % 2], pb[n % 2]
                pool.use(A)
                for k in range(KT):
                    P.emit("tensor", lambda e, k=k, parw=parw, n=n, A=A: e.matmul(
                        A[:], vsb[parw][:, k, :], xe_bf[:, k, n * 512:(n + 1) * 512],
                        start=(k == 0), stop=(k == KT - 1), skip_group_check=True),
                        inc=(s_pe if k == KT - 1 else None))
                pe_a = P.n(s_pe)
                pool.use(B)
                for k in range(KT):
                    P.emit("tensor", lambda e, k=k, parv=parv, n=n, B=B: e.matmul(
                        B[:], vsb[parv][:, k, :], xe_bf[:, k, n * 512:(n + 1) * 512],
                        start=(k == 0), stop=(k == KT - 1), skip_group_check=True),
                        inc=(s_pe if k == KT - 1 else None))
                pe_b = P.n(s_pe)
                P.wait("scalar", s_pe, pe_b)
                if t1_guard[0]:
                    P.wait("scalar", s_ve, t1_guard[0])
                P.emit("scalar", lambda e, B=B: e.activation(t1[:], B[:], AF.Silu), inc=s_act)
                pool.free(B, s_act)
                a_t1 = P.n(s_act)
                P.wait("vector", s_pe, pe_a)
                P.wait("vector", s_act, a_t1)
                P.emit("vector", lambda e, j=j, n=n, A=A: e.tensor_tensor(
                    out=h_bf[:, j, n * 512:(n + 1) * 512], in0=A[:], in1=t1[:], op=ALU.mult),
                    inc=s_ve)
                t1_guard[0] = P.n(s_ve)
                pool.free(A, s_ve)
            wfree[idxw] = (s_pe, pe_a)
            wfree[idxv] = (s_pe, pe_b)
        h_done = P.n(s_ve)
        P.wait("tensor", s_ve, h_done)

        ytm_guard = [0]
        P.wait("tensor", s_wo, wo_rdy)
        P.wait("vector", s_lc, sm_rdy)
        P.wait("vector", s_ve, v_gate)
        for stile in range(16):
            for n in range(2):
                Y = pyt[n]
                pool.use(Y)
                for k in range(16):
                    P.emit("tensor", lambda e, k=k, stile=stile, n=n, Y=Y: e.matmul(
                        Y[:], h_bf[:, k, stile * 128:(stile + 1) * 128],
                        wob[:, k, n * 512:(n + 1) * 512],
                        start=(k == 0), stop=(k == 15), skip_group_check=True),
                        inc=(s_pe if k == 15 else None))
                pe_y = P.n(s_pe)
                P.wait("vector", s_pe, pe_y)
                if ytm_guard[0]:
                    P.wait("vector", s_st, ytm_guard[0])
                P.emit("vector", lambda e, n=n, Y=Y, stile=stile: e.tensor_scalar(
                    out=ytm[:, n, :], in0=Y[:],
                    scalar1=gatec[:, stile:stile + 1], scalar2=None, op0=ALU.mult), inc=s_ve)
                pool.free(Y, s_ve)
            v_y = P.n(s_ve)
            P.wait("sync", s_ve, v_y)
            P.emit("sync", lambda e, stile=stile: e.dma_start(
                out=ye_d[stile * 128:(stile + 1) * 128, :].rearrange("p (a b) -> p a b", b=512),
                in_=ytm[:]), inc=s_st, dma=True)
            ytm_guard[0] = P.n(s_st)
        P.wait("sync", s_st, P.n(s_st))



def build_moe_nc():
    nc = bass.Bass("TRN2", target_bir_lowering=False)
    NTOK = BATCH * SEQ
    d = {}
    d["lg"] = nc.dram_tensor("lg", [NTOK, 8], F32, kind="ExternalInput").ap()
    d["xf"] = nc.dram_tensor("xf", [NTOK + 1, 1024], BF16, kind="ExternalInput").ap()
    d["identbf"] = nc.dram_tensor("identbf", [128, 128], F32, kind="ExternalInput").ap()
    d["we"] = nc.dram_tensor("we", [128, 16, 8, 128], F32, kind="ExternalInput").ap()
    d["vex"] = nc.dram_tensor("vex", [128, 16, 8, 128], F32, kind="ExternalInput").ap()
    d["wo"] = nc.dram_tensor("wo", [2048, 1024], F32, kind="ExternalInput").ap()
    d["ehot"] = nc.dram_tensor("ehot", [128, 8], F32, kind="ExternalInput").ap()
    d["slotinit"] = nc.dram_tensor("slotinit", [2048 + 2048, 2], F32, kind="ExternalInput").ap()
    d["trius"] = nc.dram_tensor("trius", [128, 128], F32, kind="ExternalInput").ap()
    d["ones"] = nc.dram_tensor("ones", [128, 1], F32, kind="ExternalInput").ap()
    d["ones1"] = nc.dram_tensor("ones1", [1, 128], F32, kind="ExternalInput").ap()
    d["identf"] = nc.dram_tensor("identf", [128, 128], F32, kind="ExternalInput").ap()
    d["ye"] = nc.dram_tensor("ye", [2048, 1024], BF16, kind="ExternalOutput").ap()
    d["slotmap"] = nc.dram_tensor("slotmap", [2048 + 2048, 2], F32).ap()
    with ExitStack() as ctx:
        block = ctx.enter_context(nc.Block())
        P = Prog(ctx, nc)
        pool = PsumPool(P)
        emit_moe(nc, P, pool, d)
        P.build_block(block)
    return nc

def host_moe_inputs(logits_all, x_full_pad, W, V, W_out, e):
    i = np.arange(128)
    trius = (i[:, None] < i[None, :]).astype(np.float32)
    ehot = np.zeros((128, 8), np.float32)
    ehot[:, e] = 1.0
    slotinit = np.zeros((2048 + 2048, 2), np.float32)
    slotinit[:, 0] = float(BATCH * SEQ)
    return dict(
        lg=np.ascontiguousarray(logits_all, dtype=np.float32),
        xf=x_full_pad,
        we=_wrl128(np.asarray(W[e], np.float32)),
        vex=_wrl128(np.asarray(V[e], np.float32)),
        wo=np.ascontiguousarray(np.asarray(W_out[e], np.float32)),
        ehot=ehot, slotinit=slotinit, trius=trius,
        ones=np.ones((128, 1), np.float32), ones1=np.ones((1, 128), np.float32),
        identf=np.eye(128, dtype=np.float32),
    )
def emit_comb(nc, P, pool, d, pfx="c"):
    NTOK = BATCH * SEQ
    NTILES = NTOK // 128
    YROWS = 8 * 2048
    lg_d = d["lg"]
    ya_d = d["ya"]
    xr_d = d["xr"]
    myidx_d = d["myidx"]
    ebase_d = d["ebase"]
    triu_d = d["trius"]
    ones_d = d["ones"]
    ones1_d = d["ones1"]
    identf_d = d["identf"]
    out_d = d["out"]
    ridx_d = d["ridx"]

    xr_r = xr_d.rearrange("(k p) t -> p k t", p=128)
    out_r = out_d.rearrange("(k p) t -> p k t", p=128)


    with ExitStack() as ctx:
        lgt = _sb(ctx, nc, pfx + "lgt", [128, NTILES, 8], F32)
        m1 = _sb(ctx, nc, pfx + "m1", [128, 1], F32)
        m2c = _sb(ctx, nc, pfx + "m2c", [128, 1], F32)
        eqa = _sb(ctx, nc, pfx + "eqa", [128, 8], F32)
        eqb = _sb(ctx, nc, pfx + "eqb", [128, 8], F32)
        mka = _sb(ctx, nc, pfx + "mka", [128, 8], F32)
        gka = _sb(ctx, nc, pfx + "gka", [128, 8], F32)
        p1c = _sb(ctx, nc, pfx + "p1c", [128, 1], F32)
        p2c = _sb(ctx, nc, pfx + "p2c", [128, 1], F32)
        offs = _sb(ctx, nc, pfx + "offs", [1, 8], F32)
        ridx_sb = _sb(ctx, nc, pfx + "ridx_sb", [128, NTILES, 8], F32)
        ro = _sb(ctx, nc, pfx + "ro", [128, 8], F32)
        gix = _sb(ctx, nc, pfx + "gix", [128, 1], U32)
        myidx = _sb(ctx, nc, pfx + "myidx_s", [128, SEQ // 128], U32)
        ebase = _sb(ctx, nc, pfx + "ebase_s", [128, 8], F32)
        ygb8 = [_sb(ctx, nc, pfx + f"ygb8_{i}", [128, 1024], F32) for i in range(8)]
        gix8 = _sb(ctx, nc, pfx + "gix8", [128, 8], U32)
        x_f = _sb(ctx, nc, pfx + "x_f", [128, KT, SEQ], F32)
        xsq = _sb(ctx, nc, pfx + "xsq", [128, 512], F32)
        srow = _sb(ctx, nc, pfx + "srow_c", [1, 2, 512], F32)
        triu_s = _sb(ctx, nc, pfx + "triu_s", [128, 128], F32)
        onesc = _sb(ctx, nc, pfx + "onesc_c", [128, 1], F32)
        ones1c = _sb(ctx, nc, pfx + "ones1c_c", [1, 128], F32)
        identf = _sb(ctx, nc, pfx + "identf_c", [128, 128], F32)
        identbf_s = _sb(ctx, nc, pfx + "identbf_c", [128, 128], BF16)

        prt = _ps(ctx, nc, pfx + "prt", [128, 512])
        prt2 = _ps(ctx, nc, pfx + "prt2", [128, 512])
        ptr = _ps(ctx, nc, pfx + "ptr", [128, 512])
        pst = _ps(ctx, nc, pfx + "pst", [128, 512])
        pst2 = _ps(ctx, nc, pfx + "pst2", [128, 512])
        pbc = _ps(ctx, nc, pfx + "pbc", [128, 512])
        pbc2 = _ps(ctx, nc, pfx + "pbc2", [128, 512])

        s_lc = P.sem(pfx + "s_lc", "sync")
        s_gi = P.sem(pfx + "s_gi", "gpsimd")
        s_gr = P.sem(pfx + "s_gr", "gpsimd")
        s_gy8 = [P.sem(pfx + f"s_gy8_{i}", "gpsimd") for i in range(8)]
        s_ibf = P.sem(pfx + "s_ibf", "gpsimd")
        s_pe = P.sem(pfx + "s_pe", "tensor")
        s_act = P.sem(pfx + "s_act", "scalar")
        s_ve = P.sem(pfx + "s_ve", "vector")
        s_st = P.sem(pfx + "s_st", "sync")

        for dst, src in [(triu_s, triu_d), (onesc, ones_d), (ones1c, ones1_d),
                         (identf, identf_d), (myidx, myidx_d), (ebase, ebase_d)]:
            P.emit("sync", lambda e, dst=dst, src=src: e.dma_start(out=dst[:], in_=src),
                   inc=s_lc, dma=True)
        P.emit("sync", lambda e: e.dma_start(
            out=lgt[:], in_=lg_d.rearrange("(i p) f -> p i f", p=128)), inc=s_lc, dma=True)
        P.emit("sync", lambda e: e.dma_start(out=x_f[:], in_=xr_r), inc=s_lc, dma=True)
        ld0 = P.n(s_lc)
        for eng in ("tensor", "scalar", "vector"):
            P.wait(eng, s_lc, ld0)
        P.emit("vector", lambda e: e.memset(offs[:], 0.0), inc=s_ve)

        if not d.get("ridx_ready"):
            def tile_cb(i):
                P.wait("vector", s_ve, P.n(s_ve))
                P.emit("vector", lambda e: e.tensor_tensor(
                    out=eqa[:], in0=prt[:, 0:8], in1=ebase[:], op=ALU.add), inc=s_ve)
                pool.free(prt, s_ve)
                P.wait("vector", s_ve, P.n(s_ve))
                P.emit("vector", lambda e: e.tensor_tensor(
                    out=eqa[:], in0=eqa[:], in1=mka[:], op=ALU.mult), inc=s_ve)
                P.emit("vector", lambda e: e.tensor_scalar(
                    out=eqb[:], in0=mka[:], scalar1=float(-YROWS), scalar2=float(YROWS),
                    op0=ALU.mult, op1=ALU.add), inc=s_ve)
                P.wait("vector", s_ve, P.n(s_ve))
                P.emit("vector", lambda e, i=i: e.tensor_tensor(
                    out=ridx_sb[:, i, :], in0=eqa[:], in1=eqb[:], op=ALU.add), inc=s_ve)

            emit_routing(P, pool, nc,
                         (lgt, m1, m2c, eqa, eqb, mka, gka, p1c, p2c, offs, None, prt, prt2,
                          triu_s, ones1c, onesc, s_pe, s_act, s_ve, tile_cb), NTILES)
            r_done = P.n(s_ve)
            P.wait("sync", s_ve, r_done)
            P.emit("sync", lambda e: e.dma_start(
                out=ridx_d.rearrange("(i p) f -> p i f", p=128), in_=ridx_sb[:]),
                inc=s_lc, dma=True)
        P.wait("gpsimd", s_lc, P.n(s_lc))

        acc_guard = [0]
        gu_guard = [0]
        NT = SEQ // 128
        for t in range(NT):
            P.emit("gpsimd", lambda e, t=t: e.indirect_dma_start(
                out=ro[:], out_offset=None, in_=ridx_d,
                in_offset=bass.IndirectOffsetOnAxis(ap=myidx[:, t:t + 1], axis=0)),
                inc=s_gr, dma=True)
            ro_rdy = P.n(s_gr)
            P.wait("vector", s_gr, ro_rdy)
            if gu_guard[0]:
                P.wait("vector", s_ve, gu_guard[0])
            P.emit("vector", lambda e: e.tensor_copy(gix8[:], ro[:]), inc=s_ve)
            v_gx = P.n(s_ve)
            gu_guard[0] = v_gx
            P.wait("gpsimd", s_ve, v_gx)
            if acc_guard[0]:
                P.wait("gpsimd", s_ve, acc_guard[0])
            for ei in range(8):
                P.emit("gpsimd", lambda e, ei=ei: e.indirect_dma_start(
                    out=ygb8[ei][:], out_offset=None, in_=ya_d,
                    in_offset=bass.IndirectOffsetOnAxis(ap=gix8[:, ei:ei + 1], axis=0)),
                    inc=s_gy8[ei], dma=True)
            for ei in range(8):
                P.wait("vector", s_gy8[ei], P.n(s_gy8[ei]))
            for a, b2 in [(0, 1), (2, 3), (4, 5), (6, 7)]:
                P.emit("vector", lambda e, a=a, b2=b2: e.tensor_tensor(
                    out=ygb8[a][:], in0=ygb8[a][:], in1=ygb8[b2][:], op=ALU.add), inc=s_ve)
            P.wait("vector", s_ve, P.n(s_ve))
            for a, b2 in [(0, 2), (4, 6)]:
                P.emit("vector", lambda e, a=a, b2=b2: e.tensor_tensor(
                    out=ygb8[a][:], in0=ygb8[a][:], in1=ygb8[b2][:], op=ALU.add), inc=s_ve)
            P.wait("vector", s_ve, P.n(s_ve))
            P.emit("vector", lambda e: e.tensor_tensor(
                out=ygb8[0][:], in0=ygb8[0][:], in1=ygb8[4][:], op=ALU.add), inc=s_ve)
            v_acc = P.n(s_ve)
            acc_guard[0] = v_acc
            P.wait("tensor", s_ve, v_acc)
            for j in range(KT):
                pool.use(ptr)
                P.emit("tensor", lambda e, j=j: e.transpose(
                    ptr[:, 0:128], ygb8[0][:, j * 128:(j + 1) * 128], identf[:]), inc=s_pe)
                pe_t = P.n(s_pe)
                P.wait("vector", s_pe, pe_t)
                P.emit("vector", lambda e, j=j, t=t: e.tensor_tensor(
                    out=x_f[:, j, t * 128:(t + 1) * 128], in0=x_f[:, j, t * 128:(t + 1) * 128],
                    in1=ptr[:, 0:128], op=ALU.add), inc=s_ve)
                pool.free(ptr, s_ve)
            acc_guard[0] = P.n(s_ve)
        x1_done = P.n(s_ve)

        xsq_guard = [0]
        for n in range(SEQ // 512):
            ns = n * 512
            P.wait("scalar", s_ve, x1_done)
            pool.use(pst)
            pool.use(pst2)
            P.wait("tensor", s_ve, x1_done)
            for j in range(KT):
                if xsq_guard[0]:
                    P.wait("scalar", s_pe, xsq_guard[0])
                P.emit("scalar", lambda e, j=j, ns=ns: e.activation(
                    xsq[:], x_f[:, j, ns:ns + 512], AF.Square), inc=s_act)
                sqv = P.n(s_act)
                P.wait("tensor", s_act, sqv)
                P.emit("tensor", lambda e, j=j, ns=ns: e.matmul(
                    pst[0:1, :], onesc[:], x_f[:, j, ns:ns + 512],
                    start=(j == 0), stop=(j == KT - 1), skip_group_check=True),
                    inc=(s_pe if j == KT - 1 else None))
                P.emit("tensor", lambda e, j=j: e.matmul(
                    pst2[0:1, :], onesc[:], xsq[:],
                    start=(j == 0), stop=(j == KT - 1), skip_group_check=True), inc=s_pe)
                xsq_guard[0] = P.n(s_pe)
            pe_stats = P.n(s_pe)
            P.wait("vector", s_pe, pe_stats)
            P.emit("vector", lambda e: e.tensor_scalar(
                out=srow[:, 0, :], in0=pst[0:1, :], scalar1=1.0 / D_MODEL, scalar2=None,
                op0=ALU.mult), inc=s_ve)
            v_mean = P.n(s_ve)
            P.wait("vector", s_ve, v_mean)
            P.emit("vector", lambda e: e.tensor_tensor(
                out=srow[:, 1, :], in0=srow[:, 0, :], in1=srow[:, 0, :], op=ALU.mult), inc=s_ve)
            P.wait("vector", s_ve, P.n(s_ve))
            P.emit("vector", lambda e: e.scalar_tensor_tensor(
                out=srow[:, 1, :], in0=pst2[0:1, :], scalar=1.0 / D_MODEL, in1=srow[:, 1, :],
                op0=ALU.mult, op1=ALU.subtract), inc=s_ve)
            pool.free(pst, s_ve)
            pool.free(pst2, s_ve)
            P.wait("vector", s_ve, P.n(s_ve))
            P.emit("vector", lambda e: e.tensor_scalar(
                out=srow[:, 1, :], in0=srow[:, 1, :], scalar1=1e-5, scalar2=None,
                op0=ALU.add), inc=s_ve)
            v_var = P.n(s_ve)
            P.wait("scalar", s_ve, v_var)
            P.emit("scalar", lambda e: e.activation(srow[:, 1, :], srow[:, 1, :], AF.Ln),
                   inc=s_act)
            P.wait("scalar", s_act, P.n(s_act))
            P.emit("scalar", lambda e: e.activation(srow[:, 1, :], srow[:, 1, :], AF.Exp,
                                                    scale=-0.5), inc=s_act)
            act_r = P.n(s_act)
            pool.use(pbc)
            P.wait("tensor", s_ve, v_mean)
            P.emit("tensor", lambda e: e.matmul(pbc[:], ones1c[:], srow[:, 0, :],
                                                start=True, stop=True,
                                                skip_group_check=True))
            pool.use(pbc2)
            P.wait("tensor", s_act, act_r)
            P.emit("tensor", lambda e: e.matmul(pbc2[:], ones1c[:], srow[:, 1, :],
                                                start=True, stop=True,
                                                skip_group_check=True), inc=s_pe)
            pe_bc = P.n(s_pe)
            P.wait("vector", s_pe, pe_bc)
            for j in range(KT):
                P.emit("vector", lambda e, j=j, ns=ns: e.tensor_tensor(
                    out=x_f[:, j, ns:ns + 512], in0=x_f[:, j, ns:ns + 512], in1=pbc[:],
                    op=ALU.subtract), inc=s_ve)
                P.wait("vector", s_ve, P.n(s_ve))
                P.emit("vector", lambda e, j=j, ns=ns: e.tensor_tensor(
                    out=x_f[:, j, ns:ns + 512], in0=x_f[:, j, ns:ns + 512], in1=pbc2[:],
                    op=ALU.mult), inc=s_ve)
            pool.free(pbc, s_ve)
            pool.free(pbc2, s_ve)
            v_out = P.n(s_ve)
            P.wait("sync", s_ve, v_out)
            P.emit("sync", lambda e, ns=ns: e.dma_start(
                out=out_r[:, :, ns:ns + 512], in_=x_f[:, :, ns:ns + 512]), inc=s_st, dma=True)
        P.wait("sync", s_st, P.n(s_st))



def build_comb_nc():
    nc = bass.Bass("TRN2", target_bir_lowering=False)
    NTOK = BATCH * SEQ
    YROWS = 8 * 2048
    d = {}
    d["lg"] = nc.dram_tensor("lg", [NTOK, 8], F32, kind="ExternalInput").ap()
    d["ya"] = nc.dram_tensor("ya", [YROWS + 1, 1024], BF16, kind="ExternalInput").ap()
    d["identbf"] = nc.dram_tensor("identbf", [128, 128], F32, kind="ExternalInput").ap()
    d["xr"] = nc.dram_tensor("xr", [D_MODEL, SEQ], F32, kind="ExternalInput").ap()
    d["myidx"] = nc.dram_tensor("myidx", [128, SEQ // 128], U32, kind="ExternalInput").ap()
    d["ebase"] = nc.dram_tensor("ebase", [128, 8], F32, kind="ExternalInput").ap()
    d["trius"] = nc.dram_tensor("trius", [128, 128], F32, kind="ExternalInput").ap()
    d["ones"] = nc.dram_tensor("ones", [128, 1], F32, kind="ExternalInput").ap()
    d["ones1"] = nc.dram_tensor("ones1", [1, 128], F32, kind="ExternalInput").ap()
    d["identf"] = nc.dram_tensor("identf", [128, 128], F32, kind="ExternalInput").ap()
    d["out"] = nc.dram_tensor("out", [D_MODEL, SEQ], F32, kind="ExternalOutput").ap()
    d["ridx"] = nc.dram_tensor("ridx", [NTOK, 8], F32).ap()
    with ExitStack() as ctx:
        block = ctx.enter_context(nc.Block())
        P = Prog(ctx, nc)
        pool = PsumPool(P)
        emit_comb(nc, P, pool, d)
        P.build_block(block)
    return nc

def host_comb_inputs(logits_all, y_all_pad, x_fm_core, b):
    i = np.arange(128)
    trius = (i[:, None] < i[None, :]).astype(np.float32)
    myidx = (b * SEQ + np.arange(SEQ, dtype=np.uint32).reshape(SEQ // 128, 128).T).astype(np.uint32)
    ebase = np.broadcast_to((np.arange(8, dtype=np.float32) * 2048), (128, 8)).copy()
    return dict(
        lg=np.ascontiguousarray(logits_all, dtype=np.float32),
        ya=y_all_pad,
        xr=np.ascontiguousarray(x_fm_core, dtype=np.float32),
        myidx=np.ascontiguousarray(myidx), ebase=ebase, trius=trius,
        ones=np.ones((128, 1), np.float32), ones1=np.ones((1, 128), np.float32),
        identf=np.eye(128, dtype=np.float32),
    )


def _barrier(P):
    for eng in ENGS:
        for name, rec in P.sems.items():
            if rec[1] > 0 and rec[2] != eng:
                P.wait(eng, name)


def build_fused_nc():
    nc = bass.Bass("TRN2", target_bir_lowering=False)
    NTOK = BATCH * SEQ
    DI, CD, NP = 2048, 2176, 16
    f = F32

    def ein(name, shape, dt=F32):
        return nc.dram_tensor(name, shape, dt, kind="ExternalInput").ap()

    d = {}
    for name, shape in [("x", [D_MODEL, SEQ]), ("wzt", [128, 16, 8, 128]), ("wxt", [128, 17, 8, 128]),
                        ("wdt", [1024, 32]), ("wot_m", [128, 8, 16, 128]), ("normw", [128, KT]),
                        ("gnw", [128, NP]), ("convw", [128, 17, 4]), ("convb", [128, 17]),
                        ("drep", [128, NP]), ("dtb", [1, 32]), ("alog", [128, 32]),
                        ("rw1t", [128, 8, 32]), ("rb1", [32, 1]), ("rw2t", [32, 3]),
                        ("rb2", [1, 3]), ("rwt", [128, 8, 8]), ("rbr", [1, 8]),
                        ("ones", [128, 1]), ("ones1", [1, 128]), ("identf", [128, 128]),
                        ("identb", [128, 128]), ("triu", [128, 128]), ("negblk", [8, 1024]),
                        ("maskd", [128, 512]),
                        ("we", [128, 16, 8, 128]), ("vex", [128, 16, 8, 128]), ("wo", [2048, 1024]),
                        ("ehot", [128, 8]), ("slotinit", [2048 + 2048, 2]),
                        ("trius", [128, 128]), ("ebase", [128, 8]),
                        ("wqt", [128, 2, 8, 512]), ("wkt", [128, 2, 8, 512]), ("wvt", [128, 2, 8, 512]),
                        ("wot", [128, 2, 8, 512]), ("maskS", [128, 4, 512]), ("maskT", [128, 4, 512]),
                        ("identbf", [128, 128]),
                        ("w1t", [128, 32, 8, 128]), ("w2t", [128, 32, 8, 128]), ("h1t", [128, 32, 32, 128]),
                        ("h2t", [128, 32, 32, 128]), ("w3t", [128, 8, 32, 128]), ("s1", [128, 32]),
                        ("s2", [128, 32]), ("sh1", [128, 32]), ("sh2", [128, 32]),
                        ("s3", [128, 8])]:
        d[name] = ein(name, shape)
    d["myidx"] = ein("myidx", [128, SEQ // 128], U32)
    out_d = nc.dram_tensor("out", [D_MODEL, SEQ], F32, kind="ExternalOutput").ap()
    x1f = nc.dram_tensor("x1f", [D_MODEL, SEQ], F32).ap()
    xtm = nc.dram_tensor("xtm_i", [SEQ, 1024], BF16).ap()
    lgx = nc.dram_tensor("lgx", [SEQ, 8], F32).ap()
    lgall = nc.dram_tensor("lgall", [NTOK, 8], F32).ap()
    xfp = nc.dram_tensor("xfp", [NTOK + 1, 1024], BF16, addr_space="Shared").ap()
    yed = nc.dram_tensor("yed", [2048, 1024], BF16).ap()
    yap = nc.dram_tensor("yap", [8 * 2048 + 1, 1024], BF16, addr_space="Shared").ap()
    x2f = nc.dram_tensor("x2f", [D_MODEL, SEQ], F32).ap()
    x3f = nc.dram_tensor("x3f", [D_MODEL, SEQ], F32).ap()
    cbounce = [nc.dram_tensor(f"cbounce{i}", [32, 128], F32).ap() for i in range(2)]
    slotmap = nc.dram_tensor("slotmap", [2048 + 2048, 2], F32).ap()
    ridx = nc.dram_tensor("ridx", [NTOK, 8], F32).ap()

    base = {k: d[k] for k in d}
    dA = dict(base, out=x1f, xtm=xtm, logits=lgx, cbounce=cbounce)
    dB = dict(base, lg=lgall, xf=xfp, ye=yed, slotmap=slotmap, ridx=ridx)
    dC = dict(base, lg=lgall, ya=yap, xr=x1f, out=x2f, ridx=ridx, ridx_ready=True)
    dD = dict(base, x=x2f, out=x3f)
    dE = dict(base, x=x3f, out=out_d)

    with ExitStack() as ctx:
        block = ctx.enter_context(nc.Block())
        P = Prog(ctx, nc)
        s_z = P.sem("f_z", "sync")
        s_zv = P.sem("f_zv", "vector")
        cc = [P.sem(f"f_cc{i}", "gpsimd") for i in range(3)]
        with ExitStack() as zctx:
            zrow = _sb(zctx, nc, "f_zrow", [1, 1024], BF16)
            P.emit("vector", lambda e: e.memset(zrow[:], 0.0), inc=s_zv)
            P.wait("sync", s_zv, P.n(s_zv))
            P.emit("sync", lambda e: e.dma_start(out=xfp[NTOK:NTOK + 1, :], in_=zrow[:]),
                   inc=s_z, dma=True)
            P.emit("sync", lambda e: e.dma_start(out=yap[8 * 2048:8 * 2048 + 1, :], in_=zrow[:]),
                   inc=s_z, dma=True)
            P.wait("sync", s_z, P.n(s_z))
            _barrier(P)

        pool = PsumPool(P)
        emit_mamba(nc, P, pool, dA)
        _barrier(P)
        P.emit("gpsimd", lambda e: e.collective_compute(
            "AllGather", ALU.bypass, replica_groups=[list(range(8))],
            ins=[lgx.opt()], outs=[lgall.opt()]), inc=cc[0])
        P.emit("gpsimd", lambda e: e.collective_compute(
            "AllGather", ALU.bypass, replica_groups=[list(range(8))],
            ins=[xtm.opt()], outs=[xfp[0:NTOK, :].opt()]), inc=cc[1])
        _barrier(P)
        pool = PsumPool(P)
        emit_moe(nc, P, pool, dB)
        _barrier(P)
        P.emit("gpsimd", lambda e: e.collective_compute(
            "AllGather", ALU.bypass, replica_groups=[list(range(8))],
            ins=[yed.opt()], outs=[yap[0:8 * 2048, :].opt()]), inc=cc[2])
        _barrier(P)
        pool = PsumPool(P)
        emit_comb(nc, P, pool, dC)
        _barrier(P)
        pool = PsumPool(P)
        emit_attn(nc, P, pool, dD)
        _barrier(P)
        pool = PsumPool(P)
        emit_ffn(nc, P, pool, dE)
        P.build_block(block)
    return nc


def host_fused_inputs(inputs, b):
    f32 = np.float32
    x = np.asarray(inputs["L0_attn_x"], f32)
    wA = {k[len("L0_attn_"):]: np.asarray(v, f32) for k, v in inputs.items()
          if k.startswith("L0_attn_") and k != "L0_attn_x"}
    wD = {"wq": inputs["L1_attn_wq"], "wk": inputs["L1_attn_wk"],
          "wv": inputs["L1_attn_wv"], "wo": inputs["L1_attn_wo"]}
    wE = {k[len("L1_ffn_"):]: np.asarray(v, f32) for k, v in inputs.items()
          if k.startswith("L1_ffn_")}
    m = {}
    m.update(host_mamba_inputs(x[b].T, wA, inputs["L0_ffn_router_w"], inputs["L0_ffn_router_b"]))
    dummy = np.zeros((1, 1), f32)
    mo = host_moe_inputs(dummy, dummy,
                         np.asarray(inputs["L0_ffn_W"], f32),
                         np.asarray(inputs["L0_ffn_V"], f32),
                         np.asarray(inputs["L0_ffn_W_out"], f32), b)
    mo.pop("lg"); mo.pop("xf")
    m.update(mo)
    co = host_comb_inputs(dummy, dummy, dummy, b)
    co.pop("lg"); co.pop("ya"); co.pop("xr")
    m.update(co)
    at = host_attn_inputs(dummy, wD)
    at.pop("x")
    m.update(at)
    ff = host_ffn_inputs(dummy, wE)
    ff.pop("x")
    m.update(ff)
    return m


def kernel(**inputs):
    nc = build_fused_nc()
    in_maps = [host_fused_inputs(inputs, b) for b in range(8)]
    res = run_bass_kernel_spmd(nc, in_maps, core_ids=list(range(8)))
    out = np.stack([res.results[b]["out"].T for b in range(8)])
    return np.ascontiguousarray(out, dtype=np.float32)


def kernel_multilaunch(**inputs):
    f32 = np.float32
    x = np.asarray(inputs["L0_attn_x"], f32)
    wA = {k[len("L0_attn_"):]: np.asarray(v, f32) for k, v in inputs.items()
          if k.startswith("L0_attn_") and k != "L0_attn_x"}
    router_w = np.asarray(inputs["L0_ffn_router_w"], f32)
    router_b = np.asarray(inputs["L0_ffn_router_b"], f32)
    W = np.asarray(inputs["L0_ffn_W"], f32)
    V = np.asarray(inputs["L0_ffn_V"], f32)
    Wo = np.asarray(inputs["L0_ffn_W_out"], f32)
    wD = {"wq": inputs["L1_attn_wq"], "wk": inputs["L1_attn_wk"],
          "wv": inputs["L1_attn_wv"], "wo": inputs["L1_attn_wo"]}
    wE = {k[len("L1_ffn_"):]: np.asarray(v, f32) for k, v in inputs.items()
          if k.startswith("L1_ffn_")}
    cores = list(range(8))

    ncA = build_mamba_nc()
    resA = run_bass_kernel_spmd(
        ncA, [host_mamba_inputs(x[b].T, wA, router_w, router_b) for b in range(8)],
        core_ids=cores)
    x1 = [resA.results[b]["out"] for b in range(8)]
    logits_all = np.ascontiguousarray(
        np.concatenate([resA.results[b]["logits"] for b in range(8)], axis=0))
    x_full_pad = np.ascontiguousarray(np.concatenate(
        [resA.results[b]["xtm"] for b in range(8)] + [np.zeros((1, 1024), f32)], axis=0))

    ncB = build_moe_nc()
    resB = run_bass_kernel_spmd(
        ncB, [host_moe_inputs(logits_all, x_full_pad, W, V, Wo, e) for e in range(8)],
        core_ids=cores)
    y_all_pad = np.ascontiguousarray(np.concatenate(
        [resB.results[e]["ye"] for e in range(8)] + [np.zeros((1, 1024), f32)], axis=0))

    ncC = build_comb_nc()
    resC = run_bass_kernel_spmd(
        ncC, [host_comb_inputs(logits_all, y_all_pad, x1[b], b) for b in range(8)],
        core_ids=cores)
    x2 = [resC.results[b]["out"] for b in range(8)]

    ncD = build_attn_nc()
    resD = run_bass_kernel_spmd(
        ncD, [host_attn_inputs(x2[b], wD) for b in range(8)], core_ids=cores)
    x3 = [resD.results[b]["out"] for b in range(8)]

    ncE = build_ffn_nc()
    resE = run_bass_kernel_spmd(
        ncE, [host_ffn_inputs(x3[b], wE) for b in range(8)], core_ids=cores)
    out = np.stack([resE.results[b]["out"].T for b in range(8)])
    return np.ascontiguousarray(out, dtype=f32)


# version 18
# speedup vs baseline: 1.0340x; 1.0340x over previous
import numpy as np
import ml_dtypes
import sys as _sys
from contextlib import ExitStack

def _ln_():
    f = _sys._getframe(2)
    return f.f_lineno

import concourse.bass as bass
import concourse.mybir as mybir
from concourse.bass_utils import run_bass_kernel_spmd

F32 = mybir.dt.float32
BF16 = mybir.dt.bfloat16
U32 = mybir.dt.uint32
AF = mybir.ActivationFunctionType
ALU = mybir.AluOpType

D_MODEL = 1024
SEQ = 2048
BATCH = 8
KT = D_MODEL // 128
ENGS = ("sync", "gpsimd", "tensor", "scalar", "vector")


class Prog:

    def __init__(self, ctx, nc):
        self.nc = nc
        self.ctx = ctx
        self.ops = {e: [] for e in ENGS}
        self.log = {}
        self.sems = {}

    def sem(self, name, producer):
        h = self.ctx.enter_context(self.nc.semaphore(name))
        self.sems[name] = [h, 0, producer]
        return name

    def n(self, sname):
        return self.sems[sname][1]

    def emit(self, eng, fn, inc=None, dma=False):
        if inc is not None:
            rec = self.sems[inc]
            assert rec[2] == eng, (inc, rec[2], eng)
            amt = 16 if dma else 1
            rec[1] += amt

            def wrapped(e, fn=fn, h=rec[0], amt=amt):
                fn(e).then_inc(h, amt)
            self.ops[eng].append(wrapped)
            self.log.setdefault(eng, []).append(("inc", inc, amt, _ln_()))
        else:
            self.ops[eng].append(fn)
            self.log.setdefault(eng, []).append(("op", None, None, _ln_()))

    def wait(self, eng, sname, val=None):
        rec = self.sems[sname]
        v = rec[1] if val is None else val
        if v <= 0:
            return

        def w(e, h=rec[0], v=v):
            e.wait_ge(h, v)
        self.ops[eng].append(w)
        self.log.setdefault(eng, []).append(("wait", sname, v, _ln_()))

    def simulate(self):
        pos = {e: 0 for e in ENGS}
        val = {name: 0 for name in self.sems}
        progress = True
        while progress:
            progress = False
            for e in ENGS:
                q = self.log.get(e, [])
                while pos[e] < len(q):
                    ev = q[pos[e]]
                    if ev[0] == "wait" and val[ev[1]] < ev[2]:
                        break
                    if ev[0] == "inc":
                        val[ev[1]] += ev[2]
                    pos[e] += 1
                    progress = True
        stuck = {e: pos[e] for e in ENGS if pos[e] < len(self.log.get(e, []))}
        if stuck:
            msg = []
            for e, p in stuck.items():
                ev = self.log[e][p]
                msg.append(f"{e}@{p}/{len(self.log[e])} line{ev[3]} waiting {ev[1]}>={ev[2]} (now {val[ev[1]]})")
            raise RuntimeError("DEADLOCK: " + "; ".join(msg))

    def build_block(self, block):
        self.simulate()
        for eng in ENGS:
            if not self.ops[eng]:
                continue
            body = self.ops[eng]

            def run(e, body=body):
                for fn in body:
                    fn(e)
            getattr(block, eng)(run)


class PsumPool:

    def __init__(self, P):
        self.P = P
        self.state = {}

    def use(self, bank):
        st = self.state.get(id(bank))
        if st is not None:
            self.P.wait("tensor", st[0], st[1])

    def free(self, bank, sname):
        self.state[id(bank)] = (sname, self.P.n(sname))


def _sb(ctx, nc, name, shape, dtype):
    return ctx.enter_context(nc.sbuf_tensor(name, shape, dtype))


def _ps(ctx, nc, name, shape, dtype=F32):
    return ctx.enter_context(nc.psum_tensor(name, shape, dtype))


def emit_ffn(nc, P, pool, d, pfx="e"):
    x_d = d["x"]
    w1t_d = d["w1t"]
    w2t_d = d["w2t"]
    h1t_d = d["h1t"]
    h2t_d = d["h2t"]
    w3t_d = d["w3t"]
    s1_d = d["s1"]
    s2_d = d["s2"]
    sh1_d = d["sh1"]
    sh2_d = d["sh2"]
    s3_d = d["s3"]
    ones_d = d["ones"]
    ones1_d = d["ones1"]
    out_d = d["out"]

    C = min(1024, SEQ)
    NCH = SEQ // C
    NW = min(512, C)
    NN = C // NW

    w1t_r = w1t_d
    w2t_r = w2t_d
    h1t_r = h1t_d
    h2t_r = h2t_d
    w3t_r = w3t_d
    x_r = x_d.rearrange("(k p) t -> p k t", p=128)
    out_r = out_d.rearrange("(k p) t -> p k t", p=128)


    with ExitStack() as ctx:
        x_bf = _sb(ctx, nc, pfx + "x_bf", [128, KT, C], BF16)
        x_f = _sb(ctx, nc, pfx + "x_f", [128, KT, C], F32)
        wa = [_sb(ctx, nc, pfx + f"wa{i}", [128, KT, 128], BF16) for i in range(2)]
        wb = [_sb(ctx, nc, pfx + f"wb{i}", [128, KT, 128], BF16) for i in range(2)]
        hw = [_sb(ctx, nc, pfx + f"hw{i}", [128, 32, 128], BF16) for i in range(2)]
        hbuf = _sb(ctx, nc, pfx + "hbuf", [128, 32, C], BF16)
        gbuf = _sb(ctx, nc, pfx + "gbuf", [128, 32, C], BF16)
        _t1 = _sb(ctx, nc, pfx + "t1b", [128, NW], F32)
        t1 = [_t1, _t1]
        s1c = _sb(ctx, nc, pfx + "s1c", [128, 32], F32)
        s2c = _sb(ctx, nc, pfx + "s2c", [128, 32], F32)
        sh1c = _sb(ctx, nc, pfx + "sh1c", [128, 32], F32)
        sh2c = _sb(ctx, nc, pfx + "sh2c", [128, 32], F32)
        s3c = _sb(ctx, nc, pfx + "s3c", [128, 8], F32)
        onesc = _sb(ctx, nc, pfx + "onesc", [128, 1], F32)
        ones1c = _sb(ctx, nc, pfx + "ones1c", [1, 128], F32)
        xsq = t1
        srow = _sb(ctx, nc, pfx + "srow", [1, 2, NW], F32)

        pa = [_ps(ctx, nc, pfx + f"pa{i}", [128, NW]) for i in range(2)]
        pb = [_ps(ctx, nc, pfx + f"pb{i}", [128, NW]) for i in range(2)]
        pst = _ps(ctx, nc, pfx + "pst", [1, NW])
        pst2 = _ps(ctx, nc, pfx + "pst2", [1, NW])
        pbc = _ps(ctx, nc, pfx + "pbc", [128, NW])
        pbc2 = _ps(ctx, nc, pfx + "pbc2", [128, NW])
        big = [pa[0], pb[0], pa[1], pb[1]]

        s_lc = P.sem(pfx + "s_lc", "sync")
        s_lx = P.sem(pfx + "s_lx", "sync")
        s_wx = P.sem(pfx + "s_wx", "gpsimd")
        s_wab = [P.sem(pfx + "s_wab0", "gpsimd"), P.sem(pfx + "s_wab1", "gpsimd")]
        s_whw = [P.sem(pfx + "s_whw0", "gpsimd"), P.sem(pfx + "s_whw1", "gpsimd")]
        s_pe = P.sem(pfx + "s_pe", "tensor")
        s_act = P.sem(pfx + "s_act", "scalar")
        s_ve = P.sem(pfx + "s_ve", "vector")
        s_st = P.sem(pfx + "s_st", "sync")

        for nm, dst, src in [("s1", s1c, s1_d), ("s2", s2c, s2_d), ("sh1", sh1c, sh1_d),
                             ("sh2", sh2c, sh2_d), ("s3", s3c, s3_d), ("ones", onesc, ones_d),
                             ("ones1", ones1c, ones1_d)]:
            P.emit("sync", lambda e, dst=dst, src=src: e.dma_start(out=dst[:], in_=src),
                   inc=s_lc, dma=True)
        ld_prolog = P.n(s_lc)
        P.wait("tensor", s_lc, ld_prolog)
        P.wait("scalar", s_lc, ld_prolog)
        P.wait("vector", s_lc, ld_prolog)

        fam_free = {"ab": [], "hw": []}

        fam_sems = {"ab": s_wab, "hw": s_whw}

        def stream_block(family, dmas):
            fam = fam_free[family]
            idx = len(fam)
            if idx >= 2:
                P.wait("gpsimd", *fam[idx - 2])
            sname = fam_sems[family][idx % 2]
            for fn in dmas:
                P.emit("gpsimd", fn, inc=sname, dma=True)
            fam.append(None)
            return idx, (sname, P.n(sname))

        def free_block(family, idx):
            fam_free[family][idx] = (s_pe, P.n(s_pe))

        t1_guard = [0, 0]
        xsq_guard = [0, 0]
        xf_guard = 0

        for c in range(NCH):
            tc = c * C
            if xf_guard:
                P.wait("sync", s_st, xf_guard)
            P.emit("sync", lambda e, tc=tc: e.dma_start(out=x_f[:], in_=x_r[:, :, tc:tc + C]),
                   inc=s_lx, dma=True)
            x_f_ready = P.n(s_lx)
            if c > 0:
                P.wait("gpsimd", *fam_free["ab"][-1])
            P.emit("gpsimd", lambda e, tc=tc: e.dma_start(out=x_bf[:], in_=x_r[:, :, tc:tc + C]),
                   inc=s_wx, dma=True)
            P.wait("tensor", s_wx, P.n(s_wx))

            for j in range(32):
                par = len(fam_free["ab"]) % 2
                idx, (wsem, wv) = stream_block("ab", [
                    lambda e, j=j, par=par: e.dma_start(out=wa[par][:], in_=w1t_r[:, j]),
                    lambda e, j=j, par=par: e.dma_start(out=wb[par][:], in_=w2t_r[:, j]),
                ])
                P.wait("tensor", wsem, wv)
                for n in range(NN):
                    ts = n * NW
                    A, B = pa[par], pb[par]
                    pool.use(A)
                    for k in range(KT):
                        P.emit("tensor", lambda e, k=k, par=par, ts=ts, A=A: e.matmul(
                            A[:], wa[par][:, k, :], x_bf[:, k, ts:ts + NW],
                            start=(k == 0), stop=(k == KT - 1)),
                            inc=(s_pe if k == KT - 1 else None))
                    pe_a = P.n(s_pe)
                    pool.use(B)
                    for k in range(KT):
                        P.emit("tensor", lambda e, k=k, par=par, ts=ts, B=B: e.matmul(
                            B[:], wb[par][:, k, :], x_bf[:, k, ts:ts + NW],
                            start=(k == 0), stop=(k == KT - 1)),
                            inc=(s_pe if k == KT - 1 else None))
                    pe_b = P.n(s_pe)
                    P.wait("scalar", s_pe, pe_a)
                    if t1_guard[0]:
                        P.wait("scalar", s_ve, t1_guard[0])
                    P.emit("scalar", lambda e, j=j, n=n, A=A: e.activation(
                        t1[n][:], A[:], AF.Silu, scale=s1c[:, j:j + 1]), inc=s_act)
                    pool.free(A, s_act)
                    act_t1 = P.n(s_act)
                    P.wait("vector", s_pe, pe_b)
                    P.wait("vector", s_act, act_t1)
                    P.emit("vector", lambda e, j=j, n=n, B=B: e.scalar_tensor_tensor(
                        out=hbuf[:, j, n * NW:(n + 1) * NW], in0=B[:], scalar=s2c[:, j:j + 1],
                        in1=t1[n][:], op0=ALU.mult, op1=ALU.mult), inc=s_ve)
                    t1_guard[0] = P.n(s_ve)
                    pool.free(B, s_ve)
                free_block("ab", idx)

            for phase, (wsrc, sin, sout, scol) in enumerate([
                    (h1t_r, hbuf, gbuf, sh1c), (h2t_r, gbuf, hbuf, sh2c)]):
                if phase == 0:
                    P.wait("tensor", s_ve, P.n(s_ve))
                else:
                    P.wait("tensor", s_act, P.n(s_act))
                for j in range(32):
                    par = len(fam_free["hw"]) % 2
                    idx, (wsem, wv) = stream_block("hw", [
                        lambda e, j=j, par=par, wsrc=wsrc: e.dma_start(
                            out=hw[par][:], in_=wsrc[:, j])])
                    P.wait("tensor", wsem, wv)
                    for n in range(NN):
                        G = big[(j * NN + n) % 4]
                        pool.use(G)
                        for k in range(32):
                            P.emit("tensor", lambda e, k=k, par=par, n=n, G=G, sin=sin: e.matmul(
                                G[:], hw[par][:, k, :], sin[:, k, n * NW:(n + 1) * NW],
                                start=(k == 0), stop=(k == 31)),
                                inc=(s_pe if k == 31 else None))
                        pe_g = P.n(s_pe)
                        P.wait("scalar", s_pe, pe_g)
                        P.emit("scalar", lambda e, j=j, n=n, G=G, sout=sout, scol=scol: e.activation(
                            sout[:, j, n * NW:(n + 1) * NW], G[:], AF.Gelu, scale=scol[:, j:j + 1]),
                            inc=s_act)
                        pool.free(G, s_act)
                    free_block("hw", idx)

            P.wait("tensor", s_act, P.n(s_act))
            P.wait("vector", s_lx, x_f_ready)
            for j3 in range(KT):
                par = len(fam_free["hw"]) % 2
                idx, (wsem, wv) = stream_block("hw", [
                    lambda e, j3=j3, par=par: e.dma_start(
                        out=hw[par][:], in_=w3t_r[:, j3])])
                P.wait("tensor", wsem, wv)
                for n in range(NN):
                    G = big[(j3 * NN + n) % 4]
                    pool.use(G)
                    for k in range(32):
                        P.emit("tensor", lambda e, k=k, par=par, n=n, G=G: e.matmul(
                            G[:], hw[par][:, k, :], hbuf[:, k, n * NW:(n + 1) * NW],
                            start=(k == 0), stop=(k == 31)),
                            inc=(s_pe if k == 31 else None))
                    pe_g = P.n(s_pe)
                    P.wait("vector", s_pe, pe_g)
                    P.emit("vector", lambda e, j3=j3, n=n, G=G: e.scalar_tensor_tensor(
                        out=x_f[:, j3, n * NW:(n + 1) * NW], in0=G[:], scalar=s3c[:, j3:j3 + 1],
                        in1=x_f[:, j3, n * NW:(n + 1) * NW], op0=ALU.mult, op1=ALU.add), inc=s_ve)
                    pool.free(G, s_ve)
                free_block("hw", idx)
            x4_done = P.n(s_ve)

            for n in range(NN):
                ns = n * NW
                P.wait("scalar", s_ve, x4_done)
                pool.use(pst)
                pool.use(pst2)
                P.wait("tensor", s_ve, x4_done)
                for j in range(KT):
                    if xsq_guard[0]:
                        P.wait("scalar", s_pe, xsq_guard[0])
                    P.emit("scalar", lambda e, j=j, n=n, ns=ns: e.activation(
                        xsq[0][:], x_f[:, j, ns:ns + NW], AF.Square), inc=s_act)
                    sqv = P.n(s_act)
                    P.wait("tensor", s_act, sqv)
                    P.emit("tensor", lambda e, j=j, ns=ns: e.matmul(
                        pst[:], onesc[:], x_f[:, j, ns:ns + NW],
                        start=(j == 0), stop=(j == KT - 1), skip_group_check=True),
                        inc=(s_pe if j == KT - 1 else None))
                    P.emit("tensor", lambda e, j=j: e.matmul(
                        pst2[:], onesc[:], xsq[0][:],
                        start=(j == 0), stop=(j == KT - 1), skip_group_check=True), inc=s_pe)
                    xsq_guard[0] = P.n(s_pe)
                pe_stats = P.n(s_pe)
                P.wait("vector", s_pe, pe_stats)
                P.emit("vector", lambda e: e.tensor_scalar(
                    out=srow[:, 0, :], in0=pst[:], scalar1=1.0 / D_MODEL, scalar2=None,
                    op0=ALU.mult), inc=s_ve)
                v_mean = P.n(s_ve)
                P.wait("vector", s_ve, v_mean)
                P.emit("vector", lambda e: e.tensor_tensor(
                    out=srow[:, 1, :], in0=srow[:, 0, :], in1=srow[:, 0, :], op=ALU.mult), inc=s_ve)
                P.wait("vector", s_ve, P.n(s_ve))
                P.emit("vector", lambda e: e.scalar_tensor_tensor(
                    out=srow[:, 1, :], in0=pst2[:], scalar=1.0 / D_MODEL, in1=srow[:, 1, :],
                    op0=ALU.mult, op1=ALU.subtract), inc=s_ve)
                pool.free(pst, s_ve)
                pool.free(pst2, s_ve)
                P.wait("vector", s_ve, P.n(s_ve))
                P.emit("vector", lambda e: e.tensor_scalar(
                    out=srow[:, 1, :], in0=srow[:, 1, :], scalar1=1e-5, scalar2=None,
                    op0=ALU.add), inc=s_ve)
                v_var = P.n(s_ve)
                P.wait("scalar", s_ve, v_var)
                P.emit("scalar", lambda e: e.activation(srow[:, 1, :], srow[:, 1, :],
                                                        AF.Ln), inc=s_act)
                P.wait("scalar", s_act, P.n(s_act))
                P.emit("scalar", lambda e: e.activation(srow[:, 1, :], srow[:, 1, :], AF.Exp,
                                                        scale=-0.5), inc=s_act)
                act_r = P.n(s_act)
                pool.use(pbc)
                P.wait("tensor", s_ve, v_mean)
                P.emit("tensor", lambda e: e.matmul(pbc[:], ones1c[:], srow[:, 0, :],
                                                    start=True, stop=True), inc=s_pe)
                pool.use(pbc2)
                P.wait("tensor", s_act, act_r)
                P.emit("tensor", lambda e: e.matmul(pbc2[:], ones1c[:], srow[:, 1, :],
                                                    start=True, stop=True), inc=s_pe)
                pe_bc = P.n(s_pe)
                P.wait("vector", s_pe, pe_bc)
                P.wait("vector", s_ve, x4_done)
                for j in range(KT):
                    P.emit("vector", lambda e, j=j, ns=ns: e.tensor_tensor(
                        out=x_f[:, j, ns:ns + NW], in0=x_f[:, j, ns:ns + NW], in1=pbc[:],
                        op=ALU.subtract), inc=s_ve)
                    P.wait("vector", s_ve, P.n(s_ve))
                    P.emit("vector", lambda e, j=j, ns=ns: e.tensor_tensor(
                        out=x_f[:, j, ns:ns + NW], in0=x_f[:, j, ns:ns + NW], in1=pbc2[:],
                        op=ALU.mult), inc=s_ve)
                pool.free(pbc, s_ve)
                pool.free(pbc2, s_ve)
                v_out = P.n(s_ve)
                P.wait("sync", s_ve, v_out)
                P.emit("sync", lambda e, tc=tc, ns=ns: e.dma_start(
                    out=out_r[:, :, tc + ns:tc + ns + NW], in_=x_f[:, :, ns:ns + NW]),
                    inc=s_st, dma=True)
            xf_guard = P.n(s_st)

        P.wait("sync", s_st, P.n(s_st))



def build_ffn_nc():
    nc = bass.Bass("TRN2", target_bir_lowering=False)
    d = {}
    d["x"] = nc.dram_tensor("x", [D_MODEL, SEQ], F32, kind="ExternalInput").ap()
    d["w1t"] = nc.dram_tensor("w1t", [128, 32, 8, 128], F32, kind="ExternalInput").ap()
    d["w2t"] = nc.dram_tensor("w2t", [128, 32, 8, 128], F32, kind="ExternalInput").ap()
    d["h1t"] = nc.dram_tensor("h1t", [128, 32, 32, 128], F32, kind="ExternalInput").ap()
    d["h2t"] = nc.dram_tensor("h2t", [128, 32, 32, 128], F32, kind="ExternalInput").ap()
    d["w3t"] = nc.dram_tensor("w3t", [128, 8, 32, 128], F32, kind="ExternalInput").ap()
    d["s1"] = nc.dram_tensor("s1", [128, 32], F32, kind="ExternalInput").ap()
    d["s2"] = nc.dram_tensor("s2", [128, 32], F32, kind="ExternalInput").ap()
    d["sh1"] = nc.dram_tensor("sh1", [128, 32], F32, kind="ExternalInput").ap()
    d["sh2"] = nc.dram_tensor("sh2", [128, 32], F32, kind="ExternalInput").ap()
    d["s3"] = nc.dram_tensor("s3", [128, 8], F32, kind="ExternalInput").ap()
    d["ones"] = nc.dram_tensor("ones", [128, 1], F32, kind="ExternalInput").ap()
    d["ones1"] = nc.dram_tensor("ones1", [1, 128], F32, kind="ExternalInput").ap()
    d["out"] = nc.dram_tensor("out", [D_MODEL, SEQ], F32, kind="ExternalOutput").ap()
    with ExitStack() as ctx:
        block = ctx.enter_context(nc.Block())
        P = Prog(ctx, nc)
        pool = PsumPool(P)
        emit_ffn(nc, P, pool, d)
        P.build_block(block)
    return nc

def _t(a):
    return np.ascontiguousarray(np.asarray(a, np.float32).T)


def _wrl(a):
    K, F = a.shape
    return np.ascontiguousarray(a.reshape(K // 128, 128, F // 512, 512).transpose(1, 2, 0, 3))


def _wrl128(a):
    K, F = a.shape
    return np.ascontiguousarray(a.reshape(K // 128, 128, F // 128, 128).transpose(1, 2, 0, 3))


def _col(s, nt):
    return np.ascontiguousarray(np.asarray(s, np.float32).reshape(nt, 128).T)


def host_ffn_inputs(x_fm_core, w):
    return dict(
        x=np.ascontiguousarray(x_fm_core, dtype=np.float32),
        w1t=_wrl128(_t(w["w1_w"])), w2t=_wrl128(_t(w["w2_w"])), h1t=_wrl128(_t(w["h1_w"])),
        h2t=_wrl128(_t(w["h2_w"])), w3t=_wrl128(_t(w["w3_w"])),
        s1=_col(w["w1_s"], 32), s2=_col(w["w2_s"], 32),
        sh1=_col(w["h1_s"], 32), sh2=_col(w["h2_s"], 32), s3=_col(w["w3_s"], 8),
        ones=np.ones((128, 1), np.float32), ones1=np.ones((1, 128), np.float32),
    )


def emit_attn(nc, P, pool, d, pfx="d"):
    NQT = SEQ // 128
    NQB = SEQ // 512
    C = min(512, SEQ)
    NCH = SEQ // C
    NW = min(512, C)
    NN = C // NW
    x_d = d["x"]
    wqt_d = d["wqt"]
    wkt_d = d["wkt"]
    wvt_d = d["wvt"]
    wot_d = d["wot"]
    maskS_d = d["maskS"]
    maskT_d = d["maskT"]
    ones_d = d["ones"]
    ones1_d = d["ones1"]
    identbf_d = d["identbf"]
    identf_d = d["identf"]
    out_d = d["out"]

    wr = {w.name: w for w in (wqt_d, wkt_d, wvt_d, wot_d)}
    x_r = x_d.rearrange("(k p) t -> p k t", p=128)
    out_r = out_d.rearrange("(k p) t -> p k t", p=128)


    with ExitStack() as ctx:
        x_bf = _sb(ctx, nc, pfx + "x_bf", [128, KT, SEQ], BF16)
        attn_bf = x_bf
        q_bf = _sb(ctx, nc, pfx + "q_bf", [128, KT, SEQ], BF16)
        k_bf = _sb(ctx, nc, pfx + "k_bf", [128, KT, SEQ], BF16)
        v_tm = _sb(ctx, nc, pfx + "v_tm", [128, NQT, 1024], BF16)
        x_f = _sb(ctx, nc, pfx + "x_f", [128, KT, C], F32)
        wbuf = [_sb(ctx, nc, pfx + f"wbuf{i}", [128, KT, 512], BF16) for i in range(2)]
        maskS = _sb(ctx, nc, pfx + "maskS_s", [128, 4, 512], BF16)
        maskT = _sb(ctx, nc, pfx + "maskT_s", [128, 4, 512], BF16)
        identbf = _sb(ctx, nc, pfx + "identbf_s", [128, 128], BF16)
        identf = _sb(ctx, nc, pfx + "identf_s", [128, 128], F32)
        onesc = _sb(ctx, nc, pfx + "onesc", [128, 1], F32)
        ones1c = _sb(ctx, nc, pfx + "ones1c", [1, 128], F32)
        pT = [_sb(ctx, nc, pfx + f"pT{i}", [128, 512], BF16) for i in range(4)]
        dummy = _sb(ctx, nc, pfx + "escratch", [128, 512], BF16)
        mstat = _sb(ctx, nc, pfx + "mstat", [128, 32], F32)
        sstat = _sb(ctx, nc, pfx + "sstat", [128, 32], F32)
        mqv = _sb(ctx, nc, pfx + "mqv", [128, 8], F32)
        fstat = _sb(ctx, nc, pfx + "fstat", [128, 32], F32)
        scol4 = _sb(ctx, nc, pfx + "scol4", [128, 8], F32)
        lns4 = _sb(ctx, nc, pfx + "lns4", [128, 8], F32)
        biasa = _sb(ctx, nc, pfx + "biasa", [128, 8], F32)
        btrow = [_sb(ctx, nc, pfx + f"btrow{i}", [1, 512], F32) for i in range(2)]
        mbx = [_sb(ctx, nc, pfx + f"mbx{i}", [128, 512], F32) for i in range(2)]
        tmpx = [_sb(ctx, nc, pfx + f"tmpx{i}", [128, 512], F32) for i in range(4)]
        xsq = _sb(ctx, nc, pfx + "xsq", [128, 512], F32)
        srow = _sb(ctx, nc, pfx + "srow", [1, 2, 512], F32)

        ps_s = [_ps(ctx, nc, pfx + f"ps_s{i}", [128, 512]) for i in range(4)]
        ps_t = [_ps(ctx, nc, pfx + f"ps_t{i}", [128, 512]) for i in range(2)]
        ps_pv = [_ps(ctx, nc, pfx + f"ps_pv{i}", [128, 512]) for i in range(2)]
        pst, pst2, pbc, pbc2 = ps_s[0], ps_s[1], ps_s[2], ps_s[3]
        big = ps_t + ps_pv

        s_lc = P.sem(pfx + "s_lc", "sync")
        s_lx = P.sem(pfx + "s_lx", "sync")
        s_wx = P.sem(pfx + "s_wx", "gpsimd")
        s_wb = [P.sem(pfx + "s_wb0", "gpsimd"), P.sem(pfx + "s_wb1", "gpsimd")]
        s_pe = P.sem(pfx + "s_pe", "tensor")
        s_act = P.sem(pfx + "s_act", "scalar")
        s_ve = P.sem(pfx + "s_ve", "vector")
        s_st = P.sem(pfx + "s_st", "sync")

        for dst, src in [(identf, identf_d), (onesc, ones_d), (ones1c, ones1_d)]:
            P.emit("sync", lambda e, dst=dst, src=src: e.dma_start(out=dst[:], in_=src),
                   inc=s_lc, dma=True)
        ld_prolog = P.n(s_lc)
        for eng in ("tensor", "scalar", "vector"):
            P.wait(eng, s_lc, ld_prolog)
        for dst, src in [(maskS, maskS_d), (maskT, maskT_d), (identbf, identbf_d)]:
            P.emit("gpsimd", lambda e, dst=dst, src=src: e.dma_start(out=dst[:], in_=src),
                   inc=s_wx, dma=True)
        P.emit("gpsimd", lambda e: e.dma_start(out=x_bf[:], in_=x_r), inc=s_wx, dma=True)
        P.wait("tensor", s_wx, P.n(s_wx))

        wfree = []

        def stream_w(src_r, fh):
            idx = len(wfree)
            par = idx % 2
            if idx >= 2:
                P.wait("gpsimd", *wfree[idx - 2])
            P.emit("gpsimd", lambda e, src_r=src_r, fh=fh, par=par: e.dma_start(
                out=wbuf[par][:], in_=src_r[:, fh]),
                inc=s_wb[par], dma=True)
            wfree.append(None)
            return idx, par, (s_wb[par], P.n(s_wb[par]))

        for wi, wname in enumerate(["wqt", "wkt", "wvt"]):
            for fh in range(2):
                idx, par, (wsem, wv) = stream_w(wr[wname], fh)
                P.wait("tensor", wsem, wv)
                if wi < 2:
                    dst = q_bf if wi == 0 else k_bf
                    for ft in range(4):
                        fg = fh * 4 + ft
                        for n in range(SEQ // 512):
                            G = big[(ft * (SEQ // 512) + n) % 4]
                            pool.use(G)
                            for k in range(KT):
                                P.emit("tensor", lambda e, k=k, par=par, ft=ft, n=n, G=G: e.matmul(
                                    G[:], wbuf[par][:, k, ft * 128:(ft + 1) * 128],
                                    x_bf[:, k, n * 512:(n + 1) * 512],
                                    start=(k == 0), stop=(k == KT - 1)),
                                    inc=(s_pe if k == KT - 1 else None))
                            pe_g = P.n(s_pe)
                            P.wait("scalar", s_pe, pe_g)
                            P.emit("scalar", lambda e, dst=dst, fg=fg, n=n, G=G, sc=(0.125 if wi == 0 else 1.0): e.activation(
                                dst[:, fg, n * 512:(n + 1) * 512], G[:], AF.Copy, scale=sc), inc=s_act)
                            pool.free(G, s_act)
                else:
                    for tt in range(NQT):
                        G = big[tt % 4]
                        pool.use(G)
                        for k in range(KT):
                            P.emit("tensor", lambda e, k=k, par=par, tt=tt, G=G: e.matmul(
                                G[:], x_bf[:, k, tt * 128:(tt + 1) * 128],
                                wbuf[par][:, k, 0:512],
                                start=(k == 0), stop=(k == KT - 1)),
                                inc=(s_pe if k == KT - 1 else None))
                        pe_g = P.n(s_pe)
                        P.wait("scalar", s_pe, pe_g)
                        P.emit("scalar", lambda e, tt=tt, fh=fh, G=G: e.activation(
                            v_tm[:, tt, fh * 512:(fh + 1) * 512], G[:], AF.Copy),
                            inc=s_act)
                        pool.free(G, s_act)
                wfree[idx] = (s_pe, P.n(s_pe))
        qkv_done_act = P.n(s_act)
        P.wait("tensor", s_act, qkv_done_act)

        pt_guard = [0, 0, 0, 0]
        tmp_guard = [0, 0, 0, 0]
        stats_exp_done = [0, 0]
        ln_done = [0, 0]
        biasa_stt = [0, 0]
        bias_tr = [0, 0]
        biasT2_g = [0, 0]
        fexp_done = [0, 0]
        for hb in range(KT):
            for qb in range(NQB):
                PV = ps_pv[qb % 2]
                pool.use(PV)
                nch = qb + 1
                if stats_exp_done[0]:
                    P.wait("vector", s_act, max(stats_exp_done))
                P.emit("vector", lambda e: e.memset(mstat[:], 1e30), inc=s_ve)
                P.emit("vector", lambda e: e.memset(sstat[:], 0.0), inc=s_ve)
                for qi in range(4):
                    qt = qb * 4 + qi
                    for ho2 in range(2):
                        ho = ho2 * 64
                        tp = (ho, 0)
                        sl = (2 * qi + ho2) * nch
                        for kc in range(nch):
                            S = ps_s[(sl + kc) % 4]
                            pool.use(S)
                            last = kc == nch - 1
                            P.emit("tensor", lambda e, S=S, ho=ho, hb=hb, qt=qt, kc=kc, tp=tp, last=last: e.matmul(
                                S[:], q_bf[ho:ho + 64, hb, qt * 128:(qt + 1) * 128].squeeze(),
                                k_bf[ho:ho + 64, hb, kc * 512:(kc + 1) * 512].squeeze(),
                                start=True, stop=not last, tile_position=tp,
                                skip_group_check=True),
                                inc=(None if last else s_pe))
                            if last:
                                P.emit("tensor", lambda e, S=S, qi2=qt % 4: e.matmul(
                                    S[:], identbf[:], maskS[:, qi2, :].squeeze(),
                                    start=False, stop=True, skip_group_check=True), inc=s_pe)
                        pe_s = P.n(s_pe)
                        P.wait("vector", s_pe, pe_s)
                        for kc in range(nch):
                            P.emit("vector", lambda e, kc=kc, sl=sl, c0=ho2 * 16 + qi * 4, nch=nch: e.tensor_reduce(
                                out=mstat[:, c0 + kc:c0 + kc + 1], in_=ps_s[(sl + kc) % 4][:],
                                axis=mybir.AxisListType.X, op=ALU.max, negate=True), inc=s_ve)
                        v_red = P.n(s_ve)
                        P.wait("scalar", s_ve, v_red)
                        for kc in range(nch):
                            P.emit("scalar", lambda e, kc=kc, sl=sl, c0=ho2 * 16 + qi * 4, nch=nch: e.activation(
                                dummy[:], ps_s[(sl + kc) % 4][:], AF.Exp, scale=1.0,
                                bias=mstat[:, c0 + kc:c0 + kc + 1],
                                accum_out=sstat[:, c0 + kc:c0 + kc + 1]), inc=s_act)
                            pool.free(ps_s[(sl + kc) % 4], s_act)
                stats_exp_done[0] = stats_exp_done[1] = P.n(s_act)
                P.wait("vector", s_act, stats_exp_done[0])
                for m0 in range(8):
                    P.emit("vector", lambda e, m0=m0: e.tensor_reduce(
                        out=mqv[:, m0:m0 + 1], in_=mstat[:, m0 * 4:m0 * 4 + 4],
                        axis=mybir.AxisListType.X, op=ALU.min), inc=s_ve)
                P.wait("vector", s_ve, P.n(s_ve))
                P.emit("vector", lambda e: e.tensor_tensor(
                    out=fstat[:].rearrange("p (m k) -> p m k", k=4),
                    in0=mqv[:].unsqueeze(2).broadcast_to([128, 8, 4]),
                    in1=mstat[:].rearrange("p (m k) -> p m k", k=4),
                    op=ALU.subtract), inc=s_ve)
                v_fs = P.n(s_ve)
                P.wait("scalar", s_ve, v_fs)
                P.emit("scalar", lambda e: e.activation(
                    fstat[:], fstat[:], AF.Exp), inc=s_act)
                fexp_done[0] = fexp_done[1] = P.n(s_act)
                P.wait("vector", s_act, fexp_done[0])
                P.emit("vector", lambda e: e.tensor_tensor(
                    out=fstat[:], in0=fstat[:], in1=sstat[:], op=ALU.mult), inc=s_ve)
                P.wait("vector", s_ve, P.n(s_ve))
                for m0 in range(8):
                    P.emit("vector", lambda e, m0=m0: e.tensor_reduce(
                        out=scol4[:, m0:m0 + 1], in_=fstat[:, m0 * 4:m0 * 4 + 4],
                        axis=mybir.AxisListType.X, op=ALU.add), inc=s_ve)
                P.wait("vector", s_ve, P.n(s_ve))
                P.emit("vector", lambda e: e.tensor_scalar(
                    out=scol4[:], in0=scol4[:], scalar1=1e-8, scalar2=None,
                    op0=ALU.add), inc=s_ve)
                v_sum = P.n(s_ve)
                P.wait("scalar", s_ve, v_sum)
                if biasa_stt[0]:
                    P.wait("scalar", s_ve, max(biasa_stt))
                P.emit("scalar", lambda e: e.activation(
                    lns4[:], scol4[:], AF.Ln), inc=s_act)
                ln_done[0] = ln_done[1] = P.n(s_act)
                P.wait("vector", s_act, ln_done[0])
                if bias_tr[0]:
                    P.wait("vector", s_pe, max(bias_tr))
                P.emit("vector", lambda e: e.scalar_tensor_tensor(
                    out=biasa[:], in0=lns4[:], scalar=-1.0,
                    in1=mqv[:], op0=ALU.mult, op1=ALU.add), inc=s_ve)
                biasa_stt[0] = biasa_stt[1] = P.n(s_ve)
                P.wait("tensor", s_ve, biasa_stt[0])
                for ho2 in range(2):
                    T0 = ps_t[ho2]
                    pool.use(T0)
                    if biasT2_g[ho2]:
                        P.wait("vector", s_pe, biasT2_g[ho2])
                    for qi in range(4):
                        P.emit("tensor", lambda e, m0=ho2 * 4 + qi, qi=qi, T0=T0: e.transpose(
                            T0[0:1, qi * 128:(qi + 1) * 128], biasa[:, m0:m0 + 1], identf[:]),
                            inc=(s_pe if qi == 3 else None))
                    bias_tr[ho2] = P.n(s_pe)
                    P.wait("vector", s_pe, bias_tr[ho2])
                    P.emit("vector", lambda e, ho2=ho2, T0=T0: e.tensor_copy(
                        btrow[ho2][:], T0[0:1, 0:512]), inc=s_ve)
                    pool.free(T0, s_ve)
                for ho2 in range(2):
                    MB = ps_t[ho2]
                    pool.use(MB)
                    P.wait("tensor", s_ve, P.n(s_ve))
                    P.emit("tensor", lambda e, ho2=ho2, MB=MB: e.matmul(
                        MB[:], ones1c[:], btrow[ho2][:],
                        start=True, stop=True, skip_group_check=True), inc=s_pe)
                    biasT2_g[ho2] = P.n(s_pe)
                    P.wait("vector", s_pe, biasT2_g[ho2])
                    P.emit("vector", lambda e, ho2=ho2, MB=MB: e.tensor_copy(
                        mbx[ho2][:], MB[:]), inc=s_ve)
                    pool.free(MB, s_ve)
                P.wait("vector", s_ve, P.n(s_ve))
                Tbanks = [ps_t[0], ps_t[1], ps_s[0], ps_s[1]]
                pend = []

                def emit_pv(ktt, ho2, par, act_pt, PV=PV, hb=hb, qb=qb):
                    ho = ho2 * 64
                    h = 2 * hb + ho2
                    P.wait("tensor", s_act, act_pt)
                    P.emit("tensor", lambda e, PV=PV, ho=ho, h=h, ktt=ktt, par=par, qb=qb: e.matmul(
                        PV[ho:ho + 64, :], v_tm[:, ktt, h * 64:h * 64 + 64],
                        pT[par][:], start=(ktt == 0), stop=(ktt == 4 * qb + 3),
                        tile_position=(0, ho), skip_group_check=True), inc=s_pe)
                    pt_guard[par] = P.n(s_pe)

                for ktt in range(4 * qb + 4):
                    for ho2 in range(2):
                        ho = ho2 * 64
                        tp = (ho, 0)
                        par = (2 * ktt + ho2) % 4
                        T = Tbanks[par]
                        pool.use(T)
                        diag = ktt >= 4 * qb
                        P.emit("tensor", lambda e, T=T, ho=ho, hb=hb, ktt=ktt, qb=qb, tp=tp, diag=diag: e.matmul(
                            T[:], k_bf[ho:ho + 64, hb, ktt * 128:(ktt + 1) * 128].squeeze(),
                            q_bf[ho:ho + 64, hb, qb * 512:(qb + 1) * 512].squeeze(),
                            start=True, stop=not diag, tile_position=tp,
                            skip_group_check=True), inc=(None if diag else s_pe))
                        if diag:
                            P.emit("tensor", lambda e, T=T, r=ktt - 4 * qb: e.matmul(
                                T[:], identbf[:], maskT[:, r, :].squeeze(),
                                start=False, stop=True, skip_group_check=True), inc=s_pe)
                        pe_t = P.n(s_pe)
                        P.wait("vector", s_pe, pe_t)
                        if tmp_guard[par]:
                            P.wait("vector", s_act, tmp_guard[par])
                        P.emit("vector", lambda e, T=T, par=par, ho2=ho2: e.tensor_tensor(
                            out=tmpx[par][:], in0=T[:], in1=mbx[ho2][:], op=ALU.add), inc=s_ve)
                        pool.free(T, s_ve)
                        v_tmp = P.n(s_ve)
                        P.wait("scalar", s_ve, v_tmp)
                        if pt_guard[par]:
                            P.wait("scalar", s_pe, pt_guard[par])
                        P.emit("scalar", lambda e, par=par: e.activation(
                            pT[par][:], tmpx[par][:], AF.Exp, scale=1.0), inc=s_act)
                        tmp_guard[par] = P.n(s_act)
                        pend.append((ktt, ho2, par, P.n(s_act)))
                        if len(pend) > 2:
                            emit_pv(*pend.pop(0))
                while pend:
                    emit_pv(*pend.pop(0))
                pe_pv = P.n(s_pe)
                P.wait("scalar", s_pe, pe_pv)
                P.emit("scalar", lambda e, PV=PV, hb=hb, qb=qb: e.activation(
                    attn_bf[:, hb, qb * 512:(qb + 1) * 512], PV[:], AF.Copy),
                    inc=s_act)
                pool.free(PV, s_act)
        attn_done_act = P.n(s_act)
        P.wait("tensor", s_act, attn_done_act)

        xf_guard = 0
        xsq_guard = [0]
        for c in range(NCH):
            tc = c * C
            if xf_guard:
                P.wait("sync", s_st, xf_guard)
            P.emit("sync", lambda e, tc=tc: e.dma_start(out=x_f[:], in_=x_r[:, :, tc:tc + C]),
                   inc=s_lx, dma=True)
            x_f_ready = P.n(s_lx)
            P.wait("vector", s_lx, x_f_ready)
            for fh in range(2):
                idx, par, (wsem, wv) = stream_w(wr["wot"], fh)
                P.wait("tensor", wsem, wv)
                for ft in range(4):
                    j3 = fh * 4 + ft
                    for n in range(NN):
                        G = big[(ft * NN + n) % 4]
                        pool.use(G)
                        for k in range(KT):
                            P.emit("tensor", lambda e, k=k, par=par, ft=ft, n=n, G=G, tc=tc: e.matmul(
                                G[:], wbuf[par][:, k, ft * 128:(ft + 1) * 128],
                                attn_bf[:, k, tc + n * NW:tc + (n + 1) * NW],
                                start=(k == 0), stop=(k == KT - 1)),
                                inc=(s_pe if k == KT - 1 else None))
                        pe_g = P.n(s_pe)
                        P.wait("vector", s_pe, pe_g)
                        P.emit("vector", lambda e, j3=j3, n=n, G=G: e.tensor_tensor(
                            out=x_f[:, j3, n * NW:(n + 1) * NW], in0=G[:],
                            in1=x_f[:, j3, n * NW:(n + 1) * NW], op=ALU.add), inc=s_ve)
                        pool.free(G, s_ve)
                wfree[idx] = (s_pe, P.n(s_pe))
            x4_done = P.n(s_ve)

            for n in range(NN):
                ns = n * NW
                P.wait("scalar", s_ve, x4_done)
                pool.use(pst)
                pool.use(pst2)
                P.wait("tensor", s_ve, x4_done)
                for j in range(KT):
                    if xsq_guard[0]:
                        P.wait("scalar", s_pe, xsq_guard[0])
                    P.emit("scalar", lambda e, j=j, ns=ns: e.activation(
                        xsq[:], x_f[:, j, ns:ns + NW], AF.Square), inc=s_act)
                    sqv = P.n(s_act)
                    P.wait("tensor", s_act, sqv)
                    P.emit("tensor", lambda e, j=j, ns=ns: e.matmul(
                        pst[:1, :NW], onesc[:], x_f[:, j, ns:ns + NW],
                        start=(j == 0), stop=(j == KT - 1), skip_group_check=True),
                        inc=(s_pe if j == KT - 1 else None))
                    P.emit("tensor", lambda e, j=j: e.matmul(
                        pst2[:1, :NW], onesc[:], xsq[:, :NW],
                        start=(j == 0), stop=(j == KT - 1), skip_group_check=True), inc=s_pe)
                    xsq_guard[0] = P.n(s_pe)
                pe_stats = P.n(s_pe)
                P.wait("vector", s_pe, pe_stats)
                P.emit("vector", lambda e: e.tensor_scalar(
                    out=srow[:, 0, :], in0=pst[:1, :NW], scalar1=1.0 / D_MODEL, scalar2=None,
                    op0=ALU.mult), inc=s_ve)
                v_mean = P.n(s_ve)
                P.wait("vector", s_ve, v_mean)
                P.emit("vector", lambda e: e.tensor_tensor(
                    out=srow[:, 1, :], in0=srow[:, 0, :], in1=srow[:, 0, :], op=ALU.mult), inc=s_ve)
                P.wait("vector", s_ve, P.n(s_ve))
                P.emit("vector", lambda e: e.scalar_tensor_tensor(
                    out=srow[:, 1, :], in0=pst2[:1, :NW], scalar=1.0 / D_MODEL, in1=srow[:, 1, :],
                    op0=ALU.mult, op1=ALU.subtract), inc=s_ve)
                pool.free(pst, s_ve)
                pool.free(pst2, s_ve)
                P.wait("vector", s_ve, P.n(s_ve))
                P.emit("vector", lambda e: e.tensor_scalar(
                    out=srow[:, 1, :], in0=srow[:, 1, :], scalar1=1e-5, scalar2=None,
                    op0=ALU.add), inc=s_ve)
                v_var = P.n(s_ve)
                P.wait("scalar", s_ve, v_var)
                P.emit("scalar", lambda e: e.activation(srow[:, 1, :], srow[:, 1, :],
                                                        AF.Ln), inc=s_act)
                P.wait("scalar", s_act, P.n(s_act))
                P.emit("scalar", lambda e: e.activation(srow[:, 1, :], srow[:, 1, :],
                                                        AF.Exp, scale=-0.5), inc=s_act)
                act_r = P.n(s_act)
                pool.use(pbc)
                P.wait("tensor", s_ve, v_mean)
                P.emit("tensor", lambda e: e.matmul(pbc[:, :NW], ones1c[:], srow[:, 0, :],
                                                    start=True, stop=True,
                                                    skip_group_check=True))
                pool.use(pbc2)
                P.wait("tensor", s_act, act_r)
                P.emit("tensor", lambda e: e.matmul(pbc2[:, :NW], ones1c[:], srow[:, 1, :],
                                                    start=True, stop=True,
                                                    skip_group_check=True), inc=s_pe)
                pe_bc = P.n(s_pe)
                P.wait("vector", s_pe, pe_bc)
                P.wait("vector", s_ve, x4_done)
                for j in range(KT):
                    P.emit("vector", lambda e, j=j, ns=ns: e.tensor_tensor(
                        out=x_f[:, j, ns:ns + NW], in0=x_f[:, j, ns:ns + NW], in1=pbc[:, :NW],
                        op=ALU.subtract), inc=s_ve)
                    P.wait("vector", s_ve, P.n(s_ve))
                    P.emit("vector", lambda e, j=j, ns=ns: e.tensor_tensor(
                        out=x_f[:, j, ns:ns + NW], in0=x_f[:, j, ns:ns + NW], in1=pbc2[:, :NW],
                        op=ALU.mult), inc=s_ve)
                pool.free(pbc, s_ve)
                pool.free(pbc2, s_ve)
                v_out = P.n(s_ve)
                P.wait("sync", s_ve, v_out)
                P.emit("sync", lambda e, tc=tc, ns=ns: e.dma_start(
                    out=out_r[:, :, tc + ns:tc + ns + NW], in_=x_f[:, :, ns:ns + NW]),
                    inc=s_st, dma=True)
            xf_guard = P.n(s_st)

        P.wait("sync", s_st, P.n(s_st))



def build_attn_nc(dbg=False):
    nc = bass.Bass("TRN2", target_bir_lowering=False)
    d = {}
    if dbg:
        d["dbgb"] = nc.dram_tensor("dbgb", [128, 8, 4, 2, 4], F32, kind="ExternalOutput").ap()
        d["dbgc"] = nc.dram_tensor("dbgc", [128, 8, 4, 2, 4], F32, kind="ExternalOutput").ap()
        d["dbgd"] = nc.dram_tensor("dbgd", [128, 8, 4, 2, 4], F32, kind="ExternalOutput").ap()
        d["dbge"] = nc.dram_tensor("dbge", [128, 8, 4, 2, 16], F32, kind="ExternalOutput").ap()
    d["x"] = nc.dram_tensor("x", [D_MODEL, SEQ], F32, kind="ExternalInput").ap()
    d["wqt"] = nc.dram_tensor("wqt", [128, 2, 8, 512], F32, kind="ExternalInput").ap()
    d["wkt"] = nc.dram_tensor("wkt", [128, 2, 8, 512], F32, kind="ExternalInput").ap()
    d["wvt"] = nc.dram_tensor("wvt", [128, 2, 8, 512], F32, kind="ExternalInput").ap()
    d["wot"] = nc.dram_tensor("wot", [128, 2, 8, 512], F32, kind="ExternalInput").ap()
    d["maskS"] = nc.dram_tensor("maskS", [128, 4, 512], F32, kind="ExternalInput").ap()
    d["maskT"] = nc.dram_tensor("maskT", [128, 4, 512], F32, kind="ExternalInput").ap()
    d["ones"] = nc.dram_tensor("ones", [128, 1], F32, kind="ExternalInput").ap()
    d["ones1"] = nc.dram_tensor("ones1", [1, 128], F32, kind="ExternalInput").ap()
    d["identbf"] = nc.dram_tensor("identbf", [128, 128], F32, kind="ExternalInput").ap()
    d["identf"] = nc.dram_tensor("identf", [128, 128], F32, kind="ExternalInput").ap()
    d["out"] = nc.dram_tensor("out", [D_MODEL, SEQ], F32, kind="ExternalOutput").ap()
    with ExitStack() as ctx:
        block = ctx.enter_context(nc.Block())
        P = Prog(ctx, nc)
        pool = PsumPool(P)
        emit_attn(nc, P, pool, d)
        P.build_block(block)
    return nc

def host_attn_masks(seq):
    i = np.arange(128)
    tri_up = (i[None, :] > i[:, None]).astype(np.float32) * -1e30
    tri_lo = (i[:, None] > i[None, :]).astype(np.float32) * -1e30
    full = np.full((128, 128), -1e30, np.float32)
    zero = np.zeros((128, 128), np.float32)
    maskS = np.zeros((128, 4, 512), np.float32)
    maskT = np.zeros((128, 4, 512), np.float32)
    for s in range(4):
        for r in range(4):
            maskS[:, s, r * 128:(r + 1) * 128] = zero if r < s else (tri_up if r == s else full)
            maskT[:, r, s * 128:(s + 1) * 128] = full if s < r else (tri_lo if s == r else zero)
    return maskS, maskT


def host_attn_inputs(x_fm_core, w):
    maskS, maskT = host_attn_masks(SEQ)
    return dict(
        x=np.ascontiguousarray(x_fm_core, dtype=np.float32),
        wqt=_wrl(_t(w["wq"])), wkt=_wrl(_t(w["wk"])), wvt=_wrl(_t(w["wv"])), wot=_wrl(_t(w["wo"])),
        maskS=maskS, maskT=maskT,
        ones=np.ones((128, 1), np.float32), ones1=np.ones((1, 128), np.float32),
        identbf=np.eye(128, dtype=np.float32), identf=np.eye(128, dtype=np.float32),
    )
def emit_mamba(nc, P, pool, d, pfx="a"):
    DI = 2048
    NH = 32
    NP = 16
    CD = 2176
    C = 512
    NCH = SEQ // C
    NQ = C // 128
    RMS_EPS = 1.1920929e-07

    x_d = d["x"]
    wzt_d = d["wzt"]
    wxt_d = d["wxt"]
    wdt_d = d["wdt"]
    wot_d = d["wot_m"]
    normw_d = d["normw"]
    gnw_d = d["gnw"]
    convw_d = d["convw"]
    convb_d = d["convb"]
    drep_d = d["drep"]
    dtb_d = d["dtb"]
    alog_d = d["alog"]
    rw1t_d = d["rw1t"]
    rb1_d = d["rb1"]
    rw2t_d = d["rw2t"]
    rb2_d = d["rb2"]
    rwt_d = d["rwt"]
    rbr_d = d["rbr"]
    ones_d = d["ones"]
    ones1_d = d["ones1"]
    identf_d = d["identf"]
    identb_d = d["identb"]
    triu_d = d["triu"]
    negblk_d = d["negblk"]
    maskd_d = d["maskd"]
    out_d = d["out"]
    xtm_d = d["xtm"]
    lg_d = d["logits"]
    cbounce = d["cbounce"]

    x_r = x_d.rearrange("(k p) t -> p k t", p=128)
    out_r = out_d.rearrange("(k p) t -> p k t", p=128)
    wzt_r = wzt_d
    wxt_r = wxt_d
    wdt_r = wdt_d.rearrange("(k p) f -> p k f", p=128)
    wot_r = wot_d


    with ExitStack() as ctx:
        xnb = _sb(ctx, nc, pfx + "xnb", [128, KT, SEQ], BF16)
        rstd_b = _sb(ctx, nc, pfx + "rstd_b", [128, SEQ], BF16)
        x_c = _sb(ctx, nc, pfx + "x_c", [128, KT, C], F32)
        xBCc = _sb(ctx, nc, pfx + "xBCc", [128, 17, C + 3], BF16)
        xbc = _sb(ctx, nc, pfx + "xbc", [128, 17, C], BF16)
        xh_tm = _sb(ctx, nc, pfx + "xh_tm", [128, NQ, DI], BF16)
        ycb = _sb(ctx, nc, pfx + "ycb", [128, NP, C], BF16)
        x1c = _sb(ctx, nc, pfx + "x1c", [128, KT, C], F32)
        wsb = [_sb(ctx, nc, pfx + f"wsb{i}", [128, 16, 128], BF16) for i in range(2)]
        wdtb = _sb(ctx, nc, pfx + "wdtb", [128, KT, 32], BF16)
        dt_tm = _sb(ctx, nc, pfx + "dt_tm", [128, NQ, 32], F32)
        W_all = _sb(ctx, nc, pfx + "W_all", [128, NH, 128], BF16)
        EbAll = _sb(ctx, nc, pfx + "EbAll", [128, NP, 128], BF16)
        CbAll = _sb(ctx, nc, pfx + "CbAll", [128, NP, 128], F32)
        H_all = _sb(ctx, nc, pfx + "H_all", [128, NP, 64], F32)
        E_q = _sb(ctx, nc, pfx + "E_q", [128, 1024], BF16)
        G_sb = _sb(ctx, nc, pfx + "G_sb", [128, 128], F32)
        cum_tm = _sb(ctx, nc, pfx + "cum_tm", [128, 32], F32)
        cumfmq = _sb(ctx, nc, pfx + "cumfmq", [8, 4, 128], F32)
        cumflat = _sb(ctx, nc, pfx + "cumflat", [1, 2048], F32)
        dA = _sb(ctx, nc, pfx + "dA", [128, 32], F32)
        totr = _sb(ctx, nc, pfx + "totr", [1, 32], F32)
        tot_b = _sb(ctx, nc, pfx + "tot_b", [128, 32], F32)
        etot_b = _sb(ctx, nc, pfx + "etot_b", [128, 32], F32)
        etot_p = _sb(ctx, nc, pfx + "etot_p", [128, NP], F32)
        w2a = _sb(ctx, nc, pfx + "w2a", [128, 32], F32)
        dtw2 = _sb(ctx, nc, pfx + "dtw2", [128, 32], F32)
        XdtA = _sb(ctx, nc, pfx + "XdtA", [128, 1024], BF16)
        XwA = _sb(ctx, nc, pfx + "XwA", [128, 1024], BF16)
        BCtm = _sb(ctx, nc, pfx + "BCtm", [128, 4, 128], BF16)
        Clo4 = _sb(ctx, nc, pfx + "Clo4", [64, 4, 128], BF16)
        sp1 = _sb(ctx, nc, pfx + "sp1", [128, 32], F32)
        sp2 = _sb(ctx, nc, pfx + "sp2", [128, 32], F32)
        zsil = _sb(ctx, nc, pfx + "zsil", [128, C], F32)
        xsqa = _sb(ctx, nc, pfx + "xsqa", [128, 512], F32)
        xnt = _sb(ctx, nc, pfx + "xnt", [128, 512], F32)
        meanacc = _sb(ctx, nc, pfx + "meanacc", [128, KT], F32)
        hrt = _sb(ctx, nc, pfx + "hrt", [32, 1], F32)
        prb = _sb(ctx, nc, pfx + "prb", [1, 8], F32)
        pfc = _sb(ctx, nc, pfx + "pfc", [128, 1], F32)
        pmc = _sb(ctx, nc, pfx + "pmc", [128, 1], F32)
        srow = _sb(ctx, nc, pfx + "srow_m", [1, 2, 512], F32)
        lgb = _sb(ctx, nc, pfx + "lgb", [128, NQ, 8], F32)
        normw = _sb(ctx, nc, pfx + "normw_s", [128, KT], F32)
        gnw = _sb(ctx, nc, pfx + "gnw_s", [128, NP], F32)
        convw = _sb(ctx, nc, pfx + "convw_s", [128, 17, 4], F32)
        convb = _sb(ctx, nc, pfx + "convb_s", [128, 17], F32)
        drep = _sb(ctx, nc, pfx + "drep_s", [128, NP], F32)
        dtbr = _sb(ctx, nc, pfx + "dtbr_s", [1, 32], F32)
        alogb = _sb(ctx, nc, pfx + "alog_s", [128, 32], F32)
        aneg = _sb(ctx, nc, pfx + "aneg_s", [128, 32], F32)
        rw1t = _sb(ctx, nc, pfx + "rw1t_s", [128, KT, 32], F32)
        rb1c = _sb(ctx, nc, pfx + "rb1_s", [32, 1], F32)
        rw2t = _sb(ctx, nc, pfx + "rw2t_s", [32, 3], F32)
        rb2r = _sb(ctx, nc, pfx + "rb2_s", [1, 3], F32)
        rwt = _sb(ctx, nc, pfx + "rwt_s", [128, KT, 8], F32)
        rbr = _sb(ctx, nc, pfx + "rbr_s", [1, 8], F32)
        onesc = _sb(ctx, nc, pfx + "onesc_m", [128, 1], F32)
        ones1c = _sb(ctx, nc, pfx + "ones1c_m", [1, 128], F32)
        identf = _sb(ctx, nc, pfx + "identf_m", [128, 128], F32)
        triu = _sb(ctx, nc, pfx + "triu_m", [128, 128], F32)
        negblk = _sb(ctx, nc, pfx + "negblk_m", [8, 1024], F32)
        maskd = _sb(ctx, nc, pfx + "maskd_m", [128, 512], F32)

        T1b = _ps(ctx, nc, pfx + "T1b", [128, 1024])
        T1m = _ps(ctx, nc, pfx + "T1m", [128, 1024])
        gm = _ps(ctx, nc, pfx + "gm", [128, 512])
        io = _ps(ctx, nc, pfx + "io", [128, 512])
        py = _ps(ctx, nc, pfx + "py", [128, 128])
        pu = _ps(ctx, nc, pfx + "pu", [128, 64])
        T1b_h = [T1b[:, 0:512], T1b[:, 512:1024]]
        T1m_h = [T1m[:, 0:512], T1m[:, 512:1024]]

        s_lc = P.sem(pfx + "s_lc", "sync")
        s_lx = P.sem(pfx + "s_lx", "sync")
        s_wx = P.sem(pfx + "s_wx", "gpsimd")
        s_ws = [P.sem(pfx + "s_ws0", "gpsimd"), P.sem(pfx + "s_ws1", "gpsimd")]
        s_cb = [P.sem(pfx + "s_cb0", "sync"), P.sem(pfx + "s_cb1", "sync")]
        s_tp = P.sem(pfx + "s_tp", "scalar")
        s_pe = P.sem(pfx + "s_pe", "tensor")
        s_act = P.sem(pfx + "s_act", "scalar")
        s_ve = P.sem(pfx + "s_ve", "vector")
        s_st = P.sem(pfx + "s_st", "sync")

        def chain(eng):
            P.wait(eng, {"tensor": s_pe, "scalar": s_act, "vector": s_ve}[eng])

        for dst, src in [(normw, normw_d), (gnw, gnw_d), (convw, convw_d), (convb, convb_d),
                         (drep, drep_d), (dtbr, dtb_d), (alogb, alog_d), (rw1t, rw1t_d),
                         (rb1c, rb1_d), (rw2t, rw2t_d), (rb2r, rb2_d), (rwt, rwt_d),
                         (rbr, rbr_d), (onesc, ones_d), (ones1c, ones1_d), (identf, identf_d),
                         (triu, triu_d), (negblk, negblk_d), (maskd, maskd_d)]:
            P.emit("sync", lambda e, dst=dst, src=src: e.dma_start(out=dst[:], in_=src),
                   inc=s_lc, dma=True)
        ld_prolog = P.n(s_lc)
        for eng in ("tensor", "scalar", "vector"):
            P.wait(eng, s_lc, ld_prolog)
        P.emit("gpsimd", lambda e: e.dma_start(out=wdtb[:], in_=wdt_r), inc=s_wx, dma=True)
        P.wait("tensor", s_wx, P.n(s_wx))
        P.emit("scalar", lambda e: e.activation(aneg[:], alogb[:], AF.Exp, scale=1.0), inc=s_act)
        P.wait("vector", s_act, P.n(s_act))
        P.emit("vector", lambda e: e.tensor_scalar(out=aneg[:], in0=aneg[:], scalar1=-1.0,
                                                   scalar2=None, op0=ALU.mult), inc=s_ve)
        P.emit("vector", lambda e: e.memset(H_all[:], 0.0), inc=s_ve)
        P.emit("vector", lambda e: e.memset(xBCc[:, :, 0:3], 0.0), inc=s_ve)
        P.emit("vector", lambda e: e.memset(meanacc[:], 0.0), inc=s_ve)

        xsq_guard = [0]
        p1_guard = [0, 0]
        for cp in range(NCH):
            tc = cp * C
            if p1_guard[0]:
                P.wait("sync", s_act, p1_guard[0])
                P.wait("sync", s_ve, p1_guard[1])
            P.emit("sync", lambda e, tc=tc: e.dma_start(out=x_c[:], in_=x_r[:, :, tc:tc + C]),
                   inc=s_lx, dma=True)
            xrdy = P.n(s_lx)
            P.wait("scalar", s_lx, xrdy)
            P.wait("tensor", s_lx, xrdy)
            P.wait("vector", s_lx, xrdy)
            pool.use(gm)
            p1bufs = [xsqa, zsil]
            p1pend = []

            def emit_p1(k, a_sq):
                P.wait("tensor", s_act, a_sq)
                P.emit("tensor", lambda e, k=k: e.matmul(
                    gm[0:1, :], onesc[:], p1bufs[k % 2][:], start=(k == 0),
                    stop=(k == KT - 1), skip_group_check=True), inc=s_pe)
                p1cnt[k] = P.n(s_pe)
                xsq_guard[0] = P.n(s_pe)

            p1cnt = {}
            for k in range(KT):
                if k >= 2:
                    P.wait("scalar", s_pe, p1cnt[k - 2])
                elif xsq_guard[0]:
                    P.wait("scalar", s_pe, xsq_guard[0])
                P.emit("scalar", lambda e, k=k: e.activation(
                    p1bufs[k % 2][:], x_c[:, k, :], AF.Square), inc=s_act)
                p1pend.append((k, P.n(s_act)))
                if len(p1pend) > 1:
                    emit_p1(*p1pend.pop(0))
            while p1pend:
                emit_p1(*p1pend.pop(0))
            pe_ss = P.n(s_pe)
            P.wait("vector", s_pe, pe_ss)
            P.emit("vector", lambda e: e.tensor_scalar(
                out=srow[:, 0, :], in0=gm[0:1, :], scalar1=1.0 / D_MODEL, scalar2=RMS_EPS,
                op0=ALU.mult, op1=ALU.add), inc=s_ve)
            pool.free(gm, s_ve)
            v_ms = P.n(s_ve)
            P.wait("scalar", s_ve, v_ms)
            P.emit("scalar", lambda e: e.activation(srow[:, 1, :], srow[:, 0, :], AF.Ln), inc=s_act)
            chain("scalar")
            P.emit("scalar", lambda e: e.activation(srow[:, 1, :], srow[:, 1, :], AF.Exp,
                                                    scale=-0.5), inc=s_act)
            a_rs = P.n(s_act)
            pool.use(io)
            P.wait("tensor", s_act, a_rs)
            P.emit("tensor", lambda e: e.matmul(io[:], ones1c[:], srow[:, 1, :],
                                                start=True, stop=True), inc=s_pe)
            pe_bc = P.n(s_pe)
            P.wait("vector", s_pe, pe_bc)
            P.emit("vector", lambda e, tc=tc: e.tensor_copy(rstd_b[:, tc:tc + C], io[:]), inc=s_ve)
            pool.free(io, s_ve)
            chain("vector")
            for k in range(KT):
                P.emit("vector", lambda e, k=k, tc=tc: e.scalar_tensor_tensor(
                    out=xnt[:], in0=x_c[:, k, :], scalar=normw[:, k:k + 1],
                    in1=rstd_b[:, tc:tc + C], op0=ALU.mult, op1=ALU.mult), inc=s_ve)
                chain("vector")
                P.emit("vector", lambda e, k=k: e.tensor_reduce(
                    out=sp1[:, 0:1], in_=xnt[:], axis=mybir.AxisListType.X, op=ALU.add),
                    inc=s_ve)
                chain("vector")
                P.emit("vector", lambda e, k=k: e.tensor_tensor(
                    out=meanacc[:, k:k + 1], in0=meanacc[:, k:k + 1], in1=sp1[:, 0:1],
                    op=ALU.add), inc=s_ve)
                P.emit("vector", lambda e, k=k, tc=tc: e.tensor_copy(
                    xnb[:, k, tc:tc + C], xnt[:]), inc=s_ve)
                chain("vector")
            p1_guard = [P.n(s_act), P.n(s_ve)]
        P.emit("vector", lambda e: e.tensor_scalar(
            out=meanacc[:], in0=meanacc[:], scalar1=1.0 / SEQ, scalar2=None, op0=ALU.mult),
            inc=s_ve)
        v_mc = P.n(s_ve)
        pool.use(gm)
        P.wait("tensor", s_ve, v_mc)
        for k in range(KT):
            P.emit("tensor", lambda e, k=k: e.matmul(
                gm[0:32, 0:1], rw1t[:, k, :], meanacc[:, k:k + 1],
                start=(k == 0), stop=(k == KT - 1), skip_group_check=True),
                inc=(s_pe if k == KT - 1 else None))
        pe_h = P.n(s_pe)
        P.wait("vector", s_pe, pe_h)
        P.emit("vector", lambda e: e.tensor_tensor(out=hrt[:], in0=gm[0:32, 0:1], in1=rb1c[:],
                                                   op=ALU.add), inc=s_ve)
        pool.free(gm, s_ve)
        v_h = P.n(s_ve)
        P.wait("scalar", s_ve, v_h)
        P.emit("scalar", lambda e: e.activation(hrt[:], hrt[:], AF.Silu), inc=s_act)
        a_h = P.n(s_act)
        pool.use(gm)
        P.wait("tensor", s_act, a_h)
        P.emit("tensor", lambda e: e.matmul(gm[0:1, 0:3], hrt[:], rw2t[:],
                                            start=True, stop=True), inc=s_pe)
        pe_l3 = P.n(s_pe)
        P.wait("vector", s_pe, pe_l3)
        P.emit("vector", lambda e: e.tensor_tensor(out=prb[:, 0:3], in0=gm[0:1, 0:3],
                                                   in1=rb2r[:], op=ALU.add), inc=s_ve)
        pool.free(gm, s_ve)
        chain("vector")
        P.emit("vector", lambda e: e.tensor_reduce(out=prb[:, 3:4], in_=prb[:, 0:3],
                                                   axis=mybir.AxisListType.X, op=ALU.max), inc=s_ve)
        chain("vector")
        P.emit("vector", lambda e: e.tensor_scalar(out=prb[:, 3:4], in0=prb[:, 3:4],
                                                   scalar1=-1.0, scalar2=None, op0=ALU.mult),
               inc=s_ve)
        v_nm3 = P.n(s_ve)
        P.wait("scalar", s_ve, v_nm3)
        P.emit("scalar", lambda e: e.activation(prb[:, 0:3], prb[:, 0:3], AF.Exp,
                                                bias=prb[:, 3:4]), inc=s_act)
        a_e3 = P.n(s_act)
        P.wait("vector", s_act, a_e3)
        P.emit("vector", lambda e: e.tensor_reduce(out=prb[:, 4:5], in_=prb[:, 0:3],
                                                   axis=mybir.AxisListType.X, op=ALU.add), inc=s_ve)
        chain("vector")
        P.emit("vector", lambda e: e.reciprocal(prb[:, 4:5], prb[:, 4:5]), inc=s_ve)
        chain("vector")
        P.emit("vector", lambda e: e.tensor_scalar(
            out=prb[:, 0:3], in0=prb[:, 0:3], scalar1=prb[:, 4:5], scalar2=None,
            op0=ALU.mult), inc=s_ve)
        chain("vector")
        P.emit("vector", lambda e: e.tensor_tensor(out=prb[:, 5:6], in0=prb[:, 1:2],
                                                   in1=prb[:, 2:3], op=ALU.add), inc=s_ve)
        v_pm = P.n(s_ve)
        pool.use(gm)
        P.wait("tensor", s_ve, v_pm)
        P.emit("tensor", lambda e: e.matmul(gm[:, 0:1], ones1c[:], prb[:, 0:1],
                                            start=True, stop=True, skip_group_check=True))
        P.emit("tensor", lambda e: e.matmul(gm[:, 1:2], ones1c[:], prb[:, 5:6],
                                            start=True, stop=True, skip_group_check=True),
               inc=s_pe)
        pe_pp = P.n(s_pe)
        P.wait("vector", s_pe, pe_pp)
        P.emit("vector", lambda e: e.tensor_copy(pfc[:], gm[:, 0:1]), inc=s_ve)
        P.emit("vector", lambda e: e.tensor_copy(pmc[:], gm[:, 1:2]), inc=s_ve)
        pool.free(gm, s_ve)
        xnb_done = P.n(s_ve)
        P.wait("tensor", s_ve, xnb_done)

        wfree = []

        def stream_ws(src_r, fi, nk):
            idx = len(wfree)
            par = idx % 2
            if idx >= 2:
                P.wait("gpsimd", *wfree[idx - 2])
            P.emit("gpsimd", lambda e, src_r=src_r, fi=fi, par=par, nk=nk: e.dma_start(
                out=wsb[par][:, 0:nk, :], in_=src_r[:, fi]),
                inc=s_ws[par], dma=True)
            wfree.append(None)
            return idx, par, (s_ws[par], P.n(s_ws[par]))

        xc_guard = p1_guard[1]
        xc_guard_a = p1_guard[0]
        hg = [0]
        xh_guard = [0]
        xp_guard = [0]
        eq_guard = [0]
        tmb_guard = [0]
        sp_guard = [0]
        for c in range(NCH):
            tc = c * C
            if xc_guard:
                P.wait("sync", s_ve, xc_guard)
            if xc_guard_a:
                P.wait("sync", s_act, xc_guard_a)
                xc_guard_a = 0
            if c > 0:
                P.wait("sync", s_st, P.n(s_st))
            P.emit("sync", lambda e, tc=tc: e.dma_start(out=x_c[:], in_=x_r[:, :, tc:tc + C]),
                   inc=s_lx, dma=True)
            xrdy = P.n(s_lx)

            iop = [io, gm]
            a_convs = []
            zbufs = [zsil, xnt]
            for f in range(17):
                idx, par, (wsem, wv) = stream_ws(wxt_r, f, 8)
                P.wait("tensor", wsem, wv)
                PS = iop[f % 2]
                pool.use(PS)
                for k in range(KT):
                    P.emit("tensor", lambda e, k=k, par=par, tc=tc, PS=PS: e.matmul(
                        PS[:], wsb[par][:, k, :], xnb[:, k, tc:tc + C],
                        start=(k == 0), stop=(k == KT - 1), skip_group_check=True),
                        inc=(s_pe if k == KT - 1 else None))
                pe_f = P.n(s_pe)
                P.wait("vector", s_pe, pe_f)
                P.emit("vector", lambda e, f=f, PS=PS: e.tensor_copy(
                    xBCc[:, f, 3:3 + C], PS[:]), inc=s_ve)
                pool.free(PS, s_ve)
                wfree[idx] = (s_pe, pe_f)
                zb = zbufs[f % 2]
                if len(a_convs) >= 2:
                    P.wait("vector", s_act, a_convs[-2])
                chain("vector")
                P.emit("vector", lambda e, f=f, zb=zb: e.tensor_scalar(
                    out=zb[:], in0=xBCc[:, f, 0:C], scalar1=convw[:, f, 0:1], scalar2=None,
                    op0=ALU.mult), inc=s_ve)
                for kk in range(1, 4):
                    chain("vector")
                    P.emit("vector", lambda e, f=f, kk=kk, zb=zb: e.scalar_tensor_tensor(
                        out=zb[:], in0=xBCc[:, f, kk:kk + C], scalar=convw[:, f, kk:kk + 1],
                        in1=zb[:], op0=ALU.mult, op1=ALU.add), inc=s_ve)
                v_cv = P.n(s_ve)
                P.wait("scalar", s_ve, v_cv)
                P.emit("scalar", lambda e, f=f, zb=zb: e.activation(
                    xbc[:, f, :], zb[:], AF.Silu, bias=convb[:, f:f + 1]), inc=s_act)
                a_convs.append(P.n(s_act))
                if f == 0 and xh_guard[0]:
                    P.wait("scalar", s_pe, xh_guard[0])
                P.wait("scalar", s_act, P.n(s_act))
                for q in range(NQ):
                    if f < NP:
                        P.emit("scalar", lambda e, q=q, f=f: e.dma_start(
                            out=xh_tm[:, q, f * 128:(f + 1) * 128],
                            in_=xbc[:, f, q * 128:(q + 1) * 128], transpose=True),
                            inc=s_tp, dma=True)
                    else:
                        P.emit("scalar", lambda e, q=q: e.dma_start(
                            out=BCtm[:, q, :], in_=xbc[:, 16, q * 128:(q + 1) * 128],
                            transpose=True), inc=s_tp, dma=True)
            for q in range(NQ):
                PS = iop[q % 2]
                pool.use(PS)
                for k in range(KT):
                    P.emit("tensor", lambda e, k=k, q=q, tc=tc, PS=PS: e.matmul(
                        PS[0:128, 0:32], xnb[:, k, tc + q * 128:tc + (q + 1) * 128], wdtb[:, k, :],
                        start=(k == 0), stop=False, skip_group_check=True))
                P.emit("tensor", lambda e, PS=PS: e.matmul(
                    PS[0:128, 0:32], ones1c[:], dtbr[:], start=False, stop=True,
                    skip_group_check=True), inc=s_pe)
                pe_dt = P.n(s_pe)
                P.wait("scalar", s_pe, pe_dt)
                if sp_guard[0]:
                    P.wait("scalar", s_ve, sp_guard[0])
                P.emit("scalar", lambda e, PS=PS: e.activation(sp1[:], PS[0:128, 0:32], AF.Abs),
                       inc=s_act)
                P.emit("scalar", lambda e, PS=PS: e.activation(sp2[:], PS[0:128, 0:32], AF.Relu),
                       inc=s_act)
                pool.free(PS, s_act)
                chain("scalar")
                P.emit("scalar", lambda e: e.activation(sp1[:], sp1[:], AF.Exp, scale=-1.0),
                       inc=s_act)
                chain("scalar")
                P.emit("scalar", lambda e: e.activation(sp1[:], sp1[:], AF.Ln,
                                                        bias=onesc[:]), inc=s_act)
                a_sp = P.n(s_act)
                P.wait("vector", s_act, a_sp)
                P.emit("vector", lambda e, q=q: e.tensor_tensor(
                    out=dt_tm[:, q, :], in0=sp2[:], in1=sp1[:], op=ALU.add), inc=s_ve)
                sp_guard[0] = P.n(s_ve)
                chain("vector")

            a_conv = P.n(s_act)
            P.emit("vector", lambda e: e.tensor_copy(xBCc[:, :, 0:3], xBCc[:, :, C:C + 3]),
                   inc=s_ve)
            tail_done = P.n(s_ve)

            P.wait("scalar", s_act, a_conv)
            tp1 = P.n(s_tp)
            P.wait("scalar", s_tp, tp1)
            for q in range(NQ):
                P.emit("scalar", lambda e, q=q: e.dma_start(
                    out=Clo4[:, q, :], in_=xbc[64:128, 16, q * 128:(q + 1) * 128]),
                    inc=s_tp, dma=True)
            tp_all = P.n(s_tp)
            P.wait("scalar", s_tp, tp_all)

            for q in range(NQ):
                P.emit("vector", lambda e, q=q: e.tensor_tensor(
                    out=dA[:], in0=dt_tm[:, q, :], in1=aneg[:], op=ALU.mult), inc=s_ve)
                v_dA = P.n(s_ve)
                pool.use(gm)
                P.wait("tensor", s_ve, v_dA)
                P.emit("tensor", lambda e: e.matmul(gm[:, 0:32], triu[:], dA[:],
                                                    start=True, stop=True,
                                                    skip_group_check=True))
                P.emit("tensor", lambda e: e.matmul(gm[0:1, 32:64], onesc[:], dA[:],
                                                    start=True, stop=True,
                                                    skip_group_check=True), inc=s_pe)
                pe_cum = P.n(s_pe)
                P.wait("vector", s_pe, pe_cum)
                P.emit("vector", lambda e: e.tensor_copy(cum_tm[:], gm[:, 0:32]), inc=s_ve)
                P.emit("vector", lambda e: e.tensor_copy(totr[:], gm[0:1, 32:64]), inc=s_ve)
                pool.free(gm, s_ve)
                v_cum = P.n(s_ve)
                pool.use(gm)
                P.wait("tensor", s_ve, v_cum)
                P.emit("tensor", lambda e: e.matmul(gm[:, 0:32], ones1c[:], totr[:],
                                                    start=True, stop=True,
                                                    skip_group_check=True), inc=s_pe)
                pe_tb = P.n(s_pe)
                P.wait("vector", s_pe, pe_tb)
                P.emit("vector", lambda e: e.tensor_copy(tot_b[:], gm[:, 0:32]), inc=s_ve)
                pool.free(gm, s_ve)
                v_tb = P.n(s_ve)
                P.wait("scalar", s_ve, v_tb)
                P.emit("scalar", lambda e: e.activation(etot_b[:], tot_b[:], AF.Exp), inc=s_act)
                a_et = P.n(s_act)
                P.wait("vector", s_act, a_et)
                P.emit("vector", lambda e: e.tensor_copy(etot_p[0:64, :], etot_b[0:64, 0:32:2]),
                       inc=s_ve)
                P.emit("vector", lambda e: e.tensor_copy(etot_p[64:128, :], etot_b[64:128, 1:32:2]),
                       inc=s_ve)
                P.emit("vector", lambda e: e.tensor_tensor(out=w2a[:], in0=tot_b[:],
                                                           in1=cum_tm[:], op=ALU.subtract),
                       inc=s_ve)
                v_w2 = P.n(s_ve)
                P.wait("scalar", s_ve, v_w2)
                P.emit("scalar", lambda e: e.activation(w2a[:], w2a[:], AF.Exp), inc=s_act)
                a_w2 = P.n(s_act)
                P.wait("vector", s_act, a_w2)
                P.emit("vector", lambda e, q=q: e.tensor_tensor(
                    out=dtw2[:], in0=dt_tm[:, q, :], in1=w2a[:], op=ALU.mult), inc=s_ve)
                v_dtw2 = P.n(s_ve)

                pool.use(gm)
                P.wait("tensor", s_ve, v_cum)
                for Q in range(4):
                    P.emit("tensor", lambda e, Q=Q: e.transpose(
                        gm[0:8, Q * 128:(Q + 1) * 128], cum_tm[:, Q * 8:(Q + 1) * 8], identf[:]),
                        inc=s_pe)
                pe_tr = P.n(s_pe)
                P.wait("vector", s_pe, pe_tr)
                for Q in range(4):
                    P.emit("vector", lambda e, Q=Q: e.tensor_copy(
                        cumfmq[:, Q, :], gm[0:8, Q * 128:(Q + 1) * 128]), inc=s_ve)
                pool.free(gm, s_ve)
                v_cfm = P.n(s_ve)
                cpar = (c * NQ + q) % 2
                P.wait("sync", s_ve, v_cfm)
                if c * NQ + q >= 2:
                    P.wait("sync", s_cb[cpar], P.n(s_cb[cpar]))
                P.emit("sync", lambda e, cpar=cpar: e.dma_start(
                    out=cbounce[cpar].rearrange("(a b) t -> b a t", b=8), in_=cumfmq[:]),
                    inc=s_cb[cpar], dma=True)
                P.wait("sync", s_cb[cpar], P.n(s_cb[cpar]))
                P.emit("sync", lambda e, cpar=cpar: e.dma_start(
                    out=cumflat[:], in_=cbounce[cpar][0:16, :].flatten().unsqueeze(0)),
                    inc=s_cb[cpar], dma=True)
                cflat_rdy = (s_cb[cpar], P.n(s_cb[cpar]))

                pool.use(py)
                P.wait("tensor", s_tp, tp_all)
                P.emit("tensor", lambda e, q=q: e.matmul(
                    py[:], xbc[0:64, 16, q * 128:(q + 1) * 128], Clo4[:, q, :],
                    start=True, stop=True, skip_group_check=True), inc=s_pe)
                pe_g = P.n(s_pe)
                P.wait("vector", s_pe, pe_g)
                P.emit("vector", lambda e: e.tensor_copy(G_sb[:], py[:]), inc=s_ve)
                pool.free(py, s_ve)
                v_g = P.n(s_ve)

                pool.use(T1b)
                pool.use(T1m)
                P.wait("tensor", *cflat_rdy)
                a_eq = 0
                for Q in range(4):
                    if Q == 2:
                        pe_now = P.n(s_pe)
                        P.wait("sync", s_pe, pe_now)
                        P.emit("sync", lambda e, cpar=cpar: e.dma_start(
                            out=cumflat[:], in_=cbounce[cpar][16:32, :].flatten().unsqueeze(0)),
                            inc=s_cb[cpar], dma=True)
                        P.wait("tensor", s_cb[cpar], P.n(s_cb[cpar]))
                    Qo = Q % 2
                    if a_eq:
                        P.wait("tensor", s_act, a_eq)
                    for hhalf in range(2):
                        P.emit("tensor", lambda e, Qo=Qo, hh=hhalf: e.matmul(
                            T1b_h[hh][:], ones1c[:],
                            cumflat[:, Qo * 1024 + hh * 512: Qo * 1024 + (hh + 1) * 512],
                            start=True, stop=True, skip_group_check=True))
                        P.emit("tensor", lambda e, Qo=Qo, hh=hhalf: e.matmul(
                            T1m_h[hh][:], ones1c[:],
                            cumflat[:, Qo * 1024 + hh * 512: Qo * 1024 + (hh + 1) * 512],
                            start=True, stop=False, skip_group_check=True))
                        P.emit("tensor", lambda e, Q=Q, hh=hhalf: e.matmul(
                            T1m_h[hh][:], cumfmq[:, Q, :],
                            negblk[:, hh * 512:(hh + 1) * 512],
                            start=False, stop=False, skip_group_check=True))
                        P.emit("tensor", lambda e, hh=hhalf: e.matmul(
                            T1m_h[hh][:], identf[:], maskd[:],
                            start=False, stop=True, skip_group_check=True), inc=s_pe)
                    pe_t1 = P.n(s_pe)
                    P.wait("scalar", s_pe, pe_t1)
                    for pq in range(4):
                        pj = Q * 4 + pq
                        P.emit("scalar", lambda e, pq=pq, pj=pj: e.activation(
                            EbAll[0:64, pj, :], T1b[0:64, pq * 256:pq * 256 + 128], AF.Exp),
                            inc=s_act)
                        P.emit("scalar", lambda e, pq=pq, pj=pj: e.activation(
                            EbAll[64:128, pj, :], T1b[64:128, pq * 256 + 128:pq * 256 + 256],
                            AF.Exp), inc=s_act)
                    if eq_guard[0]:
                        P.wait("scalar", s_ve, eq_guard[0])
                    if tmb_guard[0]:
                        P.wait("scalar", s_st, tmb_guard[0])
                    P.emit("scalar", lambda e, Q=Q: e.activation(E_q[:], T1m[:], AF.Exp),
                           inc=s_act)
                    a_eq = P.n(s_act)
                    P.wait("vector", s_act, a_eq)
                    P.wait("vector", s_ve, v_g)
                    P.emit("vector", lambda e, Q=Q: e.tensor_tensor(
                        out=W_all[:, Q * 8:(Q + 1) * 8, :],
                        in0=E_q[:].rearrange("p (h t) -> p h t", t=128),
                        in1=G_sb[:].unsqueeze(1).broadcast_to([128, 8, 128]), op=ALU.mult),
                        inc=s_ve)
                    eq_guard[0] = P.n(s_ve)
                pool.free(T1b, s_act)
                pool.free(T1m, s_act)
                v_wall = P.n(s_ve)
                P.emit("vector", lambda e, q=q: e.tensor_tensor(
                    out=CbAll[0:64, :, :], in0=EbAll[0:64, :, :],
                    in1=Clo4[:, q, :].unsqueeze(1).broadcast_to([64, NP, 128]), op=ALU.mult),
                    inc=s_ve)
                P.emit("vector", lambda e, q=q: e.tensor_tensor(
                    out=CbAll[64:128, :, :], in0=EbAll[64:128, :, :],
                    in1=xbc[64:128, 16, q * 128:(q + 1) * 128].unsqueeze(1).broadcast_to(
                        [64, NP, 128]), op=ALU.mult), inc=s_ve)
                v_cb = P.n(s_ve)

                P.wait("vector", s_tp, tp_all)
                P.wait("vector", s_ve, max(v_wall, v_cb, v_dtw2))
                for r in range(2):
                    if xp_guard[0]:
                        P.wait("vector", s_pe, xp_guard[0])
                    P.wait("vector", s_ve, P.n(s_ve))
                    dtr = dt_tm[:, q, r * 16:(r + 1) * 16].unsqueeze(2).broadcast_to(
                        [128, 16, 64])
                    dwr = dtw2[:, r * 16:(r + 1) * 16].unsqueeze(2).broadcast_to(
                        [128, 16, 64])
                    xhv = xh_tm[:, q, r * 1024:(r + 1) * 1024].rearrange(
                        "p (h v) -> p h v", v=64)
                    P.emit("vector", lambda e, dtr=dtr, xhv=xhv: e.tensor_tensor(
                        out=XdtA[:].rearrange("p (h v) -> p h v", v=64), in0=xhv, in1=dtr,
                        op=ALU.mult), inc=s_ve)
                    P.emit("vector", lambda e, dwr=dwr, xhv=xhv: e.tensor_tensor(
                        out=XwA[:].rearrange("p (h v) -> p h v", v=64), in0=xhv, in1=dwr,
                        op=ALU.mult), inc=s_ve)
                    v_xp = P.n(s_ve)
                    pool.use(T1b)
                    P.wait("tensor", s_ve, v_xp)
                    if hg[0]:
                        P.wait("tensor", s_ve, hg[0])
                    for pj8 in range(8):
                        pj = r * 8 + pj8
                        he, hod = 2 * pj, 2 * pj + 1
                        yc = T1b[:, pj8 * 128:(pj8 + 1) * 128]
                        P.emit("tensor", lambda e, pj8=pj8, yc=yc, he=he: e.matmul(
                            yc[0:64, :], XdtA[:, pj8 * 128:pj8 * 128 + 64],
                            W_all[:, he, :],
                            start=True, stop=False, skip_group_check=True))
                        P.emit("tensor", lambda e, pj=pj, yc=yc: e.matmul(
                            yc[0:64, :], H_all[0:64, pj, :], CbAll[0:64, pj, :],
                            start=False, stop=True, skip_group_check=True))
                        P.emit("tensor", lambda e, pj8=pj8, yc=yc, hod=hod: e.matmul(
                            yc[64:128, :], XdtA[:, pj8 * 128 + 64:pj8 * 128 + 128],
                            W_all[:, hod, :],
                            start=True, stop=False, tile_position=(0, 64),
                            skip_group_check=True))
                        P.emit("tensor", lambda e, pj=pj, yc=yc: e.matmul(
                            yc[64:128, :], H_all[64:128, pj, :], CbAll[64:128, pj, :],
                            start=False, stop=True, tile_position=(64, 64),
                            skip_group_check=True))
                        if r == 0 and pj8 == 0:
                            pool.use(T1m)
                        P.emit("tensor", lambda e, pj=pj, q=q, pj8=pj8: e.matmul(
                            T1m[0:64, pj * 64:(pj + 1) * 64], BCtm[:, q, 0:64],
                            XwA[:, pj8 * 128:pj8 * 128 + 64],
                            start=True, stop=True, skip_group_check=True))
                        P.emit("tensor", lambda e, pj=pj, q=q, pj8=pj8: e.matmul(
                            T1m[64:128, pj * 64:(pj + 1) * 64], BCtm[:, q, 0:64],
                            XwA[:, pj8 * 128 + 64:pj8 * 128 + 128],
                            start=True, stop=True, tile_position=(0, 64),
                            skip_group_check=True), inc=(s_pe if pj8 == 7 else None))
                    pe_yu = P.n(s_pe)
                    xp_guard[0] = pe_yu
                    P.wait("vector", s_pe, pe_yu)
                    drr = drep[:, r * 8:(r + 1) * 8].unsqueeze(2).broadcast_to(
                        [128, 8, 128])
                    P.emit("vector", lambda e, r=r, q=q, drr=drr: e.tensor_tensor(
                        out=XdtA[:].rearrange("p (h v) -> p h v", v=128),
                        in0=xbc[:, r * 8:(r + 1) * 8, q * 128:(q + 1) * 128], in1=drr,
                        op=ALU.mult), inc=s_ve)
                    P.wait("vector", s_ve, P.n(s_ve))
                    P.emit("vector", lambda e, r=r, q=q: e.tensor_tensor(
                        out=ycb[:, r * 8:(r + 1) * 8, q * 128:(q + 1) * 128],
                        in0=XdtA[:].rearrange("p (h v) -> p h v", v=128), in1=T1b[:, :].rearrange(
                            "p (h v) -> p h v", v=128), op=ALU.add), inc=s_ve)
                    pool.free(T1b, s_ve)
                P.wait("vector", s_pe, P.n(s_pe))
                etr = etot_p[:].unsqueeze(2).broadcast_to([128, 16, 64])
                P.wait("vector", s_ve, P.n(s_ve))
                P.emit("vector", lambda e, etr=etr: e.tensor_tensor(
                    out=H_all[:], in0=H_all[:], in1=etr, op=ALU.mult), inc=s_ve)
                P.wait("vector", s_ve, P.n(s_ve))
                P.emit("vector", lambda e: e.tensor_tensor(
                    out=H_all[:], in0=H_all[:], in1=T1m[:, :].rearrange(
                        "p (h v) -> p h v", v=64), op=ALU.add), inc=s_ve)
                pool.free(T1m, s_ve)
                hg[0] = P.n(s_ve)
            xh_guard[0] = P.n(s_pe)

            pool.use(T1b)
            gsum_pend = []
            xsqa2 = [xsqa, xnt]
            gcnt = {}

            def emit_gsum(j, a_sq):
                P.wait("tensor", s_act, a_sq)
                P.emit("tensor", lambda e, j=j: e.matmul(
                    T1b[0:1, 0:512], onesc[:], xsqa2[j % 2][:], start=(j == 0),
                    stop=(j == NP - 1), skip_group_check=True), inc=s_pe)
                gcnt[j] = P.n(s_pe)

            for j in range(NP):
                idx, par, (wsem, wv) = stream_ws(wzt_r, j, 8)
                P.wait("tensor", wsem, wv)
                PS = iop[j % 2]
                pool.use(PS)
                for k in range(KT):
                    P.emit("tensor", lambda e, k=k, par=par, tc=tc, PS=PS: e.matmul(
                        PS[:], wsb[par][:, k, :], xnb[:, k, tc:tc + C],
                        start=(k == 0), stop=(k == KT - 1), skip_group_check=True),
                        inc=(s_pe if k == KT - 1 else None))
                pe_z = P.n(s_pe)
                if gsum_pend:
                    emit_gsum(*gsum_pend.pop(0))
                P.wait("scalar", s_pe, pe_z)
                P.emit("scalar", lambda e, PS=PS: e.activation(zsil[:], PS[:], AF.Silu),
                       inc=s_act)
                pool.free(PS, s_act)
                a_z = P.n(s_act)
                wfree[idx] = (s_pe, pe_z)
                P.wait("vector", s_act, a_z)
                P.emit("vector", lambda e, j=j: e.tensor_tensor(
                    out=ycb[:, j, :], in0=ycb[:, j, :], in1=zsil[:], op=ALU.mult), inc=s_ve)
                v_yg = P.n(s_ve)
                P.wait("scalar", s_ve, v_yg)
                if j >= 2:
                    P.wait("scalar", s_pe, gcnt[j - 2])
                P.emit("scalar", lambda e, j=j: e.activation(
                    xsqa2[j % 2][:], ycb[:, j, :], AF.Square), inc=s_act)
                gsum_pend.append((j, P.n(s_act)))
            while gsum_pend:
                emit_gsum(*gsum_pend.pop(0))
            pe_gs = P.n(s_pe)
            P.wait("vector", s_pe, pe_gs)
            P.emit("vector", lambda e: e.tensor_scalar(
                out=srow[:, 0, :], in0=T1b[0:1, 0:512], scalar1=1.0 / DI, scalar2=1e-5,
                op0=ALU.mult, op1=ALU.add), inc=s_ve)
            pool.free(T1b, s_ve)
            v_gms = P.n(s_ve)
            P.wait("scalar", s_ve, v_gms)
            P.emit("scalar", lambda e: e.activation(srow[:, 1, :], srow[:, 0, :], AF.Ln),
                   inc=s_act)
            chain("scalar")
            P.emit("scalar", lambda e: e.activation(srow[:, 1, :], srow[:, 1, :], AF.Exp,
                                                    scale=-0.5), inc=s_act)
            a_gr = P.n(s_act)
            pool.use(io)
            P.wait("tensor", s_act, a_gr)
            P.emit("tensor", lambda e: e.matmul(io[:], ones1c[:], srow[:, 1, :],
                                                start=True, stop=True, skip_group_check=True),
                   inc=s_pe)
            pe_gb = P.n(s_pe)
            P.wait("vector", s_pe, pe_gb)
            P.emit("vector", lambda e: e.tensor_copy(zsil[:], io[:]), inc=s_ve)
            pool.free(io, s_ve)
            chain("vector")
            for j in range(NP):
                P.emit("vector", lambda e, j=j: e.scalar_tensor_tensor(
                    out=ycb[:, j, :], in0=ycb[:, j, :], scalar=gnw[:, j:j + 1], in1=zsil[:],
                    op0=ALU.mult, op1=ALU.mult), inc=s_ve)
            ygn_done = P.n(s_ve)
            P.wait("tensor", s_ve, ygn_done)

            P.wait("vector", s_lx, xrdy)
            for j3 in range(KT):
                idx, par, (wsem, wv) = stream_ws(wot_r, j3, 16)
                P.wait("tensor", wsem, wv)
                PS = iop[j3 % 2]
                pool.use(PS)
                for k in range(16):
                    P.emit("tensor", lambda e, k=k, par=par, PS=PS: e.matmul(
                        PS[:], wsb[par][:, k, :], ycb[:, k, :],
                        start=(k == 0), stop=(k == 15), skip_group_check=True),
                        inc=(s_pe if k == 15 else None))
                pe_o = P.n(s_pe)
                wfree[idx] = (s_pe, pe_o)
                P.wait("vector", s_pe, pe_o)
                chain("vector")
                P.emit("vector", lambda e, j3=j3, tc=tc: e.scalar_tensor_tensor(
                    out=xnt[:], in0=x_c[:, j3, :], scalar=normw[:, j3:j3 + 1],
                    in1=rstd_b[:, tc:tc + C], op0=ALU.mult, op1=ALU.mult), inc=s_ve)
                chain("vector")
                P.emit("vector", lambda e, j3=j3: e.scalar_tensor_tensor(
                    out=xnt[:], in0=xnt[:], scalar=pfc[:], in1=x_c[:, j3, :],
                    op0=ALU.mult, op1=ALU.add), inc=s_ve)
                chain("vector")
                P.emit("vector", lambda e, j3=j3: e.tensor_tensor(
                    out=xnt[:], in0=xnt[:], in1=x_c[:, j3, :], op=ALU.add), inc=s_ve)
                chain("vector")
                P.emit("vector", lambda e, j3=j3, PS=PS: e.scalar_tensor_tensor(
                    out=x1c[:, j3, :], in0=PS[:], scalar=pmc[:], in1=xnt[:],
                    op0=ALU.mult, op1=ALU.add), inc=s_ve)
                pool.free(PS, s_ve)
            x1_done = P.n(s_ve)

            P.wait("scalar", s_pe, pe_gs)
            P.wait("scalar", s_ve, x1_done)
            pool.use(T1b)
            pool.use(T1m)
            P.wait("tensor", s_ve, x1_done)
            plbufs = [xsqa, xnt]
            plpend = []
            plcnt = {}

            def emit_plm(j, a_sq):
                P.wait("tensor", s_act, a_sq)
                P.emit("tensor", lambda e, j=j: e.matmul(
                    T1m[0:1, 0:512], onesc[:], plbufs[j % 2][:], start=(j == 0),
                    stop=(j == KT - 1), skip_group_check=True), inc=s_pe)
                plcnt[j] = P.n(s_pe)
                xsq_guard[0] = P.n(s_pe)

            for j in range(KT):
                P.emit("tensor", lambda e, j=j: e.matmul(
                    T1b[0:1, 0:512], onesc[:], x1c[:, j, :], start=(j == 0), stop=(j == KT - 1),
                    skip_group_check=True), inc=(s_pe if j == KT - 1 else None))
                if j >= 2:
                    P.wait("scalar", s_pe, plcnt[j - 2])
                elif xsq_guard[0]:
                    P.wait("scalar", s_pe, xsq_guard[0])
                P.emit("scalar", lambda e, j=j: e.activation(
                    plbufs[j % 2][:], x1c[:, j, :], AF.Square), inc=s_act)
                plpend.append((j, P.n(s_act)))
                if len(plpend) > 1:
                    emit_plm(*plpend.pop(0))
            while plpend:
                emit_plm(*plpend.pop(0))
            pe_st2 = P.n(s_pe)
            P.wait("vector", s_pe, pe_st2)
            P.emit("vector", lambda e: e.tensor_scalar(
                out=srow[:, 0, :], in0=T1b[0:1, 0:512], scalar1=1.0 / D_MODEL, scalar2=None,
                op0=ALU.mult), inc=s_ve)
            v_mean = P.n(s_ve)
            chain("vector")
            P.emit("vector", lambda e: e.tensor_tensor(
                out=srow[:, 1, :], in0=srow[:, 0, :], in1=srow[:, 0, :], op=ALU.mult), inc=s_ve)
            chain("vector")
            P.emit("vector", lambda e: e.scalar_tensor_tensor(
                out=srow[:, 1, :], in0=T1m[0:1, 0:512], scalar=1.0 / D_MODEL, in1=srow[:, 1, :],
                op0=ALU.mult, op1=ALU.subtract), inc=s_ve)
            pool.free(T1b, s_ve)
            pool.free(T1m, s_ve)
            chain("vector")
            P.emit("vector", lambda e: e.tensor_scalar(
                out=srow[:, 1, :], in0=srow[:, 1, :], scalar1=1e-5, scalar2=None, op0=ALU.add),
                inc=s_ve)
            v_var = P.n(s_ve)
            P.wait("scalar", s_ve, v_var)
            P.emit("scalar", lambda e: e.activation(srow[:, 1, :], srow[:, 1, :], AF.Ln),
                   inc=s_act)
            chain("scalar")
            P.emit("scalar", lambda e: e.activation(srow[:, 1, :], srow[:, 1, :], AF.Exp,
                                                    scale=-0.5), inc=s_act)
            act_r = P.n(s_act)
            pool.use(io)
            P.wait("tensor", s_ve, v_mean)
            P.emit("tensor", lambda e: e.matmul(io[:, 0:512], ones1c[:], srow[:, 0, :],
                                                start=True, stop=True, skip_group_check=True))
            pool.use(gm)
            P.wait("tensor", s_act, act_r)
            P.emit("tensor", lambda e: e.matmul(gm[:, 0:512], ones1c[:], srow[:, 1, :],
                                                start=True, stop=True, skip_group_check=True),
                   inc=s_pe)
            pe_bc2 = P.n(s_pe)
            P.wait("vector", s_pe, pe_bc2)
            for j in range(KT):
                P.emit("vector", lambda e, j=j: e.tensor_tensor(
                    out=x1c[:, j, :], in0=x1c[:, j, :], in1=io[:, 0:512], op=ALU.subtract),
                    inc=s_ve)
                chain("vector")
                P.emit("vector", lambda e, j=j: e.tensor_tensor(
                    out=x1c[:, j, :], in0=x1c[:, j, :], in1=gm[:, 0:512], op=ALU.mult), inc=s_ve)
            pool.free(io, s_ve)
            pool.free(gm, s_ve)
            v_ln = P.n(s_ve)
            P.wait("sync", s_ve, v_ln)
            P.emit("sync", lambda e, tc=tc: e.dma_start(
                out=out_r[:, :, tc:tc + C], in_=x1c[:]), inc=s_st, dma=True)

            P.wait("tensor", s_ve, v_ln)
            for q in range(NQ):
                pool.use(gm)
                for k in range(KT):
                    P.emit("tensor", lambda e, k=k, q=q: e.matmul(
                        gm[:, 0:8], x1c[:, k, q * 128:(q + 1) * 128], rwt[:, k, :],
                        start=(k == 0), stop=False, skip_group_check=True))
                P.emit("tensor", lambda e: e.matmul(
                    gm[:, 0:8], ones1c[:], rbr[:], start=False, stop=True,
                    skip_group_check=True), inc=s_pe)
                pe_lg = P.n(s_pe)
                P.wait("vector", s_pe, pe_lg)
                P.emit("vector", lambda e, q=q: e.tensor_copy(lgb[:, q, :], gm[:, 0:8]),
                       inc=s_ve)
                pool.free(gm, s_ve)
                if tmb_guard[0]:
                    P.wait("vector", s_st, tmb_guard[0])
                for j in range(KT):
                    pool.use(io)
                    P.emit("tensor", lambda e, j=j, q=q: e.transpose(
                        io[:, 0:128], x1c[:, j, q * 128:(q + 1) * 128], identf[:]), inc=s_pe)
                    pe_tt = P.n(s_pe)
                    P.wait("vector", s_pe, pe_tt)
                    P.emit("vector", lambda e, j=j: e.tensor_copy(
                        E_q[:].rearrange("p (k f) -> p k f", f=128)[:, j, :], io[:, 0:128]),
                        inc=s_ve)
                    pool.free(io, s_ve)
                v_tm_q = P.n(s_ve)
                P.wait("sync", s_ve, v_tm_q)
                P.emit("sync", lambda e, tc=tc, q=q: e.dma_start(
                    out=xtm_d[tc + q * 128: tc + (q + 1) * 128, :].rearrange(
                        "p (k f) -> p k f", f=128),
                    in_=E_q[:].rearrange("p (k f) -> p k f", f=128)), inc=s_st, dma=True)
                tmb_guard[0] = P.n(s_st)
            v_lg_all = P.n(s_ve)
            P.wait("sync", s_ve, v_lg_all)
            P.emit("sync", lambda e, tc=tc: e.dma_start(
                out=lg_d[tc:tc + C, :].rearrange("(q p) f -> p q f", p=128), in_=lgb[:]),
                inc=s_st, dma=True)
            xc_guard = P.n(s_ve)

        P.wait("sync", s_st, P.n(s_st))



def build_mamba_nc():
    nc = bass.Bass("TRN2", target_bir_lowering=False)
    DI, CD, NP = 2048, 2176, 16
    d = {}
    d["x"] = nc.dram_tensor("x", [D_MODEL, SEQ], F32, kind="ExternalInput").ap()
    d["wzt"] = nc.dram_tensor("wzt", [128, 16, 8, 128], F32, kind="ExternalInput").ap()
    d["wxt"] = nc.dram_tensor("wxt", [128, 17, 8, 128], F32, kind="ExternalInput").ap()
    d["wdt"] = nc.dram_tensor("wdt", [1024, 32], F32, kind="ExternalInput").ap()
    d["wot_m"] = nc.dram_tensor("wot_m", [128, 8, 16, 128], F32, kind="ExternalInput").ap()
    d["normw"] = nc.dram_tensor("normw", [128, KT], F32, kind="ExternalInput").ap()
    d["gnw"] = nc.dram_tensor("gnw", [128, NP], F32, kind="ExternalInput").ap()
    d["convw"] = nc.dram_tensor("convw", [128, 17, 4], F32, kind="ExternalInput").ap()
    d["convb"] = nc.dram_tensor("convb", [128, 17], F32, kind="ExternalInput").ap()
    d["drep"] = nc.dram_tensor("drep", [128, NP], F32, kind="ExternalInput").ap()
    d["dtb"] = nc.dram_tensor("dtb", [1, 32], F32, kind="ExternalInput").ap()
    d["alog"] = nc.dram_tensor("alog", [128, 32], F32, kind="ExternalInput").ap()
    d["rw1t"] = nc.dram_tensor("rw1t", [128, 8, 32], F32, kind="ExternalInput").ap()
    d["rb1"] = nc.dram_tensor("rb1", [32, 1], F32, kind="ExternalInput").ap()
    d["rw2t"] = nc.dram_tensor("rw2t", [32, 3], F32, kind="ExternalInput").ap()
    d["rb2"] = nc.dram_tensor("rb2", [1, 3], F32, kind="ExternalInput").ap()
    d["rwt"] = nc.dram_tensor("rwt", [128, 8, 8], F32, kind="ExternalInput").ap()
    d["rbr"] = nc.dram_tensor("rbr", [1, 8], F32, kind="ExternalInput").ap()
    d["ones"] = nc.dram_tensor("ones", [128, 1], F32, kind="ExternalInput").ap()
    d["ones1"] = nc.dram_tensor("ones1", [1, 128], F32, kind="ExternalInput").ap()
    d["identf"] = nc.dram_tensor("identf", [128, 128], F32, kind="ExternalInput").ap()
    d["identb"] = nc.dram_tensor("identb", [128, 128], F32, kind="ExternalInput").ap()
    d["triu"] = nc.dram_tensor("triu", [128, 128], F32, kind="ExternalInput").ap()
    d["negblk"] = nc.dram_tensor("negblk", [8, 1024], F32, kind="ExternalInput").ap()
    d["maskd"] = nc.dram_tensor("maskd", [128, 512], F32, kind="ExternalInput").ap()
    d["out"] = nc.dram_tensor("out", [D_MODEL, SEQ], F32, kind="ExternalOutput").ap()
    d["xtm"] = nc.dram_tensor("xtm", [SEQ, 1024], BF16, kind="ExternalOutput").ap()
    d["logits"] = nc.dram_tensor("logits", [SEQ, 8], F32, kind="ExternalOutput").ap()
    d["cbounce"] = [nc.dram_tensor(f"cbounce{i}", [32, 128], F32).ap() for i in range(2)]
    with ExitStack() as ctx:
        block = ctx.enter_context(nc.Block())
        P = Prog(ctx, nc)
        pool = PsumPool(P)
        emit_mamba(nc, P, pool, d)
        P.build_block(block)
    return nc

def host_mamba_inputs(x_fm_core, w, router_w, router_b):
    i = np.arange(128)
    triu = (i[:, None] <= i[None, :]).astype(np.float32)
    maskd = np.tile((i[:, None] > i[None, :]).astype(np.float32) * -1e30, (1, 4))
    negblk = np.zeros((8, 1024), np.float32)
    for k in range(8):
        negblk[k, k * 128:(k + 1) * 128] = -1.0
    ip = np.asarray(w["in_proj_w"], np.float32)
    f32 = np.float32
    return dict(
        x=np.ascontiguousarray(x_fm_core, dtype=f32),
        wzt=np.ascontiguousarray(_t(ip[0:2048]).reshape(8, 128, 16, 128).transpose(1, 2, 0, 3)),
        wxt=np.ascontiguousarray(_t(ip[2048:4224]).reshape(8, 128, 17, 128).transpose(1, 2, 0, 3)),
        wdt=_t(ip[4224:4256]),
        wot_m=np.ascontiguousarray(_t(w["out_proj_w"]).reshape(16, 128, 8, 128).transpose(1, 2, 0, 3)),
        normw=np.ascontiguousarray(np.asarray(w["norm_w"], f32).reshape(8, 128).T),
        gnw=np.ascontiguousarray(np.asarray(w["gnorm_w"], f32).reshape(16, 128).T),
        convw=np.ascontiguousarray(np.asarray(w["conv_w"], f32).reshape(17, 128, 4).transpose(1, 0, 2)),
        convb=np.ascontiguousarray(np.asarray(w["conv_b"], f32).reshape(17, 128).T),
        drep=np.ascontiguousarray(np.repeat(np.asarray(w["D_param"], f32), 64).reshape(16, 128).T),
        dtb=np.asarray(w["dt_bias"], f32).reshape(1, 32).copy(),
        alog=np.ascontiguousarray(np.broadcast_to(np.asarray(w["A_log"], f32), (128, 32))),
        rw1t=np.ascontiguousarray(_t(w["r_w1"]).reshape(8, 128, 32).transpose(1, 0, 2)),
        rb1=np.asarray(w["r_b1"], f32).reshape(32, 1).copy(),
        rw2t=_t(w["r_w2"]), rb2=np.asarray(w["r_b2"], f32).reshape(1, 3).copy(),
        rwt=np.ascontiguousarray(_t(router_w).reshape(8, 128, 8).transpose(1, 0, 2)),
        rbr=np.asarray(router_b, f32).reshape(1, 8).copy(),
        ones=np.ones((128, 1), f32), ones1=np.ones((1, 128), f32),
        identf=np.eye(128, dtype=f32), identb=np.eye(128, dtype=f32),
        triu=np.ascontiguousarray(triu), negblk=negblk,
        maskd=np.ascontiguousarray(maskd),
    )
def emit_routing(P, pool, nc, names, NTILES, CAP=2048.0):
    (lgt, m1, m2c, eqa, eqb, mka, gka, p1c, p2c, offs, totev, prt, prt2,
     triu_s, ones1c, onesc, s_pe, s_act, s_ve, out_cb) = names
    ALUx = ALU
    for i in range(NTILES):
        P.wait("vector", s_ve, P.n(s_ve))
        P.emit("vector", lambda e, i=i: e.tensor_reduce(
            out=m1[:], in_=lgt[:, i, :], axis=mybir.AxisListType.X, op=ALUx.max), inc=s_ve)
        P.wait("vector", s_ve, P.n(s_ve))
        P.emit("vector", lambda e, i=i: e.tensor_tensor(
            out=eqa[:], in0=lgt[:, i, :], in1=m1[:].broadcast_to([128, 8]),
            op=ALUx.is_ge), inc=s_ve)
        P.wait("vector", s_ve, P.n(s_ve))
        P.emit("vector", lambda e, i=i: e.scalar_tensor_tensor(
            out=mka[:], in0=eqa[:], scalar=-1e30, in1=lgt[:, i, :],
            op0=ALUx.mult, op1=ALUx.add), inc=s_ve)
        P.wait("vector", s_ve, P.n(s_ve))
        P.emit("vector", lambda e: e.tensor_reduce(
            out=m2c[:], in_=mka[:], axis=mybir.AxisListType.X, op=ALUx.max), inc=s_ve)
        P.wait("vector", s_ve, P.n(s_ve))
        P.emit("vector", lambda e: e.tensor_tensor(
            out=eqb[:], in0=mka[:], in1=m2c[:].broadcast_to([128, 8]), op=ALUx.is_ge),
            inc=s_ve)
        P.emit("vector", lambda e: e.tensor_tensor(
            out=p1c[:], in0=m1[:], in1=m2c[:], op=ALUx.subtract), inc=s_ve)
        v_p = P.n(s_ve)
        P.wait("scalar", s_ve, v_p)
        P.emit("scalar", lambda e: e.activation(p1c[:], p1c[:], AF.Sigmoid), inc=s_act)
        a_p = P.n(s_act)
        P.wait("vector", s_act, a_p)
        P.emit("vector", lambda e: e.tensor_scalar(
            out=p2c[:], in0=p1c[:], scalar1=-1.0, scalar2=1.0, op0=ALUx.mult, op1=ALUx.add),
            inc=s_ve)
        P.wait("vector", s_ve, P.n(s_ve))
        P.emit("vector", lambda e: e.tensor_scalar(
            out=gka[:], in0=eqa[:], scalar1=p1c[:], scalar2=None, op0=ALUx.mult), inc=s_ve)
        P.wait("vector", s_ve, P.n(s_ve))
        P.emit("vector", lambda e: e.scalar_tensor_tensor(
            out=gka[:], in0=eqb[:], scalar=p2c[:], in1=gka[:], op0=ALUx.mult, op1=ALUx.add),
            inc=s_ve)
        P.emit("vector", lambda e: e.tensor_tensor(
            out=mka[:], in0=eqa[:], in1=eqb[:], op=ALUx.add), inc=s_ve)
        v_m = P.n(s_ve)
        pool.use(prt)
        P.wait("tensor", s_ve, v_m)
        P.emit("tensor", lambda e: e.matmul(prt[:, 0:8], triu_s[:], mka[:],
                                            start=True, stop=False, skip_group_check=True))
        P.emit("tensor", lambda e: e.matmul(prt[:, 0:8], ones1c[:], offs[:],
                                            start=False, stop=True, skip_group_check=True))
        pool.use(prt2)
        P.emit("tensor", lambda e: e.matmul(prt2[0:1, 0:8], onesc[:], mka[:],
                                            start=True, stop=True, skip_group_check=True),
               inc=s_pe)
        pe_c = P.n(s_pe)
        P.wait("vector", s_pe, pe_c)
        P.emit("vector", lambda e: e.tensor_tensor(
            out=offs[:], in0=offs[:], in1=prt2[0:1, 0:8], op=ALUx.add), inc=s_ve)
        pool.free(prt2, s_ve)
        P.wait("vector", s_ve, P.n(s_ve))
        P.emit("vector", lambda e, CAP=CAP: e.tensor_scalar(
            out=eqb[:], in0=prt[:, 0:8], scalar1=CAP, scalar2=None, op0=ALUx.is_lt), inc=s_ve)
        P.wait("vector", s_ve, P.n(s_ve))
        P.emit("vector", lambda e: e.tensor_tensor(
            out=mka[:], in0=mka[:], in1=eqb[:], op=ALUx.mult), inc=s_ve)
        P.wait("vector", s_ve, P.n(s_ve))
        P.emit("vector", lambda e: e.tensor_tensor(
            out=gka[:], in0=gka[:], in1=mka[:], op=ALUx.mult), inc=s_ve)
        out_cb(i)


def emit_routing16(P, pool, nc, n, NTILES):
    (lgt, m1, eqa, eqb, mka, gka, p1c, pfxr, offs, prt, prt2,
     triu_s, ones1c, onesc, s_pe, s_act, s_ve, out_cb) = n
    B3 = [128, 16, 8]
    for g in range(NTILES // 16):
        L = lgt[:, g * 16:(g + 1) * 16, :]
        P.wait("vector", s_ve, P.n(s_ve))
        P.emit("vector", lambda e, L=L: e.tensor_reduce(
            out=m1[:, :, 0], in_=L, axis=mybir.AxisListType.X, op=ALU.max), inc=s_ve)
        P.wait("vector", s_ve, P.n(s_ve))
        P.emit("vector", lambda e, L=L: e.tensor_tensor(
            out=eqa[:], in0=L, in1=m1[:, :, 0:1].broadcast_to(B3), op=ALU.is_ge), inc=s_ve)
        P.wait("vector", s_ve, P.n(s_ve))
        P.emit("vector", lambda e, L=L: e.scalar_tensor_tensor(
            out=mka[:], in0=eqa[:], scalar=-1e30, in1=L, op0=ALU.mult, op1=ALU.add),
            inc=s_ve)
        P.wait("vector", s_ve, P.n(s_ve))
        P.emit("vector", lambda e: e.tensor_reduce(
            out=m1[:, :, 1], in_=mka[:], axis=mybir.AxisListType.X, op=ALU.max), inc=s_ve)
        P.wait("vector", s_ve, P.n(s_ve))
        P.emit("vector", lambda e: e.tensor_tensor(
            out=eqb[:], in0=mka[:], in1=m1[:, :, 1:2].broadcast_to(B3), op=ALU.is_ge),
            inc=s_ve)
        P.emit("vector", lambda e: e.tensor_tensor(
            out=p1c[:], in0=m1[:, :, 0], in1=m1[:, :, 1], op=ALU.subtract), inc=s_ve)
        v_p = P.n(s_ve)
        P.wait("scalar", s_ve, v_p)
        P.emit("scalar", lambda e: e.activation(p1c[:], p1c[:], AF.Sigmoid), inc=s_act)
        a_p = P.n(s_act)
        P.wait("vector", s_act, a_p)
        P.emit("vector", lambda e: e.tensor_tensor(
            out=mka[:], in0=eqa[:], in1=eqb[:], op=ALU.add), inc=s_ve)
        P.emit("vector", lambda e: e.tensor_tensor(
            out=gka[:], in0=eqa[:], in1=p1c[:].unsqueeze(2).broadcast_to(B3),
            op=ALU.mult), inc=s_ve)
        P.wait("vector", s_ve, P.n(s_ve))
        P.emit("vector", lambda e: e.tensor_scalar(
            out=p1c[:], in0=p1c[:], scalar1=-1.0, scalar2=1.0, op0=ALU.mult, op1=ALU.add),
            inc=s_ve)
        P.wait("vector", s_ve, P.n(s_ve))
        P.emit("vector", lambda e: e.tensor_tensor(
            out=eqb[:], in0=eqb[:], in1=p1c[:].unsqueeze(2).broadcast_to(B3),
            op=ALU.mult), inc=s_ve)
        P.wait("vector", s_ve, P.n(s_ve))
        P.emit("vector", lambda e: e.tensor_tensor(
            out=gka[:], in0=gka[:], in1=eqb[:], op=ALU.add), inc=s_ve)
        v_m = P.n(s_ve)
        pool.use(prt)
        pool.use(prt2)
        P.wait("tensor", s_ve, v_m)
        P.emit("tensor", lambda e: e.matmul(
            prt2[0:1, 0:128], onesc[:], mka[:].rearrange("p a b -> p (a b)"),
            start=True, stop=True, skip_group_check=True), inc=s_pe)
        pe_tot = P.n(s_pe)
        P.wait("vector", s_pe, pe_tot)
        P.emit("vector", lambda e: e.tensor_copy(pfxr[:, 0, :], offs[:]), inc=s_ve)
        for t in range(1, 16):
            P.wait("vector", s_ve, P.n(s_ve))
            P.emit("vector", lambda e, t=t: e.scalar_tensor_tensor(
                out=pfxr[:, t, :], in0=prt2[0:1, (t - 1) * 8:t * 8], scalar=1.0,
                in1=pfxr[:, t - 1, :], op0=ALU.mult, op1=ALU.add), inc=s_ve)
        P.wait("vector", s_ve, P.n(s_ve))
        P.emit("vector", lambda e: e.scalar_tensor_tensor(
            out=offs[:], in0=prt2[0:1, 15 * 8:16 * 8], scalar=1.0, in1=pfxr[:, 15, :],
            op0=ALU.mult, op1=ALU.add), inc=s_ve)
        v_pfx = P.n(s_ve)
        pool.free(prt2, s_ve)
        P.wait("tensor", s_ve, v_pfx)
        for t in range(16):
            P.emit("tensor", lambda e, t=t: e.matmul(
                prt[:, t * 8:(t + 1) * 8], triu_s[:], mka[:, t, :],
                start=True, stop=False, skip_group_check=True))
            P.emit("tensor", lambda e, t=t: e.matmul(
                prt[:, t * 8:(t + 1) * 8], ones1c[:], pfxr[:, t, :],
                start=False, stop=True, skip_group_check=True),
                inc=(s_pe if t == 15 else None))
        pe_c = P.n(s_pe)
        P.wait("vector", s_pe, pe_c)
        P.emit("vector", lambda e: e.tensor_scalar(
            out=eqb[:], in0=prt[:, 0:128].rearrange("p (a b) -> p a b", b=8),
            scalar1=2048.0, scalar2=None, op0=ALU.is_lt), inc=s_ve)
        P.wait("vector", s_ve, P.n(s_ve))
        P.emit("vector", lambda e: e.tensor_tensor(
            out=mka[:], in0=mka[:], in1=eqb[:], op=ALU.mult), inc=s_ve)
        P.wait("vector", s_ve, P.n(s_ve))
        P.emit("vector", lambda e: e.tensor_tensor(
            out=gka[:], in0=gka[:], in1=mka[:], op=ALU.mult), inc=s_ve)
        out_cb(g)



def emit_moe(nc, P, pool, d, pfx="b"):
    NTOK = BATCH * SEQ
    NTILES = NTOK // 128
    lg_d = d["lg"]
    xf_d = d["xf"]
    we_d = d["we"]
    ve_d = d["vex"]
    wo_d = d["wo"]
    ehot_d = d["ehot"]
    slotinit_d = d["slotinit"]
    triu_d = d["trius"]
    ones_d = d["ones"]
    ones1_d = d["ones1"]
    identf_d = d["identf"]
    ye_d = d["ye"]
    slotmap = d["slotmap"]

    we_r = we_d
    ve_r = ve_d
    wo_r = wo_d.rearrange("(k p) f -> p k f", p=128)


    with ExitStack() as ctx:
        lgt = _sb(ctx, nc, pfx + "lgt", [128, NTILES, 8], F32)
        m1 = _sb(ctx, nc, pfx + "m1", [128, 16, 2], F32)
        eqa = _sb(ctx, nc, pfx + "eqa", [128, 16, 8], F32)
        eqb = _sb(ctx, nc, pfx + "eqb", [128, 16, 8], F32)
        mka = _sb(ctx, nc, pfx + "mka", [128, 16, 8], F32)
        gka = _sb(ctx, nc, pfx + "gka", [128, 16, 8], F32)
        p1c = _sb(ctx, nc, pfx + "p1c", [128, 16], F32)
        pfxr = _sb(ctx, nc, pfx + "pfxr", [1, 16, 8], F32)
        offs = _sb(ctx, nc, pfx + "offs", [1, 8], F32)
        tokc = _sb(ctx, nc, pfx + "tokc", [128, 1], U32)
        tokf = _sb(ctx, nc, pfx + "tokf", [128, 1], F32)
        t2048 = _sb(ctx, nc, pfx + "t2048", [128, 1], F32)
        toff0 = _sb(ctx, nc, pfx + "toff0", [128, 16], F32)
        toff2k = _sb(ctx, nc, pfx + "toff2k", [128, 16], F32)
        scr16 = _sb(ctx, nc, pfx + "scr16", [128, 6, 16], F32)
        ridx16 = _sb(ctx, nc, pfx + "ridx16", [128, 16, 8], F32)
        pay = [_sb(ctx, nc, pfx + f"pay{i}", [128, 16, 2], F32) for i in range(2)]
        slotu = [_sb(ctx, nc, pfx + f"slotu{i}", [128, 16], U32) for i in range(2)]
        smap = _sb(ctx, nc, pfx + "smap", [128, 16, 2], F32)
        gidx = _sb(ctx, nc, pfx + "gidx", [128, 1], U32)
        gatec = _sb(ctx, nc, pfx + "gatec", [128, 16], F32)
        xg = [_sb(ctx, nc, pfx + f"xg{i}", [128, 1024], F32) for i in range(2)]
        xe_bf = _sb(ctx, nc, pfx + "xe_bf", [128, KT, 2048], BF16)
        h_bf = _sb(ctx, nc, pfx + "h_bf", [128, 16, 2048], BF16)
        wob = _sb(ctx, nc, pfx + "wob", [128, 16, 1024], BF16)
        vsb = [_sb(ctx, nc, pfx + f"vsb{i}", [128, KT, 128], BF16) for i in range(2)]
        t1 = _sb(ctx, nc, pfx + "t1", [128, 512], F32)
        ytm = _sb(ctx, nc, pfx + "ytm", [128, 2, 512], BF16)
        ehot = _sb(ctx, nc, pfx + "ehot_s", [128, 8], F32)
        ebase = _sb(ctx, nc, pfx + "ebase_s", [128, 8], F32)
        triu_s = _sb(ctx, nc, pfx + "triu_s", [128, 128], F32)
        onesc = _sb(ctx, nc, pfx + "onesc_b", [128, 1], F32)
        ones1c = _sb(ctx, nc, pfx + "ones1c_b", [1, 128], F32)
        identf = _sb(ctx, nc, pfx + "identf_b", [128, 128], F32)
        identbf_s = _sb(ctx, nc, pfx + "identbf_b", [128, 128], BF16)

        prt = _ps(ctx, nc, pfx + "prt", [128, 512])
        prt2 = _ps(ctx, nc, pfx + "prt2", [128, 512])
        pa = [_ps(ctx, nc, pfx + f"pa{i}", [128, 512]) for i in range(2)]
        pb = [_ps(ctx, nc, pfx + f"pb{i}", [128, 512]) for i in range(2)]
        pyt = [_ps(ctx, nc, pfx + f"pyt{i}", [128, 512]) for i in range(2)]

        s_lc = P.sem(pfx + "s_lc", "sync")
        s_ibf = P.sem(pfx + "s_ibf", "gpsimd")
        s_w = [P.sem(pfx + "s_w0", "gpsimd"), P.sem(pfx + "s_w1", "gpsimd")]
        s_gi = P.sem(pfx + "s_gi", "gpsimd")
        s_gio = P.sem(pfx + "s_gio", "gpsimd")
        s_wo = P.sem(pfx + "s_wo", "gpsimd")
        s_gg = [P.sem(pfx + "s_gg0", "gpsimd"), P.sem(pfx + "s_gg1", "gpsimd")]
        s_pe = P.sem(pfx + "s_pe", "tensor")
        s_act = P.sem(pfx + "s_act", "scalar")
        s_ve = P.sem(pfx + "s_ve", "vector")
        s_st = P.sem(pfx + "s_st", "sync")

        for dst, src in [(ehot, ehot_d), (ebase, d["ebase"]), (triu_s, triu_d),
                         (onesc, ones_d), (ones1c, ones1_d), (identf, identf_d)]:
            P.emit("sync", lambda e, dst=dst, src=src: e.dma_start(out=dst[:], in_=src),
                   inc=s_lc, dma=True)
        P.emit("sync", lambda e: e.dma_start(
            out=lgt[:], in_=lg_d.rearrange("(i p) f -> p i f", p=128)), inc=s_lc, dma=True)
        P.emit("sync", lambda e: e.dma_start(out=slotmap, in_=slotinit_d), inc=s_lc, dma=True)
        ld0 = P.n(s_lc)
        for eng in ("tensor", "scalar", "vector", "gpsimd"):
            P.wait(eng, s_lc, ld0)
        P.emit("vector", lambda e: e.memset(offs[:], 0.0), inc=s_ve)
        P.emit("gpsimd", lambda e: e.iota(tokc[:], pattern=[[0, 1]], base=0,
                                          channel_multiplier=1), inc=s_gio)
        P.wait("vector", s_gio, P.n(s_gio))
        P.emit("vector", lambda e: e.tensor_copy(tokf[:], tokc[:]), inc=s_ve)
        P.wait("vector", s_ve, P.n(s_ve))
        P.emit("vector", lambda e: e.tensor_scalar(
            out=t2048[:], in0=tokf[:], scalar1=2048.0, scalar2=None, op0=ALU.add), inc=s_ve)
        for t in range(16):
            P.emit("vector", lambda e, t=t: e.tensor_scalar(
                out=toff0[:, t:t + 1], in0=tokf[:], scalar1=float(t * 128), scalar2=None,
                op0=ALU.add), inc=s_ve)
        P.wait("vector", s_ve, P.n(s_ve))
        P.emit("vector", lambda e: e.tensor_scalar(
            out=toff2k[:], in0=toff0[:], scalar1=2048.0, scalar2=None, op0=ALU.add), inc=s_ve)
        P.wait("vector", s_ve, P.n(s_ve))

        want_ridx = "ridx" in d
        scat_n = [0, 0]

        def group_cb(g):
            pr = g % 2
            B3 = [128, 16, 8]
            cntv = prt[:, 0:128].rearrange("p (a b) -> p a b", b=8)
            eb3 = ehot[:].unsqueeze(1).broadcast_to(B3)
            P.wait("vector", s_ve, P.n(s_ve))
            P.emit("vector", lambda e: e.tensor_tensor(out=eqa[:], in0=gka[:], in1=eb3,
                                                       op=ALU.mult), inc=s_ve)
            P.wait("vector", s_ve, P.n(s_ve))
            P.emit("vector", lambda e: e.tensor_reduce(
                out=scr16[:, 0, :], in_=eqa[:], axis=mybir.AxisListType.X, op=ALU.add),
                inc=s_ve)
            P.wait("vector", s_ve, P.n(s_ve))
            P.emit("vector", lambda e, cntv=cntv: e.tensor_tensor(
                out=eqa[:], in0=cntv, in1=eb3, op=ALU.mult), inc=s_ve)
            P.wait("vector", s_ve, P.n(s_ve))
            P.emit("vector", lambda e: e.tensor_reduce(
                out=scr16[:, 1, :], in_=eqa[:], axis=mybir.AxisListType.X, op=ALU.add),
                inc=s_ve)
            P.wait("vector", s_ve, P.n(s_ve))
            P.emit("vector", lambda e: e.tensor_tensor(out=eqa[:], in0=mka[:], in1=eb3,
                                                       op=ALU.mult), inc=s_ve)
            P.wait("vector", s_ve, P.n(s_ve))
            P.emit("vector", lambda e: e.tensor_reduce(
                out=scr16[:, 2, :], in_=eqa[:], axis=mybir.AxisListType.X, op=ALU.add),
                inc=s_ve)
            if want_ridx:
                ebb = ebase[:].unsqueeze(1).broadcast_to(B3)
                P.wait("vector", s_ve, P.n(s_ve))
                P.emit("vector", lambda e, cntv=cntv, ebb=ebb: e.tensor_tensor(
                    out=ridx16[:], in0=cntv, in1=ebb, op=ALU.add), inc=s_ve)
                P.wait("vector", s_ve, P.n(s_ve))
                P.emit("vector", lambda e, cntv=cntv: e.tensor_scalar(
                    out=eqa[:], in0=cntv, scalar1=1024.0, scalar2=None,
                    op0=ALU.is_ge), inc=s_ve)
                P.wait("vector", s_ve, P.n(s_ve))
                P.emit("vector", lambda e: e.scalar_tensor_tensor(
                    out=ridx16[:], in0=eqa[:], scalar=7168.0, in1=ridx16[:],
                    op0=ALU.mult, op1=ALU.add), inc=s_ve)
                P.wait("vector", s_ve, P.n(s_ve))
                P.emit("vector", lambda e: e.tensor_tensor(
                    out=ridx16[:], in0=ridx16[:], in1=mka[:], op=ALU.mult), inc=s_ve)
                P.wait("vector", s_ve, P.n(s_ve))
                P.emit("vector", lambda e: e.tensor_scalar(
                    out=eqa[:], in0=mka[:], scalar1=-16384.0, scalar2=16384.0,
                    op0=ALU.mult, op1=ALU.add), inc=s_ve)
                P.wait("vector", s_ve, P.n(s_ve))
                P.emit("vector", lambda e, g=g: e.tensor_tensor(
                    out=rix_sb[:, g * 16:(g + 1) * 16, :], in0=ridx16[:], in1=eqa[:],
                    op=ALU.add), inc=s_ve)
            pool.free(prt, s_ve)
            P.wait("vector", s_ve, P.n(s_ve))
            P.emit("vector", lambda e: e.tensor_scalar(
                out=scr16[:, 3, :], in0=scr16[:, 2, :], scalar1=-1.0, scalar2=1.0,
                op0=ALU.mult, op1=ALU.add), inc=s_ve)
            P.wait("vector", s_ve, P.n(s_ve))
            P.emit("vector", lambda e: e.tensor_tensor(
                out=scr16[:, 3, :], in0=scr16[:, 3, :], in1=toff2k[:], op=ALU.mult), inc=s_ve)
            P.wait("vector", s_ve, P.n(s_ve))
            P.emit("vector", lambda e: e.tensor_tensor(
                out=scr16[:, 4, :], in0=scr16[:, 1, :], in1=scr16[:, 2, :], op=ALU.mult),
                inc=s_ve)
            P.wait("vector", s_ve, P.n(s_ve))
            if scat_n[pr]:
                P.wait("vector", s_gi, scat_n[pr])
            P.emit("vector", lambda e, pr=pr: e.scalar_tensor_tensor(
                out=slotf(pr), in0=scr16[:, 4, :], scalar=1.0, in1=scr16[:, 3, :],
                op0=ALU.mult, op1=ALU.add), inc=s_ve) if False else None
            P.emit("vector", lambda e: e.tensor_tensor(
                out=scr16[:, 4, :], in0=scr16[:, 4, :], in1=scr16[:, 3, :], op=ALU.add),
                inc=s_ve)
            P.wait("vector", s_ve, P.n(s_ve))
            P.emit("vector", lambda e, pr=pr: e.tensor_copy(slotu[pr][:], scr16[:, 4, :]),
                   inc=s_ve)
            P.emit("vector", lambda e, g=g, pr=pr: e.tensor_scalar(
                out=pay[pr][:, :, 0], in0=toff0[:], scalar1=float(g * 2048), scalar2=None,
                op0=ALU.add), inc=s_ve)
            P.emit("vector", lambda e, pr=pr: e.tensor_copy(pay[pr][:, :, 1], scr16[:, 0, :]),
                   inc=s_ve)
            v_pay = P.n(s_ve)
            P.wait("gpsimd", s_ve, v_pay)
            if scat_n[1 - pr]:
                P.wait("gpsimd", s_gi, scat_n[1 - pr])
            for t in range(16):
                P.emit("gpsimd", lambda e, pr=pr, t=t: e.indirect_dma_start(
                    out=slotmap, out_offset=bass.IndirectOffsetOnAxis(
                        ap=slotu[pr][:, t:t + 1], axis=0),
                    in_=pay[pr][:, t, :], in_offset=None), inc=s_gi, dma=True)
            scat_n[pr] = P.n(s_gi)

        rix_sb = None
        if want_ridx:
            rix_sb = _sb(ctx, nc, pfx + "rix_sb", [128, NTILES, 8], F32)
        emit_routing16(P, pool, nc,
                       (lgt, m1, eqa, eqb, mka, gka, p1c, pfxr, offs, prt, prt2,
                        triu_s, ones1c, onesc, s_pe, s_act, s_ve, group_cb), NTILES)
        if want_ridx:
            r_done = P.n(s_ve)
            P.wait("sync", s_ve, r_done)
            P.emit("sync", lambda e: e.dma_start(
                out=d["ridx"].rearrange("(i p) f -> p i f", p=128), in_=rix_sb[:]),
                inc=s_lc, dma=True)

        P.wait("sync", s_gi, P.n(s_gi))
        P.emit("sync", lambda e: e.dma_start(
            out=smap[:], in_=slotmap[0:2048, :].rearrange("(s p) c -> p s c", p=128)),
            inc=s_lc, dma=True)
        sm_rdy = P.n(s_lc)
        P.wait("vector", s_lc, sm_rdy)
        P.emit("vector", lambda e: e.tensor_copy(gatec[:], smap[:, :, 1]), inc=s_ve)
        v_gate = P.n(s_ve)
        xg_guard = [0, 0]
        P.emit("gpsimd", lambda e: e.dma_start(out=wob[:], in_=wo_r), inc=s_wo, dma=True)
        wo_rdy = P.n(s_wo)
        for st in range(16):
            par = st % 2
            P.wait("vector", s_lc, sm_rdy)
            P.wait("vector", s_ve, P.n(s_ve))
            P.emit("vector", lambda e, st=st: e.tensor_copy(gidx[:], smap[:, st, 0:1]),
                   inc=s_ve)
            v_gi = P.n(s_ve)
            P.wait("gpsimd", s_ve, v_gi)
            if st >= 2:
                P.wait("gpsimd", s_pe, xg_guard[par])
            P.emit("gpsimd", lambda e, par=par: e.indirect_dma_start(
                out=xg[par][:], out_offset=None, in_=xf_d,
                in_offset=bass.IndirectOffsetOnAxis(ap=gidx[:], axis=0)), inc=s_gg[par],
                dma=True)
            g_rdy = P.n(s_gg[par])
            P.wait("tensor", s_gg[par], g_rdy)
            for j in range(KT):
                pool.use(prt)
                P.emit("tensor", lambda e, j=j, par=par: e.transpose(
                    prt[:, 0:128], xg[par][:, j * 128:(j + 1) * 128], identf[:]), inc=s_pe)
                pe_t = P.n(s_pe)
                P.wait("vector", s_pe, pe_t)
                P.emit("vector", lambda e, j=j, st=st: e.tensor_copy(
                    xe_bf[:, j, st * 128:(st + 1) * 128], prt[:, 0:128]), inc=s_ve)
                pool.free(prt, s_ve)
            xg_guard[par] = P.n(s_pe)
        xe_done = P.n(s_ve)
        P.wait("tensor", s_ve, xe_done)

        wfree = []

        def stream_b(src_r, fi, nk, buf):
            idx = len(wfree)
            par = idx % 2
            if idx >= 2:
                P.wait("gpsimd", *wfree[idx - 2])
            P.emit("gpsimd", lambda e, src_r=src_r, fi=fi, par=par, nk=nk, buf=buf: e.dma_start(
                out=buf[par][:, 0:nk, :], in_=src_r[:, fi]),
                inc=s_w[par], dma=True)
            wfree.append(None)
            return idx, par, (s_w[par], P.n(s_w[par]))

        t1_guard = [0]
        for j in range(16):
            idxw, parw, (ws, wv) = stream_b(we_r, j, 8, vsb)
            P.wait("tensor", ws, wv)
            idxv, parv, (ws2, wv2) = stream_b(ve_r, j, 8, vsb)
            P.wait("tensor", ws2, wv2)
            for n in range(4):
                A, B = pa[n % 2], pb[n % 2]
                pool.use(A)
                for k in range(KT):
                    P.emit("tensor", lambda e, k=k, parw=parw, n=n, A=A: e.matmul(
                        A[:], vsb[parw][:, k, :], xe_bf[:, k, n * 512:(n + 1) * 512],
                        start=(k == 0), stop=(k == KT - 1), skip_group_check=True),
                        inc=(s_pe if k == KT - 1 else None))
                pe_a = P.n(s_pe)
                pool.use(B)
                for k in range(KT):
                    P.emit("tensor", lambda e, k=k, parv=parv, n=n, B=B: e.matmul(
                        B[:], vsb[parv][:, k, :], xe_bf[:, k, n * 512:(n + 1) * 512],
                        start=(k == 0), stop=(k == KT - 1), skip_group_check=True),
                        inc=(s_pe if k == KT - 1 else None))
                pe_b = P.n(s_pe)
                P.wait("scalar", s_pe, pe_b)
                if t1_guard[0]:
                    P.wait("scalar", s_ve, t1_guard[0])
                P.emit("scalar", lambda e, B=B: e.activation(t1[:], B[:], AF.Silu), inc=s_act)
                pool.free(B, s_act)
                a_t1 = P.n(s_act)
                P.wait("vector", s_pe, pe_a)
                P.wait("vector", s_act, a_t1)
                P.emit("vector", lambda e, j=j, n=n, A=A: e.tensor_tensor(
                    out=h_bf[:, j, n * 512:(n + 1) * 512], in0=A[:], in1=t1[:], op=ALU.mult),
                    inc=s_ve)
                t1_guard[0] = P.n(s_ve)
                pool.free(A, s_ve)
            wfree[idxw] = (s_pe, pe_a)
            wfree[idxv] = (s_pe, pe_b)
        h_done = P.n(s_ve)
        P.wait("tensor", s_ve, h_done)

        ytm_guard = [0]
        P.wait("tensor", s_wo, wo_rdy)
        P.wait("vector", s_lc, sm_rdy)
        P.wait("vector", s_ve, v_gate)
        for stile in range(16):
            for n in range(2):
                Y = pyt[n]
                pool.use(Y)
                for k in range(16):
                    P.emit("tensor", lambda e, k=k, stile=stile, n=n, Y=Y: e.matmul(
                        Y[:], h_bf[:, k, stile * 128:(stile + 1) * 128],
                        wob[:, k, n * 512:(n + 1) * 512],
                        start=(k == 0), stop=(k == 15), skip_group_check=True),
                        inc=(s_pe if k == 15 else None))
                pe_y = P.n(s_pe)
                P.wait("vector", s_pe, pe_y)
                if ytm_guard[0]:
                    P.wait("vector", s_st, ytm_guard[0])
                P.emit("vector", lambda e, n=n, Y=Y, stile=stile: e.tensor_scalar(
                    out=ytm[:, n, :], in0=Y[:],
                    scalar1=gatec[:, stile:stile + 1], scalar2=None, op0=ALU.mult), inc=s_ve)
                pool.free(Y, s_ve)
            v_y = P.n(s_ve)
            P.wait("sync", s_ve, v_y)
            P.emit("sync", lambda e, stile=stile: e.dma_start(
                out=ye_d[stile * 128:(stile + 1) * 128, :].rearrange("p (a b) -> p a b", b=512),
                in_=ytm[:]), inc=s_st, dma=True)
            ytm_guard[0] = P.n(s_st)
            if stile == 7:
                d["_ye_half"] = P.n(s_st)
        P.wait("sync", s_st, P.n(s_st))



def build_moe_nc():
    nc = bass.Bass("TRN2", target_bir_lowering=False)
    NTOK = BATCH * SEQ
    d = {}
    d["lg"] = nc.dram_tensor("lg", [NTOK, 8], F32, kind="ExternalInput").ap()
    d["xf"] = nc.dram_tensor("xf", [NTOK + 1, 1024], BF16, kind="ExternalInput").ap()
    d["identbf"] = nc.dram_tensor("identbf", [128, 128], F32, kind="ExternalInput").ap()
    d["we"] = nc.dram_tensor("we", [128, 16, 8, 128], F32, kind="ExternalInput").ap()
    d["vex"] = nc.dram_tensor("vex", [128, 16, 8, 128], F32, kind="ExternalInput").ap()
    d["wo"] = nc.dram_tensor("wo", [2048, 1024], F32, kind="ExternalInput").ap()
    d["ehot"] = nc.dram_tensor("ehot", [128, 8], F32, kind="ExternalInput").ap()
    d["slotinit"] = nc.dram_tensor("slotinit", [2048 + 2048, 2], F32, kind="ExternalInput").ap()
    d["trius"] = nc.dram_tensor("trius", [128, 128], F32, kind="ExternalInput").ap()
    d["ones"] = nc.dram_tensor("ones", [128, 1], F32, kind="ExternalInput").ap()
    d["ones1"] = nc.dram_tensor("ones1", [1, 128], F32, kind="ExternalInput").ap()
    d["identf"] = nc.dram_tensor("identf", [128, 128], F32, kind="ExternalInput").ap()
    d["ye"] = nc.dram_tensor("ye", [2048, 1024], BF16, kind="ExternalOutput").ap()
    d["slotmap"] = nc.dram_tensor("slotmap", [2048 + 2048, 2], F32).ap()
    with ExitStack() as ctx:
        block = ctx.enter_context(nc.Block())
        P = Prog(ctx, nc)
        pool = PsumPool(P)
        emit_moe(nc, P, pool, d)
        P.build_block(block)
    return nc

def host_moe_inputs(logits_all, x_full_pad, W, V, W_out, e):
    i = np.arange(128)
    trius = (i[:, None] < i[None, :]).astype(np.float32)
    ehot = np.zeros((128, 8), np.float32)
    ehot[:, e] = 1.0
    slotinit = np.zeros((2048 + 2048, 2), np.float32)
    slotinit[:, 0] = float(BATCH * SEQ)
    return dict(
        lg=np.ascontiguousarray(logits_all, dtype=np.float32),
        xf=x_full_pad,
        we=_wrl128(np.asarray(W[e], np.float32)),
        vex=_wrl128(np.asarray(V[e], np.float32)),
        wo=np.ascontiguousarray(np.asarray(W_out[e], np.float32)),
        ehot=ehot, slotinit=slotinit, trius=trius,
        ones=np.ones((128, 1), np.float32), ones1=np.ones((1, 128), np.float32),
        identf=np.eye(128, dtype=np.float32),
    )
def emit_comb(nc, P, pool, d, pfx="c"):
    NTOK = BATCH * SEQ
    NTILES = NTOK // 128
    YROWS = 8 * 2048
    lg_d = d["lg"]
    ya_d = d["ya"]
    xr_d = d["xr"]
    myidx_d = d["myidx"]
    ebase_d = d["ebase"]
    triu_d = d["trius"]
    ones_d = d["ones"]
    ones1_d = d["ones1"]
    identf_d = d["identf"]
    out_d = d["out"]
    ridx_d = d["ridx"]

    xr_r = xr_d.rearrange("(k p) t -> p k t", p=128)
    out_r = out_d.rearrange("(k p) t -> p k t", p=128)


    with ExitStack() as ctx:
        lgt = _sb(ctx, nc, pfx + "lgt", [128, NTILES, 8], F32)
        m1 = _sb(ctx, nc, pfx + "m1", [128, 1], F32)
        m2c = _sb(ctx, nc, pfx + "m2c", [128, 1], F32)
        eqa = _sb(ctx, nc, pfx + "eqa", [128, 8], F32)
        eqb = _sb(ctx, nc, pfx + "eqb", [128, 8], F32)
        mka = _sb(ctx, nc, pfx + "mka", [128, 8], F32)
        gka = _sb(ctx, nc, pfx + "gka", [128, 8], F32)
        p1c = _sb(ctx, nc, pfx + "p1c", [128, 1], F32)
        p2c = _sb(ctx, nc, pfx + "p2c", [128, 1], F32)
        offs = _sb(ctx, nc, pfx + "offs", [1, 8], F32)
        ridx_sb = _sb(ctx, nc, pfx + "ridx_sb", [128, NTILES, 8], F32)
        ro = _sb(ctx, nc, pfx + "ro", [128, 8], F32)
        gix = _sb(ctx, nc, pfx + "gix", [128, 1], U32)
        myidx = _sb(ctx, nc, pfx + "myidx_s", [128, SEQ // 128], U32)
        ebase = _sb(ctx, nc, pfx + "ebase_s", [128, 8], F32)
        ygb8 = [_sb(ctx, nc, pfx + f"ygb8_{i}", [128, 1024], F32) for i in range(8)]
        gix8 = _sb(ctx, nc, pfx + "gix8", [128, 8], U32)
        x_f = _sb(ctx, nc, pfx + "x_f", [128, KT, SEQ], F32)
        xsq = _sb(ctx, nc, pfx + "xsq", [128, 512], F32)
        srow = _sb(ctx, nc, pfx + "srow_c", [1, 2, 512], F32)
        triu_s = _sb(ctx, nc, pfx + "triu_s", [128, 128], F32)
        onesc = _sb(ctx, nc, pfx + "onesc_c", [128, 1], F32)
        ones1c = _sb(ctx, nc, pfx + "ones1c_c", [1, 128], F32)
        identf = _sb(ctx, nc, pfx + "identf_c", [128, 128], F32)
        identbf_s = _sb(ctx, nc, pfx + "identbf_c", [128, 128], BF16)

        prt = _ps(ctx, nc, pfx + "prt", [128, 512])
        prt2 = _ps(ctx, nc, pfx + "prt2", [128, 512])
        ptr = _ps(ctx, nc, pfx + "ptr", [128, 512])
        pst = _ps(ctx, nc, pfx + "pst", [128, 512])
        pst2 = _ps(ctx, nc, pfx + "pst2", [128, 512])
        pbc = _ps(ctx, nc, pfx + "pbc", [128, 512])
        pbc2 = _ps(ctx, nc, pfx + "pbc2", [128, 512])

        s_lc = P.sem(pfx + "s_lc", "sync")
        s_gi = P.sem(pfx + "s_gi", "gpsimd")
        s_gr = P.sem(pfx + "s_gr", "gpsimd")
        s_gy8 = [P.sem(pfx + f"s_gy8_{i}", "gpsimd") for i in range(8)]
        s_ibf = P.sem(pfx + "s_ibf", "gpsimd")
        s_pe = P.sem(pfx + "s_pe", "tensor")
        s_act = P.sem(pfx + "s_act", "scalar")
        s_ve = P.sem(pfx + "s_ve", "vector")
        s_st = P.sem(pfx + "s_st", "sync")

        for dst, src in [(triu_s, triu_d), (onesc, ones_d), (ones1c, ones1_d),
                         (identf, identf_d), (myidx, myidx_d), (ebase, ebase_d)]:
            P.emit("sync", lambda e, dst=dst, src=src: e.dma_start(out=dst[:], in_=src),
                   inc=s_lc, dma=True)
        P.emit("sync", lambda e: e.dma_start(
            out=lgt[:], in_=lg_d.rearrange("(i p) f -> p i f", p=128)), inc=s_lc, dma=True)
        P.emit("sync", lambda e: e.dma_start(out=x_f[:], in_=xr_r), inc=s_lc, dma=True)
        ld0 = P.n(s_lc)
        for eng in ("tensor", "scalar", "vector"):
            P.wait(eng, s_lc, ld0)
        P.emit("vector", lambda e: e.memset(offs[:], 0.0), inc=s_ve)

        if not d.get("ridx_ready"):
            def tile_cb(i):
                P.wait("vector", s_ve, P.n(s_ve))
                P.emit("vector", lambda e: e.tensor_tensor(
                    out=eqa[:], in0=prt[:, 0:8], in1=ebase[:], op=ALU.add), inc=s_ve)
                pool.free(prt, s_ve)
                P.wait("vector", s_ve, P.n(s_ve))
                P.emit("vector", lambda e: e.tensor_tensor(
                    out=eqa[:], in0=eqa[:], in1=mka[:], op=ALU.mult), inc=s_ve)
                P.emit("vector", lambda e: e.tensor_scalar(
                    out=eqb[:], in0=mka[:], scalar1=float(-YROWS), scalar2=float(YROWS),
                    op0=ALU.mult, op1=ALU.add), inc=s_ve)
                P.wait("vector", s_ve, P.n(s_ve))
                P.emit("vector", lambda e, i=i: e.tensor_tensor(
                    out=ridx_sb[:, i, :], in0=eqa[:], in1=eqb[:], op=ALU.add), inc=s_ve)

            emit_routing(P, pool, nc,
                         (lgt, m1, m2c, eqa, eqb, mka, gka, p1c, p2c, offs, None, prt, prt2,
                          triu_s, ones1c, onesc, s_pe, s_act, s_ve, tile_cb), NTILES)
            r_done = P.n(s_ve)
            P.wait("sync", s_ve, r_done)
            P.emit("sync", lambda e: e.dma_start(
                out=ridx_d.rearrange("(i p) f -> p i f", p=128), in_=ridx_sb[:]),
                inc=s_lc, dma=True)
        P.wait("gpsimd", s_lc, P.n(s_lc))

        acc_guard = [0]
        gu_guard = [0]
        NT = SEQ // 128
        for t in range(NT):
            P.emit("gpsimd", lambda e, t=t: e.indirect_dma_start(
                out=ro[:], out_offset=None, in_=ridx_d,
                in_offset=bass.IndirectOffsetOnAxis(ap=myidx[:, t:t + 1], axis=0)),
                inc=s_gr, dma=True)
            ro_rdy = P.n(s_gr)
            P.wait("vector", s_gr, ro_rdy)
            if gu_guard[0]:
                P.wait("vector", s_ve, gu_guard[0])
            P.emit("vector", lambda e: e.tensor_copy(gix8[:], ro[:]), inc=s_ve)
            v_gx = P.n(s_ve)
            gu_guard[0] = v_gx
            P.wait("gpsimd", s_ve, v_gx)
            if acc_guard[0]:
                P.wait("gpsimd", s_ve, acc_guard[0])
            for ei in range(8):
                P.emit("gpsimd", lambda e, ei=ei: e.indirect_dma_start(
                    out=ygb8[ei][:], out_offset=None, in_=ya_d,
                    in_offset=bass.IndirectOffsetOnAxis(ap=gix8[:, ei:ei + 1], axis=0)),
                    inc=s_gy8[ei], dma=True)
            for ei in range(8):
                P.wait("vector", s_gy8[ei], P.n(s_gy8[ei]))
            for a, b2 in [(0, 1), (2, 3), (4, 5), (6, 7)]:
                P.emit("vector", lambda e, a=a, b2=b2: e.tensor_tensor(
                    out=ygb8[a][:], in0=ygb8[a][:], in1=ygb8[b2][:], op=ALU.add), inc=s_ve)
            P.wait("vector", s_ve, P.n(s_ve))
            for a, b2 in [(0, 2), (4, 6)]:
                P.emit("vector", lambda e, a=a, b2=b2: e.tensor_tensor(
                    out=ygb8[a][:], in0=ygb8[a][:], in1=ygb8[b2][:], op=ALU.add), inc=s_ve)
            P.wait("vector", s_ve, P.n(s_ve))
            P.emit("vector", lambda e: e.tensor_tensor(
                out=ygb8[0][:], in0=ygb8[0][:], in1=ygb8[4][:], op=ALU.add), inc=s_ve)
            v_acc = P.n(s_ve)
            acc_guard[0] = v_acc
            P.wait("tensor", s_ve, v_acc)
            for j in range(KT):
                pool.use(ptr)
                P.emit("tensor", lambda e, j=j: e.transpose(
                    ptr[:, 0:128], ygb8[0][:, j * 128:(j + 1) * 128], identf[:]), inc=s_pe)
                pe_t = P.n(s_pe)
                P.wait("vector", s_pe, pe_t)
                P.emit("vector", lambda e, j=j, t=t: e.tensor_tensor(
                    out=x_f[:, j, t * 128:(t + 1) * 128], in0=x_f[:, j, t * 128:(t + 1) * 128],
                    in1=ptr[:, 0:128], op=ALU.add), inc=s_ve)
                pool.free(ptr, s_ve)
            acc_guard[0] = P.n(s_ve)
        x1_done = P.n(s_ve)

        xsq_guard = [0]
        for n in range(SEQ // 512):
            ns = n * 512
            P.wait("scalar", s_ve, x1_done)
            pool.use(pst)
            pool.use(pst2)
            P.wait("tensor", s_ve, x1_done)
            for j in range(KT):
                if xsq_guard[0]:
                    P.wait("scalar", s_pe, xsq_guard[0])
                P.emit("scalar", lambda e, j=j, ns=ns: e.activation(
                    xsq[:], x_f[:, j, ns:ns + 512], AF.Square), inc=s_act)
                sqv = P.n(s_act)
                P.wait("tensor", s_act, sqv)
                P.emit("tensor", lambda e, j=j, ns=ns: e.matmul(
                    pst[0:1, :], onesc[:], x_f[:, j, ns:ns + 512],
                    start=(j == 0), stop=(j == KT - 1), skip_group_check=True),
                    inc=(s_pe if j == KT - 1 else None))
                P.emit("tensor", lambda e, j=j: e.matmul(
                    pst2[0:1, :], onesc[:], xsq[:],
                    start=(j == 0), stop=(j == KT - 1), skip_group_check=True), inc=s_pe)
                xsq_guard[0] = P.n(s_pe)
            pe_stats = P.n(s_pe)
            P.wait("vector", s_pe, pe_stats)
            P.emit("vector", lambda e: e.tensor_scalar(
                out=srow[:, 0, :], in0=pst[0:1, :], scalar1=1.0 / D_MODEL, scalar2=None,
                op0=ALU.mult), inc=s_ve)
            v_mean = P.n(s_ve)
            P.wait("vector", s_ve, v_mean)
            P.emit("vector", lambda e: e.tensor_tensor(
                out=srow[:, 1, :], in0=srow[:, 0, :], in1=srow[:, 0, :], op=ALU.mult), inc=s_ve)
            P.wait("vector", s_ve, P.n(s_ve))
            P.emit("vector", lambda e: e.scalar_tensor_tensor(
                out=srow[:, 1, :], in0=pst2[0:1, :], scalar=1.0 / D_MODEL, in1=srow[:, 1, :],
                op0=ALU.mult, op1=ALU.subtract), inc=s_ve)
            pool.free(pst, s_ve)
            pool.free(pst2, s_ve)
            P.wait("vector", s_ve, P.n(s_ve))
            P.emit("vector", lambda e: e.tensor_scalar(
                out=srow[:, 1, :], in0=srow[:, 1, :], scalar1=1e-5, scalar2=None,
                op0=ALU.add), inc=s_ve)
            v_var = P.n(s_ve)
            P.wait("scalar", s_ve, v_var)
            P.emit("scalar", lambda e: e.activation(srow[:, 1, :], srow[:, 1, :], AF.Ln),
                   inc=s_act)
            P.wait("scalar", s_act, P.n(s_act))
            P.emit("scalar", lambda e: e.activation(srow[:, 1, :], srow[:, 1, :], AF.Exp,
                                                    scale=-0.5), inc=s_act)
            act_r = P.n(s_act)
            pool.use(pbc)
            P.wait("tensor", s_ve, v_mean)
            P.emit("tensor", lambda e: e.matmul(pbc[:], ones1c[:], srow[:, 0, :],
                                                start=True, stop=True,
                                                skip_group_check=True))
            pool.use(pbc2)
            P.wait("tensor", s_act, act_r)
            P.emit("tensor", lambda e: e.matmul(pbc2[:], ones1c[:], srow[:, 1, :],
                                                start=True, stop=True,
                                                skip_group_check=True), inc=s_pe)
            pe_bc = P.n(s_pe)
            P.wait("vector", s_pe, pe_bc)
            for j in range(KT):
                P.emit("vector", lambda e, j=j, ns=ns: e.tensor_tensor(
                    out=x_f[:, j, ns:ns + 512], in0=x_f[:, j, ns:ns + 512], in1=pbc[:],
                    op=ALU.subtract), inc=s_ve)
                P.wait("vector", s_ve, P.n(s_ve))
                P.emit("vector", lambda e, j=j, ns=ns: e.tensor_tensor(
                    out=x_f[:, j, ns:ns + 512], in0=x_f[:, j, ns:ns + 512], in1=pbc2[:],
                    op=ALU.mult), inc=s_ve)
            pool.free(pbc, s_ve)
            pool.free(pbc2, s_ve)
            v_out = P.n(s_ve)
            P.wait("sync", s_ve, v_out)
            P.emit("sync", lambda e, ns=ns: e.dma_start(
                out=out_r[:, :, ns:ns + 512], in_=x_f[:, :, ns:ns + 512]), inc=s_st, dma=True)
        P.wait("sync", s_st, P.n(s_st))



def build_comb_nc():
    nc = bass.Bass("TRN2", target_bir_lowering=False)
    NTOK = BATCH * SEQ
    YROWS = 8 * 2048
    d = {}
    d["lg"] = nc.dram_tensor("lg", [NTOK, 8], F32, kind="ExternalInput").ap()
    d["ya"] = nc.dram_tensor("ya", [YROWS + 1, 1024], BF16, kind="ExternalInput").ap()
    d["identbf"] = nc.dram_tensor("identbf", [128, 128], F32, kind="ExternalInput").ap()
    d["xr"] = nc.dram_tensor("xr", [D_MODEL, SEQ], F32, kind="ExternalInput").ap()
    d["myidx"] = nc.dram_tensor("myidx", [128, SEQ // 128], U32, kind="ExternalInput").ap()
    d["ebase"] = nc.dram_tensor("ebase", [128, 8], F32, kind="ExternalInput").ap()
    d["trius"] = nc.dram_tensor("trius", [128, 128], F32, kind="ExternalInput").ap()
    d["ones"] = nc.dram_tensor("ones", [128, 1], F32, kind="ExternalInput").ap()
    d["ones1"] = nc.dram_tensor("ones1", [1, 128], F32, kind="ExternalInput").ap()
    d["identf"] = nc.dram_tensor("identf", [128, 128], F32, kind="ExternalInput").ap()
    d["out"] = nc.dram_tensor("out", [D_MODEL, SEQ], F32, kind="ExternalOutput").ap()
    d["ridx"] = nc.dram_tensor("ridx", [NTOK, 8], F32).ap()
    with ExitStack() as ctx:
        block = ctx.enter_context(nc.Block())
        P = Prog(ctx, nc)
        pool = PsumPool(P)
        emit_comb(nc, P, pool, d)
        P.build_block(block)
    return nc

def host_comb_inputs(logits_all, y_all_pad, x_fm_core, b):
    i = np.arange(128)
    trius = (i[:, None] < i[None, :]).astype(np.float32)
    myidx = (b * SEQ + np.arange(SEQ, dtype=np.uint32).reshape(SEQ // 128, 128).T).astype(np.uint32)
    ebase = np.broadcast_to((np.arange(8, dtype=np.float32) * 1024), (128, 8)).copy()
    return dict(
        lg=np.ascontiguousarray(logits_all, dtype=np.float32),
        ya=y_all_pad,
        xr=np.ascontiguousarray(x_fm_core, dtype=np.float32),
        myidx=np.ascontiguousarray(myidx), ebase=ebase, trius=trius,
        ones=np.ones((128, 1), np.float32), ones1=np.ones((1, 128), np.float32),
        identf=np.eye(128, dtype=np.float32),
    )


def _barrier(P):
    for eng in ENGS:
        for name, rec in P.sems.items():
            if rec[1] > 0 and rec[2] != eng:
                P.wait(eng, name)


def build_fused_nc():
    nc = bass.Bass("TRN2", target_bir_lowering=False)
    NTOK = BATCH * SEQ
    DI, CD, NP = 2048, 2176, 16
    f = F32

    def ein(name, shape, dt=F32):
        return nc.dram_tensor(name, shape, dt, kind="ExternalInput").ap()

    d = {}
    for name, shape in [("x", [D_MODEL, SEQ]), ("wzt", [128, 16, 8, 128]), ("wxt", [128, 17, 8, 128]),
                        ("wdt", [1024, 32]), ("wot_m", [128, 8, 16, 128]), ("normw", [128, KT]),
                        ("gnw", [128, NP]), ("convw", [128, 17, 4]), ("convb", [128, 17]),
                        ("drep", [128, NP]), ("dtb", [1, 32]), ("alog", [128, 32]),
                        ("rw1t", [128, 8, 32]), ("rb1", [32, 1]), ("rw2t", [32, 3]),
                        ("rb2", [1, 3]), ("rwt", [128, 8, 8]), ("rbr", [1, 8]),
                        ("ones", [128, 1]), ("ones1", [1, 128]), ("identf", [128, 128]),
                        ("identb", [128, 128]), ("triu", [128, 128]), ("negblk", [8, 1024]),
                        ("maskd", [128, 512]),
                        ("we", [128, 16, 8, 128]), ("vex", [128, 16, 8, 128]), ("wo", [2048, 1024]),
                        ("ehot", [128, 8]), ("slotinit", [2048 + 2048, 2]),
                        ("trius", [128, 128]), ("ebase", [128, 8]),
                        ("wqt", [128, 2, 8, 512]), ("wkt", [128, 2, 8, 512]), ("wvt", [128, 2, 8, 512]),
                        ("wot", [128, 2, 8, 512]), ("maskS", [128, 4, 512]), ("maskT", [128, 4, 512]),
                        ("identbf", [128, 128]),
                        ("w1t", [128, 32, 8, 128]), ("w2t", [128, 32, 8, 128]), ("h1t", [128, 32, 32, 128]),
                        ("h2t", [128, 32, 32, 128]), ("w3t", [128, 8, 32, 128]), ("s1", [128, 32]),
                        ("s2", [128, 32]), ("sh1", [128, 32]), ("sh2", [128, 32]),
                        ("s3", [128, 8])]:
        d[name] = ein(name, shape)
    d["myidx"] = ein("myidx", [128, SEQ // 128], U32)
    out_d = nc.dram_tensor("out", [D_MODEL, SEQ], F32, kind="ExternalOutput").ap()
    x1f = nc.dram_tensor("x1f", [D_MODEL, SEQ], F32).ap()
    xtm = nc.dram_tensor("xtm_i", [SEQ, 1024], BF16).ap()
    lgx = nc.dram_tensor("lgx", [SEQ, 8], F32).ap()
    lgall = nc.dram_tensor("lgall", [NTOK, 8], F32).ap()
    xfp = nc.dram_tensor("xfp", [NTOK + 1, 1024], BF16, addr_space="Shared").ap()
    yed = nc.dram_tensor("yed", [2048, 1024], BF16).ap()
    yap = nc.dram_tensor("yap", [8 * 2048 + 1, 1024], BF16, addr_space="Shared").ap()
    x2f = nc.dram_tensor("x2f", [D_MODEL, SEQ], F32).ap()
    x3f = nc.dram_tensor("x3f", [D_MODEL, SEQ], F32).ap()
    cbounce = [nc.dram_tensor(f"cbounce{i}", [32, 128], F32).ap() for i in range(2)]
    slotmap = nc.dram_tensor("slotmap", [2048 + 2048, 2], F32).ap()
    ridx = nc.dram_tensor("ridx", [NTOK, 8], F32).ap()

    base = {k: d[k] for k in d}
    dA = dict(base, out=x1f, xtm=xtm, logits=lgx, cbounce=cbounce)
    dB = dict(base, lg=lgall, xf=xfp, ye=yed, slotmap=slotmap, ridx=ridx)
    dC = dict(base, lg=lgall, ya=yap, xr=x1f, out=x2f, ridx=ridx, ridx_ready=True)
    dD = dict(base, x=x2f, out=x3f)
    dE = dict(base, x=x3f, out=out_d)

    with ExitStack() as ctx:
        block = ctx.enter_context(nc.Block())
        P = Prog(ctx, nc)
        s_z = P.sem("f_z", "sync")
        s_zv = P.sem("f_zv", "vector")
        cc = [P.sem(f"f_cc{i}", "gpsimd") for i in range(4)]
        with ExitStack() as zctx:
            zrow = _sb(zctx, nc, "f_zrow", [1, 1024], BF16)
            P.emit("vector", lambda e: e.memset(zrow[:], 0.0), inc=s_zv)
            P.wait("sync", s_zv, P.n(s_zv))
            P.emit("sync", lambda e: e.dma_start(out=xfp[NTOK:NTOK + 1, :], in_=zrow[:]),
                   inc=s_z, dma=True)
            P.emit("sync", lambda e: e.dma_start(out=yap[8 * 2048:8 * 2048 + 1, :], in_=zrow[:]),
                   inc=s_z, dma=True)
            P.wait("sync", s_z, P.n(s_z))
            _barrier(P)

        pool = PsumPool(P)
        emit_mamba(nc, P, pool, dA)
        _barrier(P)
        P.emit("gpsimd", lambda e: e.collective_compute(
            "AllGather", ALU.bypass, replica_groups=[list(range(8))],
            ins=[lgx.opt()], outs=[lgall.opt()]), inc=cc[0])
        P.emit("gpsimd", lambda e: e.collective_compute(
            "AllGather", ALU.bypass, replica_groups=[list(range(8))],
            ins=[xtm.opt()], outs=[xfp[0:NTOK, :].opt()]), inc=cc[1])
        _barrier(P)
        pool = PsumPool(P)
        emit_moe(nc, P, pool, dB)
        P.wait("gpsimd", "bs_st", dB["_ye_half"])
        P.emit("gpsimd", lambda e: e.collective_compute(
            "AllGather", ALU.bypass, replica_groups=[list(range(8))],
            ins=[yed[0:1024, :].opt()], outs=[yap[0:8192, :].opt()]), inc=cc[2])
        P.wait("gpsimd", "bs_st", P.n("bs_st"))
        P.emit("gpsimd", lambda e: e.collective_compute(
            "AllGather", ALU.bypass, replica_groups=[list(range(8))],
            ins=[yed[1024:2048, :].opt()], outs=[yap[8192:16384, :].opt()]), inc=cc[3])
        _barrier(P)
        pool = PsumPool(P)
        emit_comb(nc, P, pool, dC)
        _barrier(P)
        pool = PsumPool(P)
        emit_attn(nc, P, pool, dD)
        _barrier(P)
        pool = PsumPool(P)
        emit_ffn(nc, P, pool, dE)
        P.build_block(block)
    return nc


def host_fused_inputs(inputs, b):
    f32 = np.float32
    x = np.asarray(inputs["L0_attn_x"], f32)
    wA = {k[len("L0_attn_"):]: np.asarray(v, f32) for k, v in inputs.items()
          if k.startswith("L0_attn_") and k != "L0_attn_x"}
    wD = {"wq": inputs["L1_attn_wq"], "wk": inputs["L1_attn_wk"],
          "wv": inputs["L1_attn_wv"], "wo": inputs["L1_attn_wo"]}
    wE = {k[len("L1_ffn_"):]: np.asarray(v, f32) for k, v in inputs.items()
          if k.startswith("L1_ffn_")}
    m = {}
    m.update(host_mamba_inputs(x[b].T, wA, inputs["L0_ffn_router_w"], inputs["L0_ffn_router_b"]))
    dummy = np.zeros((1, 1), f32)
    mo = host_moe_inputs(dummy, dummy,
                         np.asarray(inputs["L0_ffn_W"], f32),
                         np.asarray(inputs["L0_ffn_V"], f32),
                         np.asarray(inputs["L0_ffn_W_out"], f32), b)
    mo.pop("lg"); mo.pop("xf")
    m.update(mo)
    co = host_comb_inputs(dummy, dummy, dummy, b)
    co.pop("lg"); co.pop("ya"); co.pop("xr")
    m.update(co)
    at = host_attn_inputs(dummy, wD)
    at.pop("x")
    m.update(at)
    ff = host_ffn_inputs(dummy, wE)
    ff.pop("x")
    m.update(ff)
    return m


def kernel(**inputs):
    nc = build_fused_nc()
    in_maps = [host_fused_inputs(inputs, b) for b in range(8)]
    res = run_bass_kernel_spmd(nc, in_maps, core_ids=list(range(8)))
    out = np.stack([res.results[b]["out"].T for b in range(8)])
    return np.ascontiguousarray(out, dtype=np.float32)


def kernel_multilaunch(**inputs):
    f32 = np.float32
    x = np.asarray(inputs["L0_attn_x"], f32)
    wA = {k[len("L0_attn_"):]: np.asarray(v, f32) for k, v in inputs.items()
          if k.startswith("L0_attn_") and k != "L0_attn_x"}
    router_w = np.asarray(inputs["L0_ffn_router_w"], f32)
    router_b = np.asarray(inputs["L0_ffn_router_b"], f32)
    W = np.asarray(inputs["L0_ffn_W"], f32)
    V = np.asarray(inputs["L0_ffn_V"], f32)
    Wo = np.asarray(inputs["L0_ffn_W_out"], f32)
    wD = {"wq": inputs["L1_attn_wq"], "wk": inputs["L1_attn_wk"],
          "wv": inputs["L1_attn_wv"], "wo": inputs["L1_attn_wo"]}
    wE = {k[len("L1_ffn_"):]: np.asarray(v, f32) for k, v in inputs.items()
          if k.startswith("L1_ffn_")}
    cores = list(range(8))

    ncA = build_mamba_nc()
    resA = run_bass_kernel_spmd(
        ncA, [host_mamba_inputs(x[b].T, wA, router_w, router_b) for b in range(8)],
        core_ids=cores)
    x1 = [resA.results[b]["out"] for b in range(8)]
    logits_all = np.ascontiguousarray(
        np.concatenate([resA.results[b]["logits"] for b in range(8)], axis=0))
    x_full_pad = np.ascontiguousarray(np.concatenate(
        [resA.results[b]["xtm"] for b in range(8)] + [np.zeros((1, 1024), f32)], axis=0))

    ncB = build_moe_nc()
    resB = run_bass_kernel_spmd(
        ncB, [host_moe_inputs(logits_all, x_full_pad, W, V, Wo, e) for e in range(8)],
        core_ids=cores)
    y_all_pad = np.ascontiguousarray(np.concatenate(
        [resB.results[e]["ye"] for e in range(8)] + [np.zeros((1, 1024), f32)], axis=0))

    ncC = build_comb_nc()
    resC = run_bass_kernel_spmd(
        ncC, [host_comb_inputs(logits_all, y_all_pad, x1[b], b) for b in range(8)],
        core_ids=cores)
    x2 = [resC.results[b]["out"] for b in range(8)]

    ncD = build_attn_nc()
    resD = run_bass_kernel_spmd(
        ncD, [host_attn_inputs(x2[b], wD) for b in range(8)], core_ids=cores)
    x3 = [resD.results[b]["out"] for b in range(8)]

    ncE = build_ffn_nc()
    resE = run_bass_kernel_spmd(
        ncE, [host_ffn_inputs(x3[b], wE) for b in range(8)], core_ids=cores)
    out = np.stack([resE.results[b]["out"].T for b in range(8)])
    return np.ascontiguousarray(out, dtype=f32)
